# version 14
# baseline (speedup 1.0000x reference)
import contextlib
import numpy as np
import ml_dtypes
import concourse.bass as bass
import concourse.mybir as mybir
from concourse.bass_utils import run_bass_kernel_spmd

F32 = mybir.dt.float32
BF16 = mybir.dt.bfloat16
AF = mybir.ActivationFunctionType
ALU = mybir.AluOpType
AX = mybir.AxisListType
BF16NP = ml_dtypes.bfloat16

B, S, D = 4, 4096, 1024
TOK = 2048
NT = TOK // 128
NCH = D // 128
H0, DK0 = 16, 64
H1, QL1, VL1 = 16, 16, 64
DH = 4096
NHH = DH // 128
SKEY = 4096
RSW = 6016
LN16 = float(np.log(16.0))
EPS = 1e-5


def _col8(b):
    return np.ascontiguousarray(b.reshape(-1, 128).T)


def _prep_shared(inp):
    s = {}
    s["wq0"] = inp["L0_attn_Wq"]
    s["wk0"] = inp["L0_attn_Wk"]
    s["wv0"] = inp["L0_attn_Wv"]
    s["bq0"] = _col8(inp["L0_attn_bq"])
    s["bk0"] = _col8(inp["L0_attn_bk"])
    s["bv0"] = inp["L0_attn_bv"].reshape(1, D)
    gwT = inp["L0_ffn_gate_w"].T
    s["gw"] = np.ascontiguousarray(
        gwT.reshape(NCH, 128, 8).transpose(1, 0, 2).reshape(128, NCH * 8))
    s["gb"] = inp["L0_ffn_gate_b"].reshape(8, 1)
    s["ew"] = np.ascontiguousarray(
        inp["L0_ffn_expert_w"].transpose(0, 2, 1)).reshape(8 * D, D)
    s["be"] = np.ascontiguousarray(
        inp["L0_ffn_expert_b"].reshape(8, NCH, 128).transpose(2, 1, 0).reshape(128, NCH * 8))
    for nm in ("Wq", "Wqc", "Wk", "Wkc", "Wv", "Wvc"):
        s["a1" + nm] = inp["L1_attn_" + nm]
    s["a1bq"] = np.ascontiguousarray(inp["L1_attn_bq"].reshape(-1, 128).T)
    s["a1bqc"] = np.ascontiguousarray(inp["L1_attn_bqc"].reshape(-1, 128).T)
    s["a1bk"] = np.ascontiguousarray(inp["L1_attn_bk"].reshape(-1, 128).T)
    s["a1bkc"] = np.ascontiguousarray(inp["L1_attn_bkc"].reshape(-1, 128).T)
    s["a1bv"] = _col8(inp["L1_attn_bv"])
    s["a1bvc"] = _col8(inp["L1_attn_bvc"])
    s["w1T"] = np.ascontiguousarray(inp["L1_ffn_w1_w"].T)
    s["w2T"] = np.ascontiguousarray(inp["L1_ffn_w2_w"].T)
    s["h1T"] = np.ascontiguousarray(inp["L1_ffn_h1_w"].T)
    s["h2T"] = np.ascontiguousarray(inp["L1_ffn_h2_w"].T)
    s["w3T"] = np.ascontiguousarray(inp["L1_ffn_w3_w"].T)
    s["s1"] = np.ascontiguousarray(inp["L1_ffn_w1_s"].reshape(-1, 128).T)
    s["s2"] = np.ascontiguousarray(inp["L1_ffn_w2_s"].reshape(-1, 128).T)
    s["sh1"] = np.ascontiguousarray(inp["L1_ffn_h1_s"].reshape(-1, 128).T)
    s["sh2"] = np.ascontiguousarray(inp["L1_ffn_h2_s"].reshape(-1, 128).T)
    s["s3"] = _col8(inp["L1_ffn_w3_s"])
    s["ident"] = np.eye(128, dtype=np.float32)
    s["ones_row"] = np.ones((1, 128), dtype=np.float32)
    s["ones_col"] = np.ones((128, 1), dtype=np.float32)
    rel = inp["L0_attn_rel_emb"]
    s["rel_ext"] = np.ascontiguousarray(np.concatenate(
        [rel, np.broadcast_to(rel[:, 4094:4095], (H0, 4096))], axis=1))
    return s


def _prep_rs(rel_ext, g):
    out = np.empty((H0, 128, RSW), dtype=np.float32)
    base = g * 2048 + 127
    for h in range(H0):
        arr = rel_ext[h]
        v = np.lib.stride_tricks.as_strided(
            arr[base:], shape=(128, RSW),
            strides=(-arr.strides[0], arr.strides[0]))
        out[h] = v
    return out.reshape(H0 * 128, RSW)



class Sems:
    def __init__(self, nc, stack, names, prefix=""):
        self.h = {}
        self.n = {}
        for nm in names:
            self.h[nm] = stack.enter_context(nc.semaphore(prefix + nm))
            self.n[nm] = 0

    def tick(self, instr, nm, k=1):
        instr.then_inc(self.h[nm], k)
        self.n[nm] += k
        return self.n[nm]

    def w(self, eng, nm, val=None):
        eng.wait_ge(self.h[nm], self.n[nm] if val is None else val)

    def ch(self, instr, eng, nm):
        instr.then_inc(self.h[nm], 1)
        self.n[nm] += 1
        eng.wait_ge(self.h[nm], self.n[nm])


class Carver:
    def __init__(self, tensor, nbytes, base=0):
        self.t = tensor
        self.nbytes = nbytes
        self.base = base
        self.cur = 0

    def alloc(self, cols, dtype, rows=128):
        esz = 4 if dtype == F32 else 2
        al = self.cur % 4
        if al:
            self.cur += 4 - al
        assert self.cur + cols * esz <= self.nbytes, (
            f"arena overflow {self.cur} + {cols}*{esz} > {self.nbytes}")
        off = (self.base + self.cur) // esz
        self.cur += cols * esz
        view = self.t.ap().bitcast(dtype)
        return view[0:rows, off:off + cols]

    def alloc_at(self, byte_off, cols, dtype, rows=128):
        esz = 4 if dtype == F32 else 2
        assert byte_off % 4 == 0 and byte_off + cols * esz <= self.nbytes
        off = (self.base + byte_off) // esz
        view = self.t.ap().bitcast(dtype)
        return view[0:rows, off:off + cols]

    def push(self):
        pass

    def pop(self):
        pass

    def sub(self, kb):
        al = self.cur % 4
        if al:
            self.cur += 4 - al
        s = Carver(self.t, kb * 1024, base=self.base + self.cur)
        self.cur += kb * 1024
        assert self.cur <= self.nbytes
        return s


class PsumPool:
    def __init__(self, pst):
        self.t = pst
        self.cur = 0

    def view_at(self, bank, shape, dtype):
        esz = 4 if dtype == F32 else 2
        rows, cols = shape
        off = bank * 2048 // esz
        view = self.t.ap().bitcast(dtype)
        return view[0:rows, off:off + cols]

    def alloc(self, nm, shape, dtype):
        esz = 4 if dtype == F32 else 2
        rows, cols = shape
        if self.cur % 2048:
            self.cur += 2048 - self.cur % 2048
        need = cols * esz
        assert self.cur + need <= 16384, f"psum overflow at {nm}"
        off = self.cur // esz
        self.cur += need
        view = self.t.ap().bitcast(dtype)
        return view[0:rows, off:off + cols]


def alloc_pools(nc, stk, kb=207):
    arena = stk.enter_context(nc.sbuf_tensor("arena", [128, kb * 512], BF16))
    pspool = stk.enter_context(nc.psum_tensor("pspool", [128, 4096], F32))
    return Carver(arena, kb * 1024), PsumPool(pspool)


def ln_stats_emit(sems, v, sc, x_ap, junk, stats, wait_fn=None):
    F = x_ap.shape[-1]
    inv = 1.0 / F
    v.reduce_sum(stats["s"], x_ap, axis=AX.X)
    v.tensor_scalar(stats["mneg"], stats["s"], -inv, None, ALU.mult)
    sc.activation(junk, x_ap, AF.Square, accum_out=stats["ms2"])



def dram_A(nc):
    io = {}
    io["x_in"] = nc.dram_tensor("x_in", [TOK, D], F32, kind="ExternalInput")
    io["wq"] = nc.dram_tensor("wq0", [D, D], F32, kind="ExternalInput")
    io["wk"] = nc.dram_tensor("wk0", [D, D], F32, kind="ExternalInput")
    io["wv"] = nc.dram_tensor("wv0", [D, D], F32, kind="ExternalInput")
    io["bq"] = nc.dram_tensor("bq0", [128, 8], F32, kind="ExternalInput")
    io["bk"] = nc.dram_tensor("bk0", [128, 8], F32, kind="ExternalInput")
    io["bv"] = nc.dram_tensor("bv0", [1, D], F32, kind="ExternalInput")
    io["ident_in"] = nc.dram_tensor("ident", [128, 128], F32, kind="ExternalInput")
    io["ones_row_in"] = nc.dram_tensor("ones_row", [1, 128], F32, kind="ExternalInput")
    io["qT_o"] = nc.dram_tensor("qT_o", [128, NCH * TOK], BF16, kind="ExternalOutput")
    io["kT_o"] = nc.dram_tensor("kT_o", [128, NCH * TOK], BF16, kind="ExternalOutput")
    io["v_o"] = nc.dram_tensor("v_o", [TOK, D], BF16, kind="ExternalOutput")
    return io


def emit_A(nc, block, stk, io, fused=False):
    nc = bass.Bass("TRN2", target_bir_lowering=False)
    x_in = io["x_in"]
    wq = io["wq"]
    wk = io["wk"]
    wv = io["wv"]
    bq = io["bq"]
    bk = io["bk"]
    bv = io["bv"]
    ident_in = io["ident_in"]
    ones_row_in = io["ones_row_in"]
    qT_o = io["qT_o"]
    kT_o = io["kT_o"]
    v_o = io["v_o"]
    T = lambda nm, shape, dt: io["_sb"].alloc(shape[-1], dt, rows=shape[0])
    PS = lambda nm, shape, dt: io["_ps"].alloc(nm, shape, dt)
    xnT = T("xnT", [128, NCH * TOK], BF16)
    qT = T("qT", [128, NCH * TOK], BF16)
    kT = T("kT", [128, NCH * TOK], BF16)
    wsb = T("wsb", [128, NCH * D], BF16)
    wsb2 = T("wsb2", [128, NCH * D], BF16)
    xt = T("xt", [128, 2 * D], F32)
    xn = T("xn", [128, 2 * D], BF16)
    stats = T("stats", [128, 8 * NT], F32)
    idnt = T("idnt", [128, 128], BF16)
    onesr = T("onesr", [1, 128], F32)
    bqs = T("bqs", [128, 8], F32)
    bks = T("bks", [128, 8], F32)
    bvrow = T("bvrow", [1, D], F32)
    bvb = T("bvb", [128, D], F32)
    vout = T("vout", [128, 2 * D], BF16)
    ps_tr = PS("ps_tr", [128, D], BF16)
    ps_a = PS("ps_a", [128, 512], F32)
    ps_b = PS("ps_b", [128, 512], F32)
    epsc = T("epsc", [128, 1], F32)
    sems = io.get("_sems") or Sems(
        nc, stk, ["ld", "wld", "xd0", "xd1", "lnV", "lnA", "tr",
                  "ev", "mm", "pe", "st", "st0", "st1", "bc",
                  "vch", "ach"], prefix="A_")

    @block.vector
    def _(v):
        v.memset(epsc[:, :], EPS)

    @block.gpsimd
    def _(g):
        g.dma_start(out=idnt[:, :], in_=ident_in[:, :]).then_inc(sems.h["ld"], 16)
        g.dma_start(out=bqs[:, :], in_=bq[:, :]).then_inc(sems.h["ld"], 16)
        g.dma_start(out=bks[:, :], in_=bk[:, :]).then_inc(sems.h["ld"], 16)
        g.dma_start(out=bvrow[:, :], in_=bv[:, :]).then_inc(sems.h["ld"], 16)
        g.dma_start(out=onesr[:, :], in_=ones_row_in[:, :]).then_inc(sems.h["ld"], 16)
        g.dma_start(out=wsb.rearrange("p (c o) -> p c o", c=NCH),
                    in_=wv.ap().rearrange("(c p) o -> p c o", p=128)
                    ).then_inc(sems.h["wld"], 16)
        g.dma_start(out=wsb2.rearrange("p (c o) -> p c o", c=NCH),
                    in_=wk.ap().rearrange("(c p) o -> p c o", p=128)
                    ).then_inc(sems.h["wld"], 16)
    sems.n["ld"] = 16 * 5
    sems.n["wld"] = 32

    @block.tensor
    def _(t):
        sems.w(t, "ld")
        t.matmul(ps_a[:, :], onesr[0:1, :], bvrow[0:1, 0:512],
                 start=True, stop=True).then_inc(sems.h["bc"], 1)
        t.matmul(ps_b[:, :], onesr[0:1, :], bvrow[0:1, 512:1024],
                 start=True, stop=True).then_inc(sems.h["bc"], 1)
    sems.n["bc"] = 2

    @block.vector
    def _(v):
        sems.w(v, "bc")
        v.tensor_copy(bvb[:, 0:512], ps_a[:, :])
        v.tensor_copy(bvb[:, 512:1024], ps_b[:, :]).then_inc(sems.h["bc"], 1)
    sems.n["bc"] = 3

    def st_col(i, j):
        return stats[:, 8 * i + j: 8 * i + j + 1]

    for i in range(NT):
        pb = i % 2

        @block.sync
        def _(s, i=i, pb=pb):
            if i >= 2:
                s.wait_ge(sems.h["tr"], i - 1)
            s.dma_start(out=xt[:, pb * D:(pb + 1) * D],
                        in_=x_in[i * 128:(i + 1) * 128, :]
                        ).then_inc(sems.h["xd" + str(pb)], 16)
        sems.n["xd" + str(pb)] += 16

        @block.vector
        def _(v, i=i, pb=pb):
            v.wait_ge(sems.h["xd" + str(pb)], 16 * (i // 2 + 1))
            xap = xt[:, pb * D:(pb + 1) * D]
            ins = v.reduce_sum(st_col(i, 0), xap, axis=AX.X)
            sems.ch(ins, v, "vch")
            ins = v.tensor_scalar(st_col(i, 1), st_col(i, 0), -1.0 / D, None,
                                  ALU.mult)
            ins.then_inc(sems.h["lnV"], 1)
            v.wait_ge(sems.h["lnV"], sems.n["lnV"] + 1)
        sems.n["lnV"] += 1

        @block.scalar
        def _(sc, i=i, pb=pb):
            sc.wait_ge(sems.h["xd" + str(pb)], 16 * (i // 2 + 1))
            if i >= 2:
                sc.wait_ge(sems.h["tr"], i - 1)
            xap = xt[:, pb * D:(pb + 1) * D]
            sc.activation(xn[:, pb * D:(pb + 1) * D], xap, AF.Square,
                          accum_out=st_col(i, 2)).then_inc(sems.h["lnA"], 1)
        sems.n["lnA"] += 1

        @block.vector
        def _(v, i=i):
            v.wait_ge(sems.h["lnA"], sems.n["lnA"])
            v.tensor_tensor(st_col(i, 3), st_col(i, 1), st_col(i, 1), ALU.mult)
            ins = v.tensor_scalar(st_col(i, 4), st_col(i, 2), 1.0 / D, None,
                                  ALU.mult)
            sems.ch(ins, v, "vch")
            v.tensor_tensor(st_col(i, 4), st_col(i, 4), st_col(i, 3),
                            ALU.subtract).then_inc(sems.h["lnV"], 1)
        sems.n["lnV"] += 1

        @block.scalar
        def _(sc, i=i):
            sc.wait_ge(sems.h["lnV"], sems.n["lnV"])
            ins = sc.activation(st_col(i, 5), st_col(i, 4), AF.Ln,
                                bias=epsc[:, 0:1])
            sems.ch(ins, sc, "ach")
            sc.activation(st_col(i, 5), st_col(i, 5), AF.Exp,
                          scale=-0.5).then_inc(sems.h["lnA"], 1)
        sems.n["lnA"] += 1

        @block.vector
        def _(v, i=i, pb=pb):
            v.wait_ge(sems.h["lnA"], sems.n["lnA"])
            if i >= 2:
                v.wait_ge(sems.h["tr"], i - 1)
            xap = xt[:, pb * D:(pb + 1) * D]
            xnap = xn[:, pb * D:(pb + 1) * D]
            v.tensor_scalar(xnap, xap, st_col(i, 1), st_col(i, 5),
                            ALU.add, ALU.mult).then_inc(sems.h["ev"], 1)
        sems.n["ev"] += 1

        @block.tensor
        def _(t, i=i, pb=pb):
            t.wait_ge(sems.h["ev"], sems.n["ev"])
            if i == 0:
                sems.w(t, "ld")
            if i >= 1:
                t.wait_ge(sems.h["pe"], i)
            for c in range(NCH):
                ins = t.transpose(ps_tr[:, c * 128:(c + 1) * 128],
                                  xn[:, pb * D + c * 128: pb * D + (c + 1) * 128],
                                  idnt[:, :])
            ins.then_inc(sems.h["tr"], 1)
        sems.n["tr"] += 1

        @block.vector
        def _(v, i=i):
            v.wait_ge(sems.h["tr"], sems.n["tr"])
            dst = xnT.rearrange("p (c t) -> p c t", c=NCH)[
                :, :, i * 128:(i + 1) * 128]
            src = ps_tr.rearrange("p (c t) -> p c t", c=NCH)
            v.tensor_copy(dst, src).then_inc(sems.h["pe"], 1)
        sems.n["pe"] += 1

    def proj(name, w_sb, wld_need, bias_ap, out_sb, out_dram, scale,
             evac_engine):
        it = 0
        for dt in range(8):
            for qc in range(4):
                psu = ps_a if it % 2 == 0 else ps_b
                @block.tensor
                def _(t, dt=dt, qc=qc, it=it, psu=psu):
                    t.wait_ge(sems.h["pe"], min(NT, 4 * qc + 4))
                    if it == 0:
                        t.wait_ge(sems.h["wld"], wld_need)
                        t.wait_ge(sems.h["bc"], 3)
                        t.wait_ge(sems.h["ev"], sems.n["ev"])
                    if it >= 2:
                        t.wait_ge(sems.h["ev"], sems.n["ev"] - 1)
                    for c in range(NCH):
                        ins = t.matmul(psu[:, :],
                                 w_sb[:, c * D + dt * 128: c * D + (dt + 1) * 128],
                                 xnT[:, c * TOK + qc * 512: c * TOK + (qc + 1) * 512],
                                 start=(c == 0), stop=(c == NCH - 1))
                    ins.then_inc(sems.h["mm"], 1)
                sems.n["mm"] += 1

                eng = block.scalar if evac_engine == "scalar" else block.vector
                @eng
                def _(e, dt=dt, qc=qc, it=it, psu=psu):
                    e.wait_ge(sems.h["mm"], sems.n["mm"])
                    dst = out_sb[:, dt * TOK + qc * 512: dt * TOK + (qc + 1) * 512]
                    if evac_engine == "scalar":
                        e.activation(dst, psu[:, :], AF.Identity,
                                     bias=bias_ap(dt), scale=scale
                                     ).then_inc(sems.h["ev"], 1)
                    else:
                        e.tensor_tensor(dst, psu[:, :], bias_ap(dt),
                                        ALU.add).then_inc(sems.h["ev"], 1)
                sems.n["ev"] += 1
                it += 1

    bq8 = T("bq8", [128, 8], F32)
    @block.vector
    def _(v):
        sems.w(v, "ld")
        v.tensor_scalar(bq8[:, :], bqs[:, :], 0.125, None, ALU.mult
                        ).then_inc(sems.h["ev"], 1)
    sems.n["ev"] += 1

    it = 0
    for tt in range(NT):
        for dvh in range(2):
            psu = ps_a if it % 2 == 0 else ps_b
            @block.tensor
            def _(t, tt=tt, dvh=dvh, it=it, psu=psu):
                t.wait_ge(sems.h["pe"], min(NT, tt + 1))
                if it == 0:
                    t.wait_ge(sems.h["wld"], 32)
                    t.wait_ge(sems.h["bc"], 3)
                    t.wait_ge(sems.h["ev"], sems.n["ev"])
                if it >= 2:
                    t.wait_ge(sems.h["ev"], sems.n["ev"] - 1)
                for c in range(NCH):
                    ins = t.matmul(psu[:, :],
                             xnT[:, c * TOK + tt * 128: c * TOK + (tt + 1) * 128],
                             wsb[:, c * D + dvh * 512: c * D + (dvh + 1) * 512],
                             start=(c == 0), stop=(c == NCH - 1))
                ins.then_inc(sems.h["mm"], 1)
            sems.n["mm"] += 1

            @block.vector
            def _(v, tt=tt, dvh=dvh, it=it, psu=psu):
                v.wait_ge(sems.h["mm"], sems.n["mm"])
                if tt >= 2 and dvh == 0:
                    v.wait_ge(sems.h["st" + str(tt % 2)], 16 * (tt // 2))
                dst = vout[:, (tt % 2) * D + dvh * 512:
                           (tt % 2) * D + (dvh + 1) * 512]
                v.tensor_tensor(dst, psu[:, :],
                                bvb[:, dvh * 512:(dvh + 1) * 512],
                                ALU.add).then_inc(sems.h["ev"], 1)
            sems.n["ev"] += 1
            it += 1

        @block.sync
        def _(s, tt=tt):
            s.wait_ge(sems.h["ev"], sems.n["ev"])
            s.dma_start(out=v_o[tt * 128:(tt + 1) * 128, :],
                        in_=vout[:, (tt % 2) * D:(tt % 2 + 1) * D]
                        ).then_inc(sems.h["st" + str(tt % 2)], 16)
        sems.n["st" + str(tt % 2)] += 16
    v_stores = (sems.n["st0"], sems.n["st1"])

    @block.gpsimd
    def _(g, vmm=sems.n["mm"]):
        g.wait_ge(sems.h["mm"], vmm)
        g.dma_start(out=wsb.rearrange("p (c o) -> p c o", c=NCH),
                    in_=wq.ap().rearrange("(c p) o -> p c o", p=128)
                    ).then_inc(sems.h["wld"], 16)
    sems.n["wld"] += 16

    if fused:
        @block.gpsimd
        def _(g):
            g.wait_ge(sems.h["st0"], v_stores[0])
            g.wait_ge(sems.h["st1"], v_stores[1])
            g.collective_compute(
                "AllGather", mybir.AluOpType.bypass,
                replica_groups=PAIRS_RG,
                ins=[io["agvsrc"].ap().opt()],
                outs=[io["agvdst"].ap().opt()],
            ).then_inc(sems.h["bc"], 1)
        sems.n["bc"] += 1

    proj("k", wsb2, 32, lambda dt: bks[:, dt:dt + 1], kT, kT_o, 1.0,
         "scalar")
    k_evacs = sems.n["ev"]

    @block.sync
    def _(s):
        s.wait_ge(sems.h["ev"], k_evacs)
        s.dma_start(out=kT_o[:, :], in_=kT[:, :]).then_inc(sems.h["st"], 16)
    sems.n["st"] += 16
    k_store = sems.n["st"]

    if fused:
        @block.gpsimd
        def _(g):
            g.wait_ge(sems.h["st"], k_store)
            g.collective_compute(
                "AllGather", mybir.AluOpType.bypass,
                replica_groups=PAIRS_RG,
                ins=[io["agksrc"].ap().opt()],
                outs=[io["agkdst"].ap().opt()],
            ).then_inc(sems.h["bc"], 1)
        sems.n["bc"] += 1

    proj("q", wsb, sems.n["wld"], lambda dt: bq8[:, dt:dt + 1], qT, qT_o,
         0.125, "scalar")

    if fused:
        @block.gpsimd
        def _(g):
            g.wait_ge(sems.h["bc"], sems.n["bc"])

    @block.sync
    def _(s):
        s.wait_ge(sems.h["ev"], sems.n["ev"])
        s.dma_start(out=qT_o[:, :], in_=qT[:, :]).then_inc(sems.h["st"], 16)
        sems.n["st"] += 16
        s.wait_ge(sems.h["st"], sems.n["st"])
        s.wait_ge(sems.h["st0"], sems.n["st0"])
        s.wait_ge(sems.h["st1"], sems.n["st1"])


def build_A():
    nc = bass.Bass("TRN2", target_bir_lowering=False)
    io = dram_A(nc)
    with contextlib.ExitStack() as stk:
        io["_sb"], io["_ps"] = alloc_pools(nc, stk)
        block = stk.enter_context(nc.Block())
        emit_A(nc, block, stk, io)
    return nc


def dram_B(nc):
    io = {}
    io["x_in"] = nc.dram_tensor("x_in", [TOK, D], F32, kind="ExternalInput")
    io["qT_i"] = nc.dram_tensor("qT_i", [128, NCH * TOK], BF16, kind="ExternalInput")
    io["kf"] = nc.dram_tensor("kf", [128, NCH * SKEY], BF16, kind="ExternalInput")
    io["vf"] = nc.dram_tensor("vf", [SKEY, D], BF16, kind="ExternalInput")
    io["rs_in"] = nc.dram_tensor("rs", [H0 * 128, RSW], F32, kind="ExternalInput")
    io["rel16"] = nc.dram_tensor("rel16", [H0, 8192], F32, kind="ExternalInput")
    io["gw_in"] = nc.dram_tensor("gw", [128, NCH * 8], F32, kind="ExternalInput")
    io["gb_in"] = nc.dram_tensor("gb", [8, 1], F32, kind="ExternalInput")
    io["ident_in"] = nc.dram_tensor("ident", [128, 128], F32, kind="ExternalInput")
    io["ones_row_in"] = nc.dram_tensor("ones_row", [1, 128], F32, kind="ExternalInput")
    io["ones_col_in"] = nc.dram_tensor("ones_col", [128, 1], F32, kind="ExternalInput")
    io["x1_o"] = nc.dram_tensor("x1_o", [TOK, D], F32, kind="ExternalOutput")
    io["xn1T_o"] = nc.dram_tensor("xn1T_o", [128, NCH * TOK], BF16, kind="ExternalOutput")
    io["cnt_o"] = nc.dram_tensor("cnt_o", [1, 8], F32, kind="ExternalOutput")
    return io


def emit_B(nc, block, stk, io, fused=False):
    nc = bass.Bass("TRN2", target_bir_lowering=False)
    KT = SKEY // 128
    x_in = io["x_in"]
    qT_i = io["qT_i"]
    kf = io["kf"]
    vf = io["vf"]
    rs_in = io["rs_in"]
    rel16 = io["rel16"]
    gw_in = io["gw_in"]
    gb_in = io["gb_in"]
    ident_in = io["ident_in"]
    ones_row_in = io["ones_row_in"]
    ones_col_in = io["ones_col_in"]
    x1_o = io["x1_o"]
    xn1T_o = io["xn1T_o"]
    cnt_o = io["cnt_o"]
    T = lambda nm, shape, dt: io["_sb"].alloc(shape[-1], dt, rows=shape[0])
    PS = lambda nm, shape, dt: io["_ps"].alloc(nm, shape, dt)
    qT = T("qT", [128, NCH * TOK], BF16)
    kvA_k = T("kvA_k", [128, SKEY], BF16)
    kvB_k = T("kvB_k", [128, SKEY], BF16)
    kvA_v = T("kvA_v", [128, KT * 130], BF16)
    kvB_v = T("kvB_v", [128, KT * 130], BF16)
    vraw = T("vraw", [128, KT * 128], BF16)
    rsA = T("rsA", [128, RSW], BF16)
    rsB = T("rsB", [128, RSW], BF16)
    U = T("U", [128, NT * H0 * 66], BF16)
    xn1T = io["_sb"].alloc_at(0, NCH * TOK, BF16)
    ssb = T("ssb", [128, 8 * 512], F32)
    pt = T("pt", [128, 8 * 512], BF16)
    sq = T("sq", [128, 1024], BF16)
    nrm = T("nrm", [1, 16], F32)
    mrow = T("mrow", [1, 16], F32)
    bmaxr = T("bmaxr", [1, 16], F32)
    bm16m = T("bm16m", [16, 1], F32)
    bm16t = T("bm16t", [16, 1], F32)
    negM2 = T("negM2", [128, 2], F32)
    idnt = T("idnt", [128, 128], BF16)
    idnt_f = T("idnt_f", [128, 128], F32)
    onesr = T("onesr", [1, 128], F32)
    onesrb = T("onesrb", [1, 128], BF16)
    onesc = T("onesc", [128, 1], BF16)
    epsc = T("epsc", [128, 1], F32)
    yt = T("yt", [128, D], F32)
    rcol = T("rcol", [128, 16], F32)
    xt = T("xt", [128, 2 * D], F32)
    xn = T("xn", [128, 2 * D], BF16)
    x1t = T("x1t", [128, 2 * D], F32)
    stats = T("stats", [128, 16 * NT], F32)
    gwsb = T("gwsb", [128, NCH * 8], BF16)
    gbsb = T("gbsb", [8, 1], F32)
    gTfull = T("gT", [128, TOK], BF16)
    gT = gTfull[0:8, 0:TOK]
    tb = gTfull[0:128, 0:512]
    gtt = T("gtt", [128, 8 * NT], F32)
    gsel = T("gsel", [128, 8 * NT], F32)
    gm = T("gm", [128, 4 * NT], F32)
    cnt = T("cnt", [1, 8], F32)
    gscr = T("gscr", [128, 8], F32)
    onesc_f = T("onesc_f", [128, 1], F32)
    ps_S0 = PS("ps_S0", [128, 512], F32)
    ps_S1 = PS("ps_S1", [128, 512], F32)
    ps_o = PS("ps_o", [65, 512], F32)
    ps_y = PS("ps_y", [128, 4 * 66], BF16)
    ps_n = PS("ps_n", [1, 512], F32)
    ps_b = PS("ps_b", [128, 8], F32)
    ps_tr = PS("ps_tr", [128, D], BF16)
    ps_g = PS("ps_g", [8, 512], F32)
    sH = [[io["_ps"].view_at(0, [128, 512], F32),
           io["_ps"].view_at(1, [128, 512], F32)],
          [io["_ps"].view_at(6, [128, 512], F32),
           io["_ps"].view_at(7, [128, 512], F32)]]
    ps_o2 = io["_ps"].view_at(5, [65, 512], F32)
    ps_oH = [ps_o, ps_o2]

    sems = io.get("_sems") or Sems(nc, stk, [
        "ld", "ld2", "qld", "kv0", "kv1", "kvr", "rs0", "rs1", "bmv",
        "mm", "bias", "exp", "evO", "trY", "uev", "mh",
        "xd0", "xd1", "lnV", "lnA", "vch", "ach", "tr", "pe",
        "st", "st0", "st1", "gmm", "gev", "bmx", "ev"], prefix="B_")

    @block.gpsimd
    def _(g):
        g.dma_start(out=idnt[:, :], in_=ident_in[:, :]).then_inc(sems.h["ld"], 16)
        g.dma_start(out=onesrb[:, :], in_=ones_row_in[:, :]).then_inc(sems.h["ld"], 16)
        g.dma_start(out=onesc[:, :], in_=ones_col_in[:, :]).then_inc(sems.h["ld"], 16)
        g.dma_start(out=gwsb[:, :], in_=gw_in[:, :]).then_inc(sems.h["ld"], 16)
    sems.n["ld"] += 64

    @block.sync
    def _(s):
        s.dma_start(out=idnt_f[:, :], in_=ident_in[:, :]).then_inc(sems.h["ld2"], 16)
        s.dma_start(out=onesr[:, :], in_=ones_row_in[:, :]).then_inc(sems.h["ld2"], 16)
        s.dma_start(out=gbsb[:, :], in_=gb_in[:, :]).then_inc(sems.h["ld2"], 16)
        s.dma_start(out=onesc_f[:, :], in_=ones_col_in[:, :]).then_inc(sems.h["ld2"], 16)
        s.dma_start(out=qT[:, :], in_=qT_i[:, :]).then_inc(sems.h["qld"], 16)
    sems.n["ld2"] += 64
    sems.n["qld"] += 16

    @block.vector
    def _(v):
        v.memset(epsc[:, :], EPS)
        for buf in (kvA_v, kvB_v):
            v.memset(buf.rearrange("p (k h c) -> p k h c", k=KT, h=2)[
                :, :, :, 64:65], 1.0)

    @block.gpsimd
    def _(g):
        g.dma_start(out=gTfull[0:16, 0:2048], in_=rel16[:, 0:2048]
                    ).then_inc(sems.h["bmx"], 16)
    sems.n["bmx"] += 16

    for j in range(4):
        @block.vector
        def _(v, j=j):
            v.wait_ge(sems.h["bmx"], sems.n["bmx"])
            op = v.tensor_reduce(bm16m[:, :] if j == 0 else bm16t[:, :],
                                 gTfull[0:16, 0:2048], axis=AX.X, op=ALU.max,
                                 apply_absolute_value=True)
            if j > 0:
                sems.ch(op, v, "vch")
                op = v.tensor_tensor(bm16m[:, :], bm16m[:, :],
                                     bm16t[:, :], ALU.max)
            op.then_inc(sems.h["bmv"], 1)
        sems.n["bmv"] += 1
        if j < 3:
            @block.gpsimd
            def _(g, j=j):
                g.wait_ge(sems.h["bmv"], sems.n["bmv"])
                g.dma_start(out=gTfull[0:16, 0:2048],
                            in_=rel16[:, (j + 1) * 2048:(j + 2) * 2048]
                            ).then_inc(sems.h["bmx"], 16)
            sems.n["bmx"] += 16

    @block.tensor
    def _(t):
        sems.w(t, "bmx")
        sems.w(t, "bmv")
        sems.w(t, "ld")
        sems.w(t, "ld2")
        t.matmul(ps_n[0:1, 0:16], bm16m[0:16, 0:1], idnt_f[0:16, 0:16],
                 start=True, stop=True).then_inc(sems.h["bmv"], 1)
    sems.n["bmv"] += 1

    @block.vector
    def _(v):
        sems.w(v, "bmv")
        v.tensor_copy(bmaxr[:, :], ps_n[0:1, 0:16]).then_inc(sems.h["bmv"], 1)
    sems.n["bmv"] += 1
    bmx_done = sems.n["bmv"]

    def kv_load(grp):
        par = grp % 2
        kbuf = kvA_k if par == 0 else kvB_k
        sem = "kv" + str(par)

        @block.sync
        def _(s, grp=grp, kbuf=kbuf, sem=sem, par=par):
            if grp >= 2:
                s.wait_ge(sems.h["mm"], kv_war[par])
            if grp >= 1:
                o = "kv" + str(1 - par)
                s.wait_ge(sems.h[o], sems.n[o])
            if fused:
                for m_ in range(2):
                    s.dma_start(out=kbuf[:, m_ * TOK:(m_ + 1) * TOK],
                                in_=kf[m_ * 128:(m_ + 1) * 128,
                                       grp * TOK:(grp + 1) * TOK]
                                ).then_inc(sems.h[sem], 16)
            else:
                s.dma_start(out=kbuf[:, :],
                            in_=kf[:, grp * SKEY:(grp + 1) * SKEY]
                            ).then_inc(sems.h[sem], 16)
            s.dma_start(
                out=vraw.rearrange("p (k c) -> p k c", k=KT),
                in_=vf.ap().rearrange("(k p) d -> p k d", p=128)[
                    :, :, grp * 128:(grp + 1) * 128],
                ).then_inc(sems.h[sem], 16)
        sems.n[sem] += 48 if fused else 32

    def kv_restripe(grp):
        par = grp % 2
        vbuf = kvA_v if par == 0 else kvB_v
        sem = "kv" + str(par)

        @block.vector
        def _(v, grp=grp, vbuf=vbuf, sem=sem, par=par):
            v.wait_ge(sems.h[sem], sems.n[sem])
            if grp >= 2:
                v.wait_ge(sems.h["mm"], kv_war[par])
            ins = v.tensor_copy(
                vbuf.rearrange("p (k h c) -> p k h c", k=KT, h=2)[
                    :, :, :, 0:64],
                vraw.rearrange("p (k h c) -> p k h c", k=KT, h=2))
            ins.then_inc(sems.h[sem], 1)
        sems.n[sem] += 1

    kv_war = [0, 0]
    rs_war = [0, 0]
    x1_done = []

    def rs_load(h):
        par = h % 2
        buf = rsA if par == 0 else rsB
        sem = "rs" + str(par)

        @block.gpsimd
        def _(g, h=h, buf=buf, sem=sem, par=par):
            if h >= 2:
                g.wait_ge(sems.h["bias"], rs_war[par])
            g.dma_start(out=buf[:, :],
                        in_=rs_in[h * 128:(h + 1) * 128, :]
                        ).then_inc(sems.h[sem], 16)
        sems.n[sem] += 16

    kv_load(0)
    kv_restripe(0)
    kv_load(1)
    rs_load(0)
    rs_load(1)

    for grp in range(8):
        par = grp % 2
        kbuf = kvA_k if par == 0 else kvB_k
        vbuf = kvA_v if par == 0 else kvB_v
        if grp >= 1:
            kv_restripe(grp)
            if grp + 1 < 8:
                kv_load(grp + 1)
        negm_ready = {}
        for h_loc in range(2):
            h = grp * 2 + h_loc
            hb = h_loc * 64

            segs = [(0, kbuf[hb:hb + 64, i * 1024:(i + 1) * 1024],
                     2, 2 * i) for i in range(4)]
            segs += [(64, qT[hb:hb + 64,
                          (h // 2) * TOK + i * 1024:
                          (h // 2) * TOK + (i + 1) * 1024], 2, 8 + 2 * i)
                     for i in range(2)]
            for (rb, srcap0, ncks, ncol) in segs:
                @block.scalar
                def _(sc, h=h, rb=rb, srcap0=srcap0):
                    sc.wait_ge(sems.h["kv" + str(grp % 2)],
                               sems.n["kv" + str(grp % 2)])
                    sc.wait_ge(sems.h["qld"], 16)
                    sc.wait_ge(sems.h["bmv"], bmx_done)
                    sc.wait_ge(sems.h["mh"], sems.n["mh"])
                    sc.activation(sq[rb:rb + 64, 0:1024], srcap0,
                                  AF.Square).then_inc(sems.h["mh"], 1)
                sems.n["mh"] += 1

                for j in range(ncks):
                    @block.tensor
                    def _(t, j=j, rb=rb):
                        t.wait_ge(sems.h["mh"], sems.n["mh"])
                        t.wait_ge(sems.h["ld"], 64)
                        t.matmul(ps_n[0:1, 0:512], onesc[rb:rb + 64, 0:1],
                                 sq[rb:rb + 64, j * 512:(j + 1) * 512],
                                 start=True, stop=True,
                                 tile_position=(rb, 0)
                                 ).then_inc(sems.h["mm"], 1)
                    sems.n["mm"] += 1

                    @block.vector
                    def _(v, j=j, ncol=ncol):
                        v.wait_ge(sems.h["mm"], sems.n["mm"])
                        ins = v.tensor_reduce(nrm[0:1, ncol + j:ncol + j + 1],
                                              ps_n[0:1, 0:512],
                                              axis=AX.X, op=ALU.max)
                        ins.then_inc(sems.h["mh"], 1)
                    sems.n["mh"] += 1

            @block.vector
            def _(v, h=h):
                v.wait_ge(sems.h["mh"], sems.n["mh"])
                ins = v.tensor_reduce(mrow[0:1, 0:1], nrm[0:1, 0:8],
                                      axis=AX.X, op=ALU.max)
                sems.ch(ins, v, "vch")
                ins = v.tensor_reduce(mrow[0:1, 1:2],
                                      nrm[0:1, 8:12],
                                      axis=AX.X, op=ALU.max)
                sems.ch(ins, v, "vch")
                ins = v.tensor_tensor(mrow[0:1, 2:3], mrow[0:1, 0:1],
                                      mrow[0:1, 1:2], ALU.mult)
                ins.then_inc(sems.h["lnV"], 1)
            sems.n["lnV"] += 1

            @block.scalar
            def _(sc, h=h):
                sc.wait_ge(sems.h["lnV"], sems.n["lnV"])
                ins = sc.activation(mrow[0:1, 3:4], mrow[0:1, 2:3], AF.Ln)
                sems.ch(ins, sc, "ach")
                sc.activation(mrow[0:1, 4:5], mrow[0:1, 3:4], AF.Exp,
                              scale=0.5).then_inc(sems.h["lnA"], 1)
            sems.n["lnA"] += 1

            @block.vector
            def _(v, h=h):
                v.wait_ge(sems.h["lnA"], sems.n["lnA"])
                v.wait_ge(sems.h["bmv"], bmx_done)
                ins = v.tensor_scalar(mrow[0:1, 5:6], mrow[0:1, 4:5], 0.125,
                                      1.0, ALU.mult, ALU.add)
                sems.ch(ins, v, "vch")
                ins = v.tensor_tensor(mrow[0:1, 6:7], mrow[0:1, 5:6],
                                      bmaxr[0:1, h:h + 1], ALU.add)
                sems.ch(ins, v, "vch")
                ins = v.tensor_scalar(mrow[0:1, 7:8], mrow[0:1, 6:7], -1.0,
                                      None, ALU.mult)
                ins.then_inc(sems.h["lnV"], 1)
            sems.n["lnV"] += 1

            @block.tensor
            def _(t, h=h):
                t.wait_ge(sems.h["lnV"], sems.n["lnV"])
                t.wait_ge(sems.h["mh"], sems.n["mh"])
                t.wait_ge(sems.h["exp"], sems.n["exp"])
                t.wait_ge(sems.h["evO"], sems.n["evO"])
                t.wait_ge(sems.h["ld2"], 64)
                t.matmul(ps_b[:, 0:1], onesr[0:1, :], mrow[0:1, 7:8],
                         start=True, stop=True).then_inc(sems.h["mm"], 1)
            sems.n["mm"] += 1

            @block.vector
            def _(v, h=h, h_loc=h_loc):
                v.wait_ge(sems.h["mm"], sems.n["mm"])
                v.tensor_copy(negM2[:, h_loc:h_loc + 1], ps_b[:, 0:1]
                              ).then_inc(sems.h["mh"], 1)
            sems.n["mh"] += 1
            negm_ready[h_loc] = sems.n["mh"]

        for qc in range(4):
            evo_before = sems.n["evO"]
            biasb = sems.n["bias"]
            expb = sems.n["exp"]
            bias_after = {(j, hh): biasb + 2 * j + hh + 1
                          for j in range(KT) for hh in range(2)}
            exp_after = {(j, hh): expb + 2 * j + hh + 1
                         for j in range(KT) for hh in range(2)}
            qk_after = {}
            pv_after = {}

            @block.tensor
            def _(t, qc=qc, kbuf=kbuf, vbuf=vbuf, grp=grp):
                for kt in range(KT + 1):
                    if kt < KT:
                        if kt == 0:
                            t.wait_ge(sems.h["kv" + str(grp % 2)],
                                      sems.n["kv" + str(grp % 2)])
                        for hh in range(2):
                            if kt >= 2:
                                t.wait_ge(sems.h["bias"],
                                          bias_after[(kt - 2, hh)])
                            hb = hh * 64
                            h = grp * 2 + hh
                            t.matmul(sH[hh][kt % 2][:, :],
                                     kbuf[hb:hb + 64,
                                          kt * 128:(kt + 1) * 128],
                                     qT[hb:hb + 64,
                                        (h // 2) * TOK + qc * 512:
                                        (h // 2) * TOK + (qc + 1) * 512],
                                     start=True, stop=True,
                                     tile_position=(hb, 0)
                                     ).then_inc(sems.h["mm"], 1)
                            sems.n["mm"] += 1
                            qk_after[(kt, hh)] = sems.n["mm"]
                    if kt >= 1:
                        j = kt - 1
                        for hh in range(2):
                            t.wait_ge(sems.h["exp"], exp_after[(j, hh)])
                            if j == 0 and hh == 0:
                                t.wait_ge(sems.h["evO"], evo_before)
                            t.matmul(ps_oH[hh][:, :],
                                     vbuf[:, j * 130 + hh * 65:
                                          j * 130 + hh * 65 + 65],
                                     pt[:, (4 * hh + j % 4) * 512:
                                        (4 * hh + j % 4 + 1) * 512],
                                     start=(j == 0), stop=(j == KT - 1)
                                     ).then_inc(sems.h["mm"], 1)
                            sems.n["mm"] += 1
                            pv_after[(j, hh)] = sems.n["mm"]

            @block.vector
            def _(v, qc=qc, grp=grp):
                for j in range(KT):
                    for hh in range(2):
                        v.wait_ge(sems.h["mm"], qk_after[(j, hh)])
                        if j == 0:
                            v.wait_ge(sems.h["rs" + str(hh)],
                                      sems.n["rs" + str(hh)])
                        if j >= 4:
                            v.wait_ge(sems.h["exp"], exp_after[(j - 4, hh)])
                        c0 = 3968 + qc * 512 - j * 128
                        rsbuf_ = rsA if hh == 0 else rsB
                        sl = 4 * hh + j % 4
                        v.tensor_tensor(ssb[:, sl * 512:(sl + 1) * 512],
                                        sH[hh][j % 2][:, :],
                                        rsbuf_[:, c0:c0 + 512],
                                        ALU.add).then_inc(sems.h["bias"], 1)
                        sems.n["bias"] += 1
                        assert sems.n["bias"] == bias_after[(j, hh)]

            @block.scalar
            def _(sc, qc=qc):
                for j in range(KT):
                    for hh in range(2):
                        sc.wait_ge(sems.h["bias"], bias_after[(j, hh)])
                        if j == 0:
                            sc.wait_ge(sems.h["mh"], negm_ready[1])
                        if j >= 4:
                            sc.wait_ge(sems.h["mm"], pv_after[(j - 4, hh)])
                        sl = 4 * hh + j % 4
                        sc.activation(pt[:, sl * 512:(sl + 1) * 512],
                                      ssb[:, sl * 512:(sl + 1) * 512],
                                      AF.Exp, bias=negM2[:, hh:hh + 1]
                                      ).then_inc(sems.h["exp"], 1)
                        sems.n["exp"] += 1
                        assert sems.n["exp"] == exp_after[(j, hh)]

            for hh in range(2):
                h = grp * 2 + hh

                @block.vector
                def _(v, h=h, qc=qc, hh=hh):
                    v.wait_ge(sems.h["mm"], pv_after[(KT - 1, hh)])
                    v.wait_ge(sems.h["trY"], sems.n["trY"])
                    v.tensor_copy(tb[0:65, :], ps_oH[hh][:, :]).then_inc(
                        sems.h["evO"], 1)
                sems.n["evO"] += 1

                @block.tensor
                def _(t, h=h, qc=qc):
                    t.wait_ge(sems.h["evO"], sems.n["evO"])
                    t.wait_ge(sems.h["uev"], sems.n["uev"])
                    t.wait_ge(sems.h["ld"], 64)
                    t.wait_ge(sems.h["ld2"], 64)
                    for j in range(4):
                        ins = t.transpose(ps_y[:, j * 66:j * 66 + 65],
                                          tb[0:65, j * 128:(j + 1) * 128],
                                          idnt[0:65, 0:65])
                    ins.then_inc(sems.h["trY"], 1)
                sems.n["trY"] += 1

                @block.vector
                def _(v, h=h, qc=qc):
                    v.wait_ge(sems.h["trY"], sems.n["trY"])
                    dst = U.rearrange("p (t s) -> p t s", t=NT)[
                        :, 4 * qc: 4 * qc + 4, h * 66:h * 66 + 65]
                    srcp = ps_y.rearrange("p (j s) -> p j s", j=4)[
                        :, :, 0:65]
                    v.tensor_copy(dst, srcp).then_inc(sems.h["uev"], 1)
                sems.n["uev"] += 1
        rs_war[0] = sems.n["bias"]
        rs_war[1] = sems.n["bias"]
        if grp + 1 < 8:
            rs_load(2 * grp + 2)
            rs_load(2 * grp + 3)
        kv_war[grp % 2] = sems.n["mm"]


    def st_col(i, j):
        return stats[:, 16 * i + j: 16 * i + j + 1]

    uev_total = sems.n["uev"]

    for tt in range(NT):
        par = tt % 2
        sp = str(par)

        @block.sync
        def _(s, tt=tt, par=par, sp=sp):
            if tt >= 2:
                s.wait_ge(sems.h["ev"], x1_done[tt - 2])
            s.dma_start(out=xt[:, par * D:(par + 1) * D],
                        in_=x_in[tt * 128:(tt + 1) * 128, :]
                        ).then_inc(sems.h["xd" + sp], 16)
        sems.n["xd" + sp] += 16

        @block.vector
        def _(v, tt=tt, par=par):
            if tt == 0:
                v.wait_ge(sems.h["uev"], uev_total)
            else:
                v.wait_ge(sems.h["lnA"], sems.n["lnA"])
            uv = U.rearrange("p (t h s) -> p t h s", t=NT, h=H0)
            ins = v.reciprocal(rcol[:, 0:16], uv[:, tt, :, 64:65])
            sems.ch(ins, v, "vch")
            for h in range(H0):
                ins = v.tensor_scalar(yt[:, h * 64:(h + 1) * 64],
                                      uv[:, tt, h, 0:64],
                                      rcol[:, h:h + 1], None, ALU.mult)
            sems.ch(ins, v, "vch")
            ins = v.reduce_sum(st_col(tt, 0), yt[:, :], axis=AX.X)
            sems.ch(ins, v, "vch")
            ins = v.tensor_scalar(st_col(tt, 1), st_col(tt, 0), -1.0 / D,
                                  None, ALU.mult)
            ins.then_inc(sems.h["lnV"], 1)
            v.wait_ge(sems.h["lnV"], sems.n["lnV"] + 1)
        sems.n["lnV"] += 1
        yt_ready = sems.n["lnV"]

        @block.scalar
        def _(sc, tt=tt, par=par):
            sc.wait_ge(sems.h["lnV"], yt_ready)
            if tt >= 2:
                sc.wait_ge(sems.h["tr"], sems.n["tr"])
            sc.activation(xn[:, par * D:(par + 1) * D], yt[:, :], AF.Square,
                          accum_out=st_col(tt, 2)).then_inc(sems.h["lnA"], 1)
        sems.n["lnA"] += 1

        @block.vector
        def _(v, tt=tt):
            v.wait_ge(sems.h["lnA"], sems.n["lnA"])
            v.tensor_tensor(st_col(tt, 3), st_col(tt, 1), st_col(tt, 1), ALU.mult)
            ins = v.tensor_scalar(st_col(tt, 4), st_col(tt, 2), 1.0 / D,
                                  None, ALU.mult)
            sems.ch(ins, v, "vch")
            v.tensor_tensor(st_col(tt, 4), st_col(tt, 4), st_col(tt, 3),
                            ALU.subtract).then_inc(sems.h["lnV"], 1)
        sems.n["lnV"] += 1

        @block.scalar
        def _(sc, tt=tt):
            sc.wait_ge(sems.h["lnV"], sems.n["lnV"])
            ins = sc.activation(st_col(tt, 5), st_col(tt, 4), AF.Ln,
                                bias=epsc[:, 0:1])
            sems.ch(ins, sc, "ach")
            sc.activation(st_col(tt, 5), st_col(tt, 5), AF.Exp,
                          scale=-0.5).then_inc(sems.h["lnA"], 1)
        sems.n["lnA"] += 1

        @block.vector
        def _(v, tt=tt, par=par, sp=sp):
            v.wait_ge(sems.h["lnA"], sems.n["lnA"])
            v.wait_ge(sems.h["xd" + sp], 16 * (tt // 2 + 1))
            if tt >= 2:
                v.wait_ge(sems.h["st" + sp], 16 * (tt // 2))
            ins = v.tensor_scalar(x1t[:, par * D:(par + 1) * D], yt[:, :],
                                  st_col(tt, 1), st_col(tt, 5),
                                  ALU.add, ALU.mult)
            sems.ch(ins, v, "vch")
            ins = v.tensor_tensor(x1t[:, par * D:(par + 1) * D],
                                  x1t[:, par * D:(par + 1) * D],
                                  xt[:, par * D:(par + 1) * D], ALU.add)
            ins.then_inc(sems.h["ev"], 1)
            v.wait_ge(sems.h["ev"], sems.n["ev"] + 1)
            ins = v.reduce_sum(st_col(tt, 8), x1t[:, par * D:(par + 1) * D],
                               axis=AX.X)
            sems.ch(ins, v, "vch")
            ins = v.tensor_scalar(st_col(tt, 9), st_col(tt, 8), -1.0 / D,
                                  None, ALU.mult)
            ins.then_inc(sems.h["lnV"], 1)
            v.wait_ge(sems.h["lnV"], sems.n["lnV"] + 1)
        sems.n["ev"] += 1
        sems.n["lnV"] += 1
        x1_done.append(0)
        x1_done[tt] = sems.n["ev"]

        @block.sync
        def _(s, tt=tt, par=par, sp=sp):
            s.wait_ge(sems.h["ev"], x1_done[tt])
            s.dma_start(out=x1_o[tt * 128:(tt + 1) * 128, :],
                        in_=x1t[:, par * D:(par + 1) * D]
                        ).then_inc(sems.h["st" + sp], 16)
        sems.n["st" + sp] += 16

        @block.scalar
        def _(sc, tt=tt, par=par):
            sc.wait_ge(sems.h["ev"], x1_done[tt])
            if tt >= 2:
                sc.wait_ge(sems.h["tr"], sems.n["tr"])
            sc.activation(xn[:, par * D:(par + 1) * D],
                          x1t[:, par * D:(par + 1) * D], AF.Square,
                          accum_out=st_col(tt, 10)).then_inc(sems.h["lnA"], 1)
        sems.n["lnA"] += 1

        @block.vector
        def _(v, tt=tt):
            v.wait_ge(sems.h["lnA"], sems.n["lnA"])
            v.tensor_tensor(st_col(tt, 11), st_col(tt, 9), st_col(tt, 9),
                            ALU.mult)
            ins = v.tensor_scalar(st_col(tt, 12), st_col(tt, 10), 1.0 / D,
                                  None, ALU.mult)
            sems.ch(ins, v, "vch")
            v.tensor_tensor(st_col(tt, 12), st_col(tt, 12), st_col(tt, 11),
                            ALU.subtract).then_inc(sems.h["lnV"], 1)
        sems.n["lnV"] += 1

        @block.scalar
        def _(sc, tt=tt):
            sc.wait_ge(sems.h["lnV"], sems.n["lnV"])
            ins = sc.activation(st_col(tt, 13), st_col(tt, 12), AF.Ln,
                                bias=epsc[:, 0:1])
            sems.ch(ins, sc, "ach")
            sc.activation(st_col(tt, 13), st_col(tt, 13), AF.Exp,
                          scale=-0.5).then_inc(sems.h["lnA"], 1)
        sems.n["lnA"] += 1

        @block.vector
        def _(v, tt=tt, par=par):
            v.wait_ge(sems.h["lnA"], sems.n["lnA"])
            if tt >= 2:
                v.wait_ge(sems.h["tr"], sems.n["tr"])
            v.tensor_scalar(xn[:, par * D:(par + 1) * D],
                            x1t[:, par * D:(par + 1) * D],
                            st_col(tt, 9), st_col(tt, 13),
                            ALU.add, ALU.mult).then_inc(sems.h["ev"], 1)
        sems.n["ev"] += 1

        @block.tensor
        def _(t, tt=tt, par=par):
            t.wait_ge(sems.h["ev"], sems.n["ev"])
            if tt >= 1:
                t.wait_ge(sems.h["pe"], sems.n["pe"])
            for c in range(NCH):
                ins = t.transpose(ps_tr[:, c * 128:(c + 1) * 128],
                                  xn[:, par * D + c * 128:
                                     par * D + (c + 1) * 128],
                                  idnt[:, :])
            ins.then_inc(sems.h["tr"], 1)
        sems.n["tr"] += 1

        @block.vector
        def _(v, tt=tt):
            v.wait_ge(sems.h["tr"], sems.n["tr"])
            dst = xn1T.rearrange("p (c t) -> p c t", c=NCH)[
                :, :, tt * 128:(tt + 1) * 128]
            srcp = ps_tr.rearrange("p (c t) -> p c t", c=NCH)
            v.tensor_copy(dst, srcp).then_inc(sems.h["pe"], 1)
        sems.n["pe"] += 1

    gate_ev = []
    for qc in range(4):
        @block.tensor
        def _(t, qc=qc):
            t.wait_ge(sems.h["pe"], sems.n["pe"])
            t.wait_ge(sems.h["ld"], 64)
            t.wait_ge(sems.h["ld2"], 64)
            if qc >= 1:
                t.wait_ge(sems.h["gev"], sems.n["gev"])
            for c in range(NCH):
                ins = t.matmul(ps_g[:, :],
                               gwsb[:, c * 8:(c + 1) * 8],
                               xn1T[:, c * TOK + qc * 512:
                                    c * TOK + (qc + 1) * 512],
                               start=(c == 0), stop=(c == NCH - 1))
            ins.then_inc(sems.h["gmm"], 1)
        sems.n["gmm"] += 1

        @block.scalar
        def _(sc, qc=qc):
            sc.wait_ge(sems.h["gmm"], sems.n["gmm"])
            sc.activation(gT[:, qc * 512:(qc + 1) * 512], ps_g[:, :],
                          AF.Identity, bias=gbsb[:, 0:1]
                          ).then_inc(sems.h["gev"], 1)
        sems.n["gev"] += 1

    for tt in range(NT):
        @block.tensor
        def _(t, tt=tt):
            t.wait_ge(sems.h["gev"], sems.n["gev"])
            t.wait_ge(sems.h["uev"], sems.n["uev"])
            ins = t.transpose(ps_y[:, 0:8], gT[0:8, tt * 128:(tt + 1) * 128],
                              idnt[0:8, 0:8])
            ins.then_inc(sems.h["gmm"], 1)
        sems.n["gmm"] += 1

        @block.vector
        def _(v, tt=tt):
            v.wait_ge(sems.h["gmm"], sems.n["gmm"])
            v.tensor_copy(gtt[:, tt * 8:(tt + 1) * 8], ps_y[:, 0:8]
                          ).then_inc(sems.h["gev"], 1)
        sems.n["gev"] += 1

    for tt in range(NT):
        @block.vector
        def _(v, tt=tt):
            gt_ = gtt[:, tt * 8:(tt + 1) * 8]
            gs_ = gsel[:, tt * 8:(tt + 1) * 8]
            m1 = gm[:, tt * 4: tt * 4 + 1]
            m2 = gm[:, tt * 4 + 1: tt * 4 + 2]
            ins = v.tensor_reduce(m1, gt_, axis=AX.X, op=ALU.max)
            sems.ch(ins, v, "vch")
            ins = v.tensor_scalar(gs_, gt_, m1, None, ALU.is_ge)
            sems.ch(ins, v, "vch")
            ins = v.scalar_tensor_tensor(gscr[:, :], gs_, -1e30, gt_,
                                         ALU.mult, ALU.add)
            sems.ch(ins, v, "vch")
            ins = v.tensor_reduce(m2, gscr[:, :], axis=AX.X, op=ALU.max)
            sems.ch(ins, v, "vch")
            v.tensor_scalar(gs_, gt_, m2, None, ALU.is_ge
                            ).then_inc(sems.h["gev"], 1)
        sems.n["gev"] += 1

    @block.tensor
    def _(t):
        t.wait_ge(sems.h["gev"], sems.n["gev"])
        for tt in range(NT):
            ins = t.matmul(ps_n[0:1, 0:8], onesc_f[:, :],
                           gsel[:, tt * 8:(tt + 1) * 8],
                           start=(tt == 0), stop=(tt == NT - 1))
        ins.then_inc(sems.h["gmm"], 1)
    sems.n["gmm"] += 1

    @block.vector
    def _(v):
        v.wait_ge(sems.h["gmm"], sems.n["gmm"])
        v.tensor_copy(cnt[:, :], ps_n[0:1, 0:8]).then_inc(sems.h["gev"], 1)
    sems.n["gev"] += 1

    @block.sync
    def _(s):
        s.wait_ge(sems.h["pe"], sems.n["pe"])
        s.dma_start(out=xn1T_o[:, :], in_=xn1T[:, :]).then_inc(sems.h["st"], 16)
        sems.n["st"] += 16
        s.wait_ge(sems.h["gev"], sems.n["gev"])
        s.dma_start(out=cnt_o[:, :], in_=cnt[:, :]).then_inc(sems.h["st"], 16)
        sems.n["st"] += 16
        s.wait_ge(sems.h["st"], sems.n["st"])
        s.wait_ge(sems.h["st0"], sems.n["st0"])
        s.wait_ge(sems.h["st1"], sems.n["st1"])


def build_B():
    nc = bass.Bass("TRN2", target_bir_lowering=False)
    io = dram_B(nc)
    with contextlib.ExitStack() as stk:
        io["_sb"], io["_ps"] = alloc_pools(nc, stk)
        block = stk.enter_context(nc.Block())
        emit_B(nc, block, stk, io)
    return nc


def dram_C(nc):
    io = {}
    io["x1_in"] = nc.dram_tensor("x1_in", [TOK, D], F32, kind="ExternalInput")
    io["xn1T_i"] = nc.dram_tensor("xn1T_i", [128, NCH * TOK], BF16, kind="ExternalInput")
    io["cnt8"] = nc.dram_tensor("cnt8", [8, 8], F32, kind="ExternalInput")
    io["ew_in"] = nc.dram_tensor("ew", [8 * D, D], F32, kind="ExternalInput")
    io["be_in"] = nc.dram_tensor("be", [128, NCH * 8], F32, kind="ExternalInput")
    io["wq_in"] = nc.dram_tensor("a1Wq", [D, 256], F32, kind="ExternalInput")
    io["wqc_in"] = nc.dram_tensor("a1Wqc", [D, 256], F32, kind="ExternalInput")
    io["wk_in"] = nc.dram_tensor("a1Wk", [D, 256], F32, kind="ExternalInput")
    io["wkc_in"] = nc.dram_tensor("a1Wkc", [D, 256], F32, kind="ExternalInput")
    io["wv_in"] = nc.dram_tensor("a1Wv", [D, D], F32, kind="ExternalInput")
    io["wvc_in"] = nc.dram_tensor("a1Wvc", [D, D], F32, kind="ExternalInput")
    io["bq_in"] = nc.dram_tensor("a1bq", [128, 2], F32, kind="ExternalInput")
    io["bqc_in"] = nc.dram_tensor("a1bqc", [128, 2], F32, kind="ExternalInput")
    io["bk_in"] = nc.dram_tensor("a1bk", [128, 2], F32, kind="ExternalInput")
    io["bkc_in"] = nc.dram_tensor("a1bkc", [128, 2], F32, kind="ExternalInput")
    io["bv_in"] = nc.dram_tensor("a1bv", [128, 8], F32, kind="ExternalInput")
    io["bvc_in"] = nc.dram_tensor("a1bvc", [128, 8], F32, kind="ExternalInput")
    io["ident_in"] = nc.dram_tensor("ident", [128, 128], F32, kind="ExternalInput")
    io["ones_row_in"] = nc.dram_tensor("ones_row", [1, 128], F32, kind="ExternalInput")
    io["ones_col_in"] = nc.dram_tensor("ones_col", [128, 1], F32, kind="ExternalInput")
    io["x2_o"] = nc.dram_tensor("x2_o", [TOK, D], F32, kind="ExternalOutput")
    io["q1T_o"] = nc.dram_tensor("q1T_o", [128, 2 * TOK], BF16, kind="ExternalOutput")
    io["k1T_o"] = nc.dram_tensor("k1T_o", [128, 2 * TOK], BF16, kind="ExternalOutput")
    io["v1_o"] = nc.dram_tensor("v1_o", [TOK, D], BF16, kind="ExternalOutput")
    return io


def emit_C(nc, block, stk, io, fused=False):
    nc = bass.Bass("TRN2", target_bir_lowering=False)
    x1_in = io["x1_in"]
    xn1T_i = io["xn1T_i"]
    cnt8 = io["cnt8"]
    ew_in = io["ew_in"]
    be_in = io["be_in"]
    wq_in = io["wq_in"]
    wqc_in = io["wqc_in"]
    wk_in = io["wk_in"]
    wkc_in = io["wkc_in"]
    wv_in = io["wv_in"]
    wvc_in = io["wvc_in"]
    bq_in = io["bq_in"]
    bqc_in = io["bqc_in"]
    bk_in = io["bk_in"]
    bkc_in = io["bkc_in"]
    bv_in = io["bv_in"]
    bvc_in = io["bvc_in"]
    ident_in = io["ident_in"]
    ones_row_in = io["ones_row_in"]
    ones_col_in = io["ones_col_in"]
    x2_o = io["x2_o"]
    q1T_o = io["q1T_o"]
    k1T_o = io["k1T_o"]
    v1_o = io["v1_o"]
    T = lambda nm, shape, dt: io["_sb"].alloc(shape[-1], dt, rows=shape[0])
    PS = lambda nm, shape, dt: io["_ps"].alloc(nm, shape, dt)
    psc_viewat = lambda b, w=512: io["_ps"].view_at(b, [128, w], F32)
    arn = io["_sb"].sub(150)
    xn1T = arn.alloc(NCH * TOK, BF16)
    acc = arn.alloc(NCH * TOK, F32)
    ewA = arn.alloc(NCH * D, BF16)
    ewB = arn.alloc(NCH * D, BF16)
    xn2T = xn1T
    relu = T("relu", [128, 2 * 1024], BF16)
    wcol = T("wcol", [128, 8], F32)
    wrow = T("wrow", [1, 8], F32)
    c8 = T("c8", [8, 8], F32)
    besb = T("besb", [128, NCH * 8], F32)
    idnt = T("idnt", [128, 128], BF16)
    idnt_f = T("idnt_f", [128, 128], F32)
    onesr = T("onesr", [1, 128], F32)
    onesc = T("onesc", [128, 1], F32)
    epsc = T("epsc", [128, 1], F32)
    yt = T("yt", [128, D], F32)
    xt = T("xt", [128, 2 * D], F32)
    xn = T("xn", [128, 2 * D], BF16)
    x2t = T("x2t", [128, 2 * D], F32)
    stats = T("stats", [128, 16 * NT], F32)
    bq8 = T("bq8", [128, 2], F32)
    bqs = T("bqs", [128, 2], F32)
    bqcs = T("bqcs", [128, 2], F32)
    bks = T("bks", [128, 2], F32)
    bkcs = T("bkcs", [128, 2], F32)
    bvs = T("bvs", [128, 8], F32)
    bvcs = T("bvcs", [128, 8], F32)
    sgt = T("sgt", [128, 2 * 512], BF16)
    amt = T("amt", [128, 2 * 512], BF16)
    ps_a = PS("ps_a", [128, 512], F32)
    ps_b = PS("ps_b", [128, 512], F32)
    ps_c = PS("ps_c", [128, 512], F32)
    ps_d = PS("ps_d", [128, 512], F32)
    ps_tr = PS("ps_tr", [128, D], BF16)
    ps_trf = PS("ps_trf", [128, D], F32)
    ps_w = PS("ps_w", [128, 8], F32)

    sems = io.get("_sems") or Sems(
        nc, stk, ["ld", "ld2", "ew0", "ew1", "mm", "ev", "lnV",
                  "lnA", "vch", "ach", "tr", "pe", "xd0", "xd1",
                  "st", "st0", "st1", "wc", "qkm", "qke"], prefix="C_")

    @block.gpsimd
    def _(g):
        g.dma_start(out=idnt[:, :], in_=ident_in[:, :]).then_inc(sems.h["ld"], 16)
        g.dma_start(out=xn1T[:, :], in_=xn1T_i[:, :]).then_inc(sems.h["ld"], 16)
        g.dma_start(out=ewA[:, :].rearrange("p (c o) -> p c o", c=NCH),
                    in_=ew_in.ap().rearrange("(e c p) o -> p (e c) o", p=128, c=NCH)[
                        :, 0:NCH, :]).then_inc(sems.h["ew0"], 16)
    sems.n["ld"] += 32
    sems.n["ew0"] += 16

    @block.sync
    def _(s):
        s.dma_start(out=idnt_f[:, :], in_=ident_in[:, :]).then_inc(sems.h["ld2"], 16)
        if fused:
            s.dma_start(out=c8[0:1, 0:8], in_=cnt8[:, :]).then_inc(sems.h["ld2"], 16)
        else:
            s.dma_start(out=c8[:, :], in_=cnt8[:, :]).then_inc(sems.h["ld2"], 16)
        s.dma_start(out=besb[:, :], in_=be_in[:, :]).then_inc(sems.h["ld2"], 16)
        s.dma_start(out=onesr[:, :], in_=ones_row_in[:, :]).then_inc(sems.h["ld2"], 16)
        s.dma_start(out=onesc[:, :], in_=ones_col_in[:, :]).then_inc(sems.h["ld2"], 16)
        s.dma_start(out=bqs[:, :], in_=bq_in[:, :]).then_inc(sems.h["ld2"], 16)
        s.dma_start(out=bqcs[:, :], in_=bqc_in[:, :]).then_inc(sems.h["ld2"], 16)
        s.dma_start(out=bks[:, :], in_=bk_in[:, :]).then_inc(sems.h["ld2"], 16)
        s.dma_start(out=bkcs[:, :], in_=bkc_in[:, :]).then_inc(sems.h["ld2"], 16)
        s.dma_start(out=bvs[:, :], in_=bv_in[:, :]).then_inc(sems.h["ld2"], 16)
        s.dma_start(out=bvcs[:, :], in_=bvc_in[:, :]).then_inc(sems.h["ld2"], 16)
    sems.n["ld2"] += 176

    @block.vector
    def _(v):
        v.memset(epsc[:, :], EPS)
        sems.w(v, "ld2")
        v.tensor_scalar(bq8[:, :], bqs[:, :], 0.125, None, ALU.mult
                        ).then_inc(sems.h["ev"], 1)
    sems.n["ev"] += 1

    if fused:
        @block.vector
        def _(v):
            sems.w(v, "ld2")
            ins = v.tensor_scalar(wrow[:, :], c8[0:1, 0:8], 1.0 / 32768.0,
                                  None, ALU.mult)
            ins.then_inc(sems.h["wc"], 1)
            v.wait_ge(sems.h["wc"], sems.n["wc"] + 1)
        sems.n["wc"] += 1
    else:
        @block.tensor
        def _(t):
            sems.w(t, "ld2")
            t.matmul(ps_w[0:1, 0:8], onesc[0:8, 0:1], c8[:, :], start=True,
                     stop=True).then_inc(sems.h["mm"], 1)
        sems.n["mm"] += 1

        @block.vector
        def _(v):
            sems.w(v, "mm")
            ins = v.tensor_scalar(wrow[:, :], ps_w[0:1, 0:8], 1.0 / 32768.0,
                                  None, ALU.mult)
            ins.then_inc(sems.h["wc"], 1)
            v.wait_ge(sems.h["wc"], sems.n["wc"] + 1)
        sems.n["wc"] += 1

    @block.tensor
    def _(t):
        sems.w(t, "wc")
        t.matmul(ps_w[:, 0:8], onesr[0:1, :], wrow[0:1, :], start=True,
                 stop=True).then_inc(sems.h["mm"], 1)
    sems.n["mm"] += 1

    @block.vector
    def _(v):
        sems.w(v, "mm")
        v.tensor_copy(wcol[:, :], ps_w[:, 0:8]).then_inc(sems.h["wc"], 1)
    sems.n["wc"] += 1
    wc_ready = sems.n["wc"]

    mset = [[psc_viewat(0), psc_viewat(1), psc_viewat(2), psc_viewat(3)],
            [psc_viewat(4), psc_viewat(5), psc_viewat(6), psc_viewat(7)]]
    mpair = [[psc_viewat(0, 1024), psc_viewat(2, 1024)],
             [psc_viewat(4, 1024), psc_viewat(6, 1024)]]
    ew_war = {0: 0, 1: 0}
    acc3 = acc.rearrange("p (c t) -> p c t", c=NCH)
    moe_act = []
    for e in range(8):
        par = e % 2
        ewb = ewA if par == 0 else ewB
        esem = "ew" + str(par)
        if e + 1 < 8:
            npar = (e + 1) % 2
            nbuf = ewA if npar == 0 else ewB
            @block.gpsimd
            def _(g, e=e, nbuf=nbuf, npar=npar):
                if e >= 1:
                    g.wait_ge(sems.h["mm"], ew_war[npar])
                g.dma_start(
                    out=nbuf[:, :].rearrange("p (c o) -> p c o", c=NCH),
                    in_=ew_in.ap().rearrange("(e c p) o -> p (e c) o",
                                             p=128, c=NCH)[
                        :, (e + 1) * NCH:(e + 2) * NCH, :]
                    ).then_inc(sems.h["ew" + str(npar)], 16)
            sems.n["ew" + str(npar)] += 16

        for dt in range(8):
            itg = len(moe_act)
            pset = mset[itg % 2]
            ppair = mpair[itg % 2]
            @block.tensor
            def _(t, e=e, dt=dt, itg=itg, pset=pset, ewb=ewb, esem=esem):
                if dt == 0:
                    t.wait_ge(sems.h[esem], sems.n[esem])
                    t.wait_ge(sems.h["ld"], 32)
                if itg >= 2:
                    wnm, wv = moe_act[itg - 2]
                    t.wait_ge(sems.h[wnm], wv)
                if itg < 2:
                    t.wait_ge(sems.h["wc"], wc_ready)
                for c in range(NCH):
                    for qc in range(4):
                        ins = t.matmul(
                            pset[qc][:, :],
                            ewb[:, c * D + dt * 128: c * D + (dt + 1) * 128],
                            xn1T[:, c * TOK + qc * 512: c * TOK + (qc + 1) * 512],
                            start=(c == 0), stop=(c == NCH - 1))
                ins.then_inc(sems.h["mm"], 1)
            sems.n["mm"] += 1
            mm_done = sems.n["mm"]

            @block.scalar
            def _(sc, e=e, dt=dt, itg=itg, ppair=ppair, mm_done=mm_done):
                sc.wait_ge(sems.h["mm"], mm_done)
                if itg >= 1:
                    wnm, wv = moe_act[itg - 1]
                    sc.wait_ge(sems.h[wnm], wv)
                for half in range(2):
                    sc.activation(relu[:, half * 1024:(half + 1) * 1024],
                                  ppair[half][:, :], AF.Relu,
                                  bias=besb[:, dt * 8 + e: dt * 8 + e + 1]
                                  ).then_inc(sems.h["lnA"], 1)
                    sems.n["lnA"] += 1

            @block.vector
            def _(v, e=e, dt=dt, itg=itg):
                v.wait_ge(sems.h["lnA"], sems.n["lnA"])
                if e == 0 and itg == 0:
                    v.wait_ge(sems.h["wc"], wc_ready)
                for half in range(2):
                    dst = acc3[:, dt, half * 1024:(half + 1) * 1024]
                    rl = relu[:, half * 1024:(half + 1) * 1024]
                    if e == 0:
                        ins = v.tensor_scalar(dst, rl, wcol[:, 0:1], None,
                                              ALU.mult)
                    else:
                        ins = v.scalar_tensor_tensor(dst, rl,
                                                     wcol[:, e:e + 1], dst,
                                                     ALU.mult, ALU.add)
                    ins.then_inc(sems.h["ev"], 1)
                    sems.n["ev"] += 1
            moe_act.append(("ev", sems.n["ev"]))
        ew_war[par] = sems.n["mm"]
    moe_done = sems.n["ev"]

    def st_col(i, j):
        return stats[:, 16 * i + j: 16 * i + j + 1]

    x2_done = []
    yt_done = {}
    for tt in range(NT):
        par = tt % 2
        sp = str(par)

        @block.sync
        def _(s, tt=tt, par=par, sp=sp):
            if tt >= 2:
                s.wait_ge(sems.h["ev"], x2_done[tt - 2])
            s.dma_start(out=xt[:, par * D:(par + 1) * D],
                        in_=x1_in[tt * 128:(tt + 1) * 128, :]
                        ).then_inc(sems.h["xd" + sp], 16)
        sems.n["xd" + sp] += 16

        @block.tensor
        def _(t, tt=tt):
            if tt == 0:
                t.wait_ge(sems.h["ev"], moe_done)
                t.wait_ge(sems.h["ld2"], 176)
            if tt >= 1:
                t.wait_ge(sems.h["lnV"], yt_done[tt - 1])
            for c in range(NCH):
                ins = t.transpose(ps_trf[:, c * 128:(c + 1) * 128],
                                  acc3[:, c, tt * 128:(tt + 1) * 128],
                                  idnt_f[:, :])
            ins.then_inc(sems.h["tr"], 1)
        sems.n["tr"] += 1

        @block.vector
        def _(v, tt=tt):
            v.wait_ge(sems.h["tr"], sems.n["tr"])
            if tt >= 1:
                v.wait_ge(sems.h["lnA"], sems.n["lnA"])
                v.wait_ge(sems.h["ev"], x2_done[tt - 1])
            ins = v.tensor_copy(yt[:, :], ps_trf[:, :])
            sems.ch(ins, v, "vch")
            ins = v.reduce_sum(st_col(tt, 0), yt[:, :], axis=AX.X)
            sems.ch(ins, v, "vch")
            ins = v.tensor_scalar(st_col(tt, 1), st_col(tt, 0), -1.0 / D,
                                  None, ALU.mult)
            ins.then_inc(sems.h["lnV"], 1)
            v.wait_ge(sems.h["lnV"], sems.n["lnV"] + 1)
        sems.n["lnV"] += 1
        yt_done[tt] = sems.n["lnV"]

        @block.scalar
        def _(sc, tt=tt, par=par):
            sc.wait_ge(sems.h["lnV"], sems.n["lnV"])
            if tt >= 2:
                sc.wait_ge(sems.h["tr"], sems.n["tr"])
            sc.activation(xn[:, par * D:(par + 1) * D], yt[:, :], AF.Square,
                          accum_out=st_col(tt, 2)).then_inc(sems.h["lnA"], 1)
        sems.n["lnA"] += 1

        @block.vector
        def _(v, tt=tt):
            v.wait_ge(sems.h["lnA"], sems.n["lnA"])
            v.tensor_tensor(st_col(tt, 3), st_col(tt, 1), st_col(tt, 1), ALU.mult)
            ins = v.tensor_scalar(st_col(tt, 4), st_col(tt, 2), 1.0 / D,
                                  None, ALU.mult)
            sems.ch(ins, v, "vch")
            v.tensor_tensor(st_col(tt, 4), st_col(tt, 4), st_col(tt, 3),
                            ALU.subtract).then_inc(sems.h["lnV"], 1)
        sems.n["lnV"] += 1

        @block.scalar
        def _(sc, tt=tt):
            sc.wait_ge(sems.h["lnV"], sems.n["lnV"])
            ins = sc.activation(st_col(tt, 5), st_col(tt, 4), AF.Ln,
                                bias=epsc[:, 0:1])
            sems.ch(ins, sc, "ach")
            sc.activation(st_col(tt, 5), st_col(tt, 5), AF.Exp,
                          scale=-0.5).then_inc(sems.h["lnA"], 1)
        sems.n["lnA"] += 1

        @block.vector
        def _(v, tt=tt, par=par, sp=sp):
            v.wait_ge(sems.h["lnA"], sems.n["lnA"])
            v.wait_ge(sems.h["xd" + sp], 16 * (tt // 2 + 1))
            if tt >= 2:
                v.wait_ge(sems.h["st" + sp], 16 * (tt // 2))
            ins = v.tensor_scalar(x2t[:, par * D:(par + 1) * D], yt[:, :],
                                  st_col(tt, 1), st_col(tt, 5),
                                  ALU.add, ALU.mult)
            sems.ch(ins, v, "vch")
            ins = v.tensor_tensor(x2t[:, par * D:(par + 1) * D],
                                  x2t[:, par * D:(par + 1) * D],
                                  xt[:, par * D:(par + 1) * D], ALU.add)
            ins.then_inc(sems.h["ev"], 1)
            v.wait_ge(sems.h["ev"], sems.n["ev"] + 1)
            ins = v.reduce_sum(st_col(tt, 8), x2t[:, par * D:(par + 1) * D],
                               axis=AX.X)
            sems.ch(ins, v, "vch")
            ins = v.tensor_scalar(st_col(tt, 9), st_col(tt, 8), -1.0 / D,
                                  None, ALU.mult)
            ins.then_inc(sems.h["lnV"], 1)
            v.wait_ge(sems.h["lnV"], sems.n["lnV"] + 1)
        sems.n["ev"] += 1
        sems.n["lnV"] += 1
        x2_done.append(sems.n["ev"])

        @block.sync
        def _(s, tt=tt, par=par, sp=sp):
            s.wait_ge(sems.h["ev"], x2_done[tt])
            s.dma_start(out=x2_o[tt * 128:(tt + 1) * 128, :],
                        in_=x2t[:, par * D:(par + 1) * D]
                        ).then_inc(sems.h["st" + sp], 16)
        sems.n["st" + sp] += 16

        @block.scalar
        def _(sc, tt=tt, par=par):
            sc.wait_ge(sems.h["ev"], x2_done[tt])
            if tt >= 2:
                sc.wait_ge(sems.h["tr"], sems.n["tr"])
            sc.activation(xn[:, par * D:(par + 1) * D],
                          x2t[:, par * D:(par + 1) * D], AF.Square,
                          accum_out=st_col(tt, 10)).then_inc(sems.h["lnA"], 1)
        sems.n["lnA"] += 1

        @block.vector
        def _(v, tt=tt):
            v.wait_ge(sems.h["lnA"], sems.n["lnA"])
            v.tensor_tensor(st_col(tt, 11), st_col(tt, 9), st_col(tt, 9),
                            ALU.mult)
            ins = v.tensor_scalar(st_col(tt, 12), st_col(tt, 10), 1.0 / D,
                                  None, ALU.mult)
            sems.ch(ins, v, "vch")
            v.tensor_tensor(st_col(tt, 12), st_col(tt, 12), st_col(tt, 11),
                            ALU.subtract).then_inc(sems.h["lnV"], 1)
        sems.n["lnV"] += 1

        @block.scalar
        def _(sc, tt=tt):
            sc.wait_ge(sems.h["lnV"], sems.n["lnV"])
            ins = sc.activation(st_col(tt, 13), st_col(tt, 12), AF.Ln,
                                bias=epsc[:, 0:1])
            sems.ch(ins, sc, "ach")
            sc.activation(st_col(tt, 13), st_col(tt, 13), AF.Exp,
                          scale=-0.5).then_inc(sems.h["lnA"], 1)
        sems.n["lnA"] += 1

        @block.vector
        def _(v, tt=tt, par=par):
            v.wait_ge(sems.h["lnA"], sems.n["lnA"])
            if tt >= 2:
                v.wait_ge(sems.h["tr"], sems.n["tr"])
            v.tensor_scalar(xn[:, par * D:(par + 1) * D],
                            x2t[:, par * D:(par + 1) * D],
                            st_col(tt, 9), st_col(tt, 13),
                            ALU.add, ALU.mult).then_inc(sems.h["ev"], 1)
        sems.n["ev"] += 1

        @block.tensor
        def _(t, tt=tt, par=par):
            t.wait_ge(sems.h["ev"], sems.n["ev"])
            if tt >= 1:
                t.wait_ge(sems.h["pe"], sems.n["pe"])
            for c in range(NCH):
                ins = t.transpose(ps_tr[:, c * 128:(c + 1) * 128],
                                  xn[:, par * D + c * 128:
                                     par * D + (c + 1) * 128],
                                  idnt[:, :])
            ins.then_inc(sems.h["tr"], 1)
        sems.n["tr"] += 1

        @block.vector
        def _(v, tt=tt):
            v.wait_ge(sems.h["tr"], sems.n["tr"])
            dst = xn2T.rearrange("p (c t) -> p c t", c=NCH)[
                :, :, tt * 128:(tt + 1) * 128]
            srcp = ps_tr.rearrange("p (c t) -> p c t", c=NCH)
            v.tensor_copy(dst, srcp).then_inc(sems.h["pe"], 1)
        sems.n["pe"] += 1
    xn2_ready = sems.n["pe"]

    arn.push()
    wbA = ewA
    wbB = ewB
    vT = acc.bitcast(BF16)[:, 0:NCH * TOK]

    @block.gpsimd
    def _(g):
        g.wait_ge(sems.h["ev"], moe_done)
        g.wait_ge(sems.h["mm"], sems.n["mm"])
        g.dma_start(out=wbA[:, 0:NCH * 256].rearrange("p (c o) -> p c o", c=NCH),
                    in_=wq_in.ap().rearrange("(c p) o -> p c o", p=128)
                    ).then_inc(sems.h["ew0"], 16)
        g.dma_start(out=wbA[:, NCH * 256:NCH * 512].rearrange("p (c o) -> p c o", c=NCH),
                    in_=wqc_in.ap().rearrange("(c p) o -> p c o", p=128)
                    ).then_inc(sems.h["ew0"], 16)
        g.dma_start(out=wbA[:, NCH * 512:NCH * 768].rearrange("p (c o) -> p c o", c=NCH),
                    in_=wk_in.ap().rearrange("(c p) o -> p c o", p=128)
                    ).then_inc(sems.h["ew0"], 16)
        g.dma_start(out=wbA[:, NCH * 768:NCH * 1024].rearrange("p (c o) -> p c o", c=NCH),
                    in_=wkc_in.ap().rearrange("(c p) o -> p c o", p=128)
                    ).then_inc(sems.h["ew0"], 16)
        g.dma_start(out=wbB[:, :].rearrange("p (c o) -> p c o", c=NCH),
                    in_=wv_in.ap().rearrange("(c p) o -> p c o", p=128)
                    ).then_inc(sems.h["ew1"], 16)
    sems.n["ew0"] += 64
    sems.n["ew1"] += 16

    qk_sb = T("qk_sb", [128, 4 * TOK], BF16)

    css_act_after = []

    def css_proj(woff, gwoff, bmain, bgate, scale_main, out_ap, ndt,
                 wbuf, wsem, wneed, dt0=0):
        for dt in range(ndt):
            for qc in range(4):
                it_g = len(css_act_after)
                pmain = ps_a if (dt * 4 + qc) % 2 == 0 else ps_c
                pgate = ps_b if (dt * 4 + qc) % 2 == 0 else ps_d
                @block.tensor
                def _(t, dt=dt, qc=qc, pmain=pmain, pgate=pgate, wbuf=wbuf,
                      woff=woff, gwoff=gwoff, wsem=wsem, wneed=wneed,
                      it_g=it_g):
                    t.wait_ge(sems.h[wsem], wneed)
                    t.wait_ge(sems.h["pe"], xn2_ready)
                    if it_g >= 2:
                        t.wait_ge(sems.h["lnA"], css_act_after[it_g - 2])
                    for c in range(NCH):
                        t.matmul(pmain[:, :],
                                 wbuf[:, woff + c * (ndt * 128) + dt * 128:
                                      woff + c * (ndt * 128) + (dt + 1) * 128],
                                 xn2T[:, c * TOK + qc * 512: c * TOK + (qc + 1) * 512],
                                 start=(c == 0), stop=(c == NCH - 1))
                    for c in range(NCH):
                        ins = t.matmul(pgate[:, :],
                                 wbuf[:, gwoff + c * (ndt * 128) + dt * 128:
                                      gwoff + c * (ndt * 128) + (dt + 1) * 128],
                                 xn2T[:, c * TOK + qc * 512: c * TOK + (qc + 1) * 512],
                                 start=(c == 0), stop=(c == NCH - 1))
                    ins.then_inc(sems.h["qkm"], 1)
                sems.n["qkm"] += 1

                @block.scalar
                def _(sc, dt=dt, qc=qc, pmain=pmain, pgate=pgate,
                      bmain=bmain, bgate=bgate, scale_main=scale_main,
                      dt0=dt0):
                    sc.wait_ge(sems.h["qkm"], sems.n["qkm"])
                    if sems.n["qke"] >= 1:
                        sc.wait_ge(sems.h["qke"], sems.n["qke"] - 1)
                    i2 = (dt * 4 + qc) % 2
                    ins = sc.activation(sgt[:, i2 * 512:(i2 + 1) * 512],
                                        pgate[:, :], AF.Sigmoid,
                                        bias=bgate[:, dt0 + dt:dt0 + dt + 1])
                    sc.activation(amt[:, i2 * 512:(i2 + 1) * 512],
                                  pmain[:, :], AF.Identity,
                                  bias=bmain[:, dt0 + dt:dt0 + dt + 1],
                                  scale=scale_main).then_inc(sems.h["lnA"], 1)
                sems.n["lnA"] += 1
                css_act_after.append(sems.n["lnA"])

                @block.vector
                def _(v, dt=dt, qc=qc, out_ap=out_ap):
                    v.wait_ge(sems.h["lnA"], sems.n["lnA"])
                    i2 = (dt * 4 + qc) % 2
                    v.tensor_tensor(out_ap[:, dt * TOK + qc * 512:
                                           dt * TOK + (qc + 1) * 512],
                                    amt[:, i2 * 512:(i2 + 1) * 512],
                                    sgt[:, i2 * 512:(i2 + 1) * 512],
                                    ALU.mult).then_inc(sems.h["qke"], 1)
                sems.n["qke"] += 1

    css_proj(0, NCH * 256, bq8, bqcs, 0.125, qk_sb[:, 0:2 * TOK], 2,
             wbA, "ew0", sems.n["ew0"])
    css_proj(NCH * 512, NCH * 768, bks, bkcs, 1.0, qk_sb[:, 2 * TOK:4 * TOK],
             2, wbA, "ew0", sems.n["ew0"])
    k1_done = sems.n["qke"]

    @block.sync
    def _(s):
        s.wait_ge(sems.h["qke"], k1_done)
        s.dma_start(out=k1T_o[:, :], in_=qk_sb[:, 2 * TOK:4 * TOK]
                    ).then_inc(sems.h["st"], 16)
    sems.n["st"] += 16
    k1_store = sems.n["st"]

    if fused:
        @block.gpsimd
        def _(g):
            g.wait_ge(sems.h["st"], k1_store)
            g.collective_compute(
                "AllGather", mybir.AluOpType.bypass,
                replica_groups=PAIRS_RG,
                ins=[io["agksrc"].ap().opt()],
                outs=[io["agkdst"].ap().opt()],
            ).then_inc(sems.h["wc"], 1)
        sems.n["wc"] += 1

    @block.gpsimd
    def _(g):
        g.wait_ge(sems.h["qkm"], sems.n["qkm"])
        g.dma_start(out=wbA[:, :].rearrange("p (c o) -> p c o", c=NCH),
                    in_=wvc_in.ap().rearrange("(c p) o -> p c o", p=128)
                    ).then_inc(sems.h["ew0"], 16)
    sems.n["ew0"] += 16

    for dt in range(8):
        for qc in range(4):
            it_g = len(css_act_after)
            pmain = ps_a if (dt * 4 + qc) % 2 == 0 else ps_c
            pgate = ps_b if (dt * 4 + qc) % 2 == 0 else ps_d
            @block.tensor
            def _(t, dt=dt, qc=qc, pmain=pmain, pgate=pgate, it_g=it_g):
                t.wait_ge(sems.h["ew0"], sems.n["ew0"])
                t.wait_ge(sems.h["ew1"], sems.n["ew1"])
                if it_g >= 2:
                    t.wait_ge(sems.h["lnA"], css_act_after[it_g - 2])
                for c in range(NCH):
                    t.matmul(pmain[:, :],
                             wbB[:, c * D + dt * 128:c * D + (dt + 1) * 128],
                             xn2T[:, c * TOK + qc * 512: c * TOK + (qc + 1) * 512],
                             start=(c == 0), stop=(c == NCH - 1))
                for c in range(NCH):
                    ins = t.matmul(pgate[:, :],
                             wbA[:, c * D + dt * 128:c * D + (dt + 1) * 128],
                             xn2T[:, c * TOK + qc * 512: c * TOK + (qc + 1) * 512],
                             start=(c == 0), stop=(c == NCH - 1))
                ins.then_inc(sems.h["qkm"], 1)
            sems.n["qkm"] += 1

            @block.scalar
            def _(sc, dt=dt, qc=qc, pmain=pmain, pgate=pgate):
                sc.wait_ge(sems.h["qkm"], sems.n["qkm"])
                sc.wait_ge(sems.h["qke"], sems.n["qke"] - 1)
                i2 = (dt * 4 + qc) % 2
                ins = sc.activation(sgt[:, i2 * 512:(i2 + 1) * 512],
                                    pgate[:, :], AF.Sigmoid,
                                    bias=bvcs[:, dt:dt + 1])
                sc.activation(amt[:, i2 * 512:(i2 + 1) * 512],
                              pmain[:, :], AF.Identity,
                              bias=bvs[:, dt:dt + 1]).then_inc(sems.h["lnA"], 1)
            sems.n["lnA"] += 1
            css_act_after.append(sems.n["lnA"])

            @block.vector
            def _(v, dt=dt, qc=qc):
                v.wait_ge(sems.h["lnA"], sems.n["lnA"])
                i2 = (dt * 4 + qc) % 2
                v.tensor_tensor(vT[:, dt * TOK + qc * 512:
                                   dt * TOK + (qc + 1) * 512],
                                amt[:, i2 * 512:(i2 + 1) * 512],
                                sgt[:, i2 * 512:(i2 + 1) * 512],
                                ALU.mult).then_inc(sems.h["qke"], 1)
            sems.n["qke"] += 1

    vt_ready = sems.n["qke"]

    vtile = T("vtile", [128, 2 * D], BF16)
    vT3 = vT.rearrange("p (c t) -> p c t", c=NCH)
    for tt in range(NT):
        par = tt % 2
        @block.tensor
        def _(t, tt=tt):
            t.wait_ge(sems.h["qke"], vt_ready)
            if tt >= 1:
                t.wait_ge(sems.h["pe"], sems.n["pe"])
            for c in range(NCH):
                ins = t.transpose(ps_tr[:, c * 128:(c + 1) * 128],
                                  vT3[:, c, tt * 128:(tt + 1) * 128],
                                  idnt[:, :])
            ins.then_inc(sems.h["tr"], 1)
        sems.n["tr"] += 1

        @block.vector
        def _(v, tt=tt, par=par):
            v.wait_ge(sems.h["tr"], sems.n["tr"])
            if tt >= 2:
                v.wait_ge(sems.h["st" + str(par)], 16 * (NT // 2 + tt // 2))
            v.tensor_copy(vtile[:, par * D:(par + 1) * D], ps_tr[:, :]
                          ).then_inc(sems.h["pe"], 1)
        sems.n["pe"] += 1

        @block.sync
        def _(s, tt=tt, par=par):
            s.wait_ge(sems.h["pe"], sems.n["pe"])
            s.dma_start(out=v1_o[tt * 128:(tt + 1) * 128, :],
                        in_=vtile[:, par * D:(par + 1) * D]
                        ).then_inc(sems.h["st" + str(par)], 16)
        sems.n["st" + str(par)] += 16

    @block.sync
    def _(s):
        s.wait_ge(sems.h["qke"], sems.n["qke"])
        s.dma_start(out=q1T_o[:, :], in_=qk_sb[:, 0:2 * TOK]).then_inc(sems.h["st"], 16)
        sems.n["st"] += 16
        s.wait_ge(sems.h["st"], sems.n["st"])
        s.wait_ge(sems.h["st0"], sems.n["st0"])
        s.wait_ge(sems.h["st1"], sems.n["st1"])

    if fused:
        @block.gpsimd
        def _(g):
            g.wait_ge(sems.h["st0"], sems.n["st0"])
            g.wait_ge(sems.h["st1"], sems.n["st1"])
            g.collective_compute(
                "AllGather", mybir.AluOpType.bypass,
                replica_groups=PAIRS_RG,
                ins=[io["agvsrc"].ap().opt()],
                outs=[io["agvdst"].ap().opt()],
            ).then_inc(sems.h["wc"], 1)
            g.wait_ge(sems.h["wc"], sems.n["wc"] + 1)
        sems.n["wc"] += 1


def build_C():
    nc = bass.Bass("TRN2", target_bir_lowering=False)
    io = dram_C(nc)
    with contextlib.ExitStack() as stk:
        io["_sb"], io["_ps"] = alloc_pools(nc, stk)
        block = stk.enter_context(nc.Block())
        emit_C(nc, block, stk, io)
    return nc


def dram_D(nc):
    io = {}
    io["x2_in"] = nc.dram_tensor("x2_in", [TOK, D], F32, kind="ExternalInput")
    io["qb_in"] = nc.dram_tensor("qb_in", [128, 4 * TOK], BF16, kind="ExternalInput")
    io["kb_in"] = nc.dram_tensor("kb_in", [128, 4 * 4224], BF16, kind="ExternalInput")
    io["v1f"] = nc.dram_tensor("v1f", [SKEY, D], BF16, kind="ExternalInput")
    io["bk_in"] = nc.dram_tensor("a1bk", [128, 2], F32, kind="ExternalInput")
    io["bkc_in"] = nc.dram_tensor("a1bkc", [128, 2], F32, kind="ExternalInput")
    io["bv_in"] = nc.dram_tensor("a1bv", [128, 8], F32, kind="ExternalInput")
    io["bvc_in"] = nc.dram_tensor("a1bvc", [128, 8], F32, kind="ExternalInput")
    io["w1_in"] = nc.dram_tensor("w1T", [D, DH], F32, kind="ExternalInput")
    io["w2_in"] = nc.dram_tensor("w2T", [D, DH], F32, kind="ExternalInput")
    io["h1_in"] = nc.dram_tensor("h1T", [DH, DH], F32, kind="ExternalInput")
    io["h2_in"] = nc.dram_tensor("h2T", [DH, DH], F32, kind="ExternalInput")
    io["w3_in"] = nc.dram_tensor("w3T", [DH, D], F32, kind="ExternalInput")
    io["s1_in"] = nc.dram_tensor("s1", [128, NHH], F32, kind="ExternalInput")
    io["s2_in"] = nc.dram_tensor("s2", [128, NHH], F32, kind="ExternalInput")
    io["sh1_in"] = nc.dram_tensor("sh1", [128, NHH], F32, kind="ExternalInput")
    io["sh2_in"] = nc.dram_tensor("sh2", [128, NHH], F32, kind="ExternalInput")
    io["s3_in"] = nc.dram_tensor("s3", [128, 8], F32, kind="ExternalInput")
    io["ident_in"] = nc.dram_tensor("ident", [128, 128], F32, kind="ExternalInput")
    io["ones_row_in"] = nc.dram_tensor("ones_row", [1, 128], F32, kind="ExternalInput")
    io["ones_col_in"] = nc.dram_tensor("ones_col", [128, 1], F32, kind="ExternalInput")
    io["x4_o"] = nc.dram_tensor("x4_o", [TOK, D], F32, kind="ExternalOutput")
    io["x3_s"] = nc.dram_tensor("x3_s", [TOK, D], F32)
    io["y_s"] = nc.dram_tensor("y_s", [128, 2 * NCH * 1024], BF16)
    io["pad_s"] = nc.dram_tensor("pad_s", [10, 128], BF16)
    return io


def emit_D(nc, block, stk, io, fused=False):
    nc = bass.Bass("TRN2", target_bir_lowering=False)
    KT1 = 33
    KT = 32
    x2_in = io["x2_in"]
    qb_in = io["qb_in"]
    kb_in = io["kb_in"]
    v1f = io["v1f"]
    bk_in = io["bk_in"]
    bkc_in = io["bkc_in"]
    bv_in = io["bv_in"]
    bvc_in = io["bvc_in"]
    w1_in = io["w1_in"]
    w2_in = io["w2_in"]
    h1_in = io["h1_in"]
    h2_in = io["h2_in"]
    w3_in = io["w3_in"]
    s1_in = io["s1_in"]
    s2_in = io["s2_in"]
    sh1_in = io["sh1_in"]
    sh2_in = io["sh2_in"]
    s3_in = io["s3_in"]
    ident_in = io["ident_in"]
    ones_row_in = io["ones_row_in"]
    ones_col_in = io["ones_col_in"]
    x4_o = io["x4_o"]
    x3_s = io["x3_s"]
    y_s = io["y_s"]
    pad_s = io["pad_s"]
    T = lambda nm, shape, dt: io["_sb"].alloc(shape[-1], dt, rows=shape[0])
    PS = lambda nm, shape, dt: io["_ps"].alloc(nm, shape, dt)
    psc_viewat = lambda b: io["_ps"].view_at(b, [128, 512], F32)
    arn = io["_sb"].sub(168)
    KB = 1024
    xn3T = arn.alloc_at(0, NCH * TOK, BF16)
    qbuf = arn.alloc_at(32 * KB, 4 * TOK, BF16)
    kbuf = arn.alloc_at(48 * KB, 4 * 4224, BF16)
    vpA = arn.alloc_at(81 * KB, KT1 * 130, BF16)
    vpB = arn.alloc_at(90 * KB, KT1 * 130, BF16)
    vraw = arn.alloc_at(99 * KB, KT * 128, BF16)
    U = arn.alloc_at(107 * KB, NT * H0 * 66, BF16)
    sq = arn.alloc_at(140 * KB, 2176, BF16)
    HTOK = TOK // 2
    hbuf = arn.alloc_at(32 * KB, NHH * HTOK, BF16)
    ubuf = arn.alloc_at(96 * KB, NHH * HTOK, BF16)
    wbA = arn.alloc_at(160 * KB, 4096, BF16)
    wbB = wbA
    pt = T("pt", [128, 4 * 512], BF16)
    tb = T("tb", [128, 512], BF16)
    nrm = T("nrm", [1, 16], F32)
    mrow = T("mrow", [1, 16], F32)
    negM2 = T("negM2", [128, 2], F32)
    negMp2 = T("negMp2", [128, 2], F32)
    idnt = T("idnt", [128, 128], BF16)
    onesr = T("onesr", [1, 128], F32)
    onesc = T("onesc", [128, 1], BF16)
    epsc = T("epsc", [128, 1], F32)
    yt = T("yt", [128, D], F32)
    rcol = T("rcol", [128, 16], F32)
    xt = T("xt", [128, 2 * D], F32)
    xn = T("xn", [128, 2 * D], BF16)
    x1t = T("x1t", [128, 2 * D], F32)
    stats = T("stats", [128, 16 * NT], F32)
    bks = T("bks", [128, 2], F32)
    bkcs = T("bkcs", [128, 2], F32)
    bvs = T("bvs", [128, 8], F32)
    bvcs = T("bvcs", [128, 8], F32)
    sgk = T("sgk", [128, 10], F32)
    kpadc = T("kpadc", [128, 2], BF16)
    vpadc = T("vpadc", [128, 8], BF16)
    padk = T("padk", [2, 128], BF16)
    padv = T("padv", [8, 128], BF16)
    vpadrow = T("vpadrow", [1, D], BF16)
    scl = T("scl", [128, 4 * NHH + 8], F32)
    tsb = T("tsb", [128, 4 * 512], BF16)
    ps_S0 = PS("ps_S0", [128, 512], F32)
    ps_S1 = PS("ps_S1", [128, 512], F32)
    ps_o = PS("ps_o", [65, 512], F32)
    ps_y = PS("ps_y", [128, 4 * 66], BF16)
    ps_n = PS("ps_n", [128, 512], F32)
    ps_b = PS("ps_b", [128, 8], F32)
    ps_tr = PS("ps_tr", [128, D], BF16)
    ps_g = PS("ps_g", [128, 512], F32)
    sH = [[io["_ps"].view_at(0, [128, 512], F32),
           io["_ps"].view_at(1, [128, 512], F32)],
          [io["_ps"].view_at(6, [128, 512], F32),
           io["_ps"].view_at(7, [128, 512], F32)]]
    ps_o2 = io["_ps"].view_at(5, [65, 512], F32)
    ps_oH = [ps_o, ps_o2]

    sems = io.get("_sems") or Sems(nc, stk, [
        "ld", "ld2", "kv0", "kv1", "mm", "exp", "evO", "trY", "uev", "mh",
        "xd0", "xd1", "lnV", "lnA", "vch", "ach", "tr", "pe",
        "st", "st0", "st1", "pad", "wld0", "wld1", "ev", "ffe", "ffb"],
        prefix="D_")

    @block.gpsimd
    def _(g):
        g.dma_start(out=idnt[:, :], in_=ident_in[:, :]).then_inc(sems.h["ld"], 16)
        g.dma_start(out=onesc[:, :], in_=ones_col_in[:, :]).then_inc(sems.h["ld"], 16)
    sems.n["ld"] += 32

    @block.sync
    def _(s):
        s.dma_start(out=onesr[:, :], in_=ones_row_in[:, :]).then_inc(sems.h["ld2"], 16)
        s.dma_start(out=bks[:, :], in_=bk_in[:, :]).then_inc(sems.h["ld2"], 16)
        s.dma_start(out=bkcs[:, :], in_=bkc_in[:, :]).then_inc(sems.h["ld2"], 16)
        s.dma_start(out=bvs[:, :], in_=bv_in[:, :]).then_inc(sems.h["ld2"], 16)
        s.dma_start(out=bvcs[:, :], in_=bvc_in[:, :]).then_inc(sems.h["ld2"], 16)
        s.dma_start(out=scl[:, 0:NHH], in_=s1_in[:, :]).then_inc(sems.h["ld2"], 16)
        s.dma_start(out=scl[:, NHH:2 * NHH], in_=s2_in[:, :]).then_inc(sems.h["ld2"], 16)
        s.dma_start(out=scl[:, 2 * NHH:3 * NHH], in_=sh1_in[:, :]).then_inc(sems.h["ld2"], 16)
        s.dma_start(out=scl[:, 3 * NHH:4 * NHH], in_=sh2_in[:, :]).then_inc(sems.h["ld2"], 16)
        s.dma_start(out=scl[:, 4 * NHH:4 * NHH + 8], in_=s3_in[:, :]).then_inc(sems.h["ld2"], 16)
        if not fused:
            s.dma_start(out=qbuf, in_=qb_in[:, :]).then_inc(sems.h["ld2"], 16)
            s.dma_start(out=kbuf, in_=kb_in[:, :]).then_inc(sems.h["ld2"], 16)
    sems.n["ld2"] += 192 if not fused else 160

    if fused:
        @block.vector
        def _(v):
            ins = v.memset(qbuf, 0.0)
            ins = v.memset(kbuf, 0.0)
            ins.then_inc(sems.h["pad"], 1)
        sems.n["pad"] += 1
        mz = sems.n["pad"]

        @block.sync
        def _(s):
            s.wait_ge(sems.h["pad"], mz)
            kb4 = kbuf.rearrange("p (T c) -> p T c", T=4)
            qb4 = qbuf.rearrange("p (T c) -> p T c", T=4)
            for h in range(16):
                Ti, j = h // 4, h % 4
                cc, p0 = (h * 16) // 128, (h * 16) % 128
                s.dma_start(out=qb4[32 * j:32 * j + 16, Ti, :],
                            in_=qb_in[p0:p0 + 16, cc * TOK:(cc + 1) * TOK]
                            ).then_inc(sems.h["ld2"], 16)
                sems.n["ld2"] += 16
                for m_ in range(2):
                    s.dma_start(
                        out=kb4[32 * j:32 * j + 16, Ti,
                                m_ * TOK:(m_ + 1) * TOK],
                        in_=kb_in[m_ * 128 + p0:m_ * 128 + p0 + 16,
                                  cc * TOK:(cc + 1) * TOK]
                        ).then_inc(sems.h["ld2"], 16)
                    sems.n["ld2"] += 16
    ld2_all = sems.n["ld2"]

    @block.vector
    def _(v):
        v.memset(epsc[:, :], EPS)
        for buf in (vpA, vpB):
            b4 = buf.rearrange("p (k h c) -> p k h c", k=KT1, h=2)
            v.memset(b4[:, :, :, 64:65], 1.0)
            v.memset(b4[:, 32, :, 0:64], 0.0)
        v.memset(negMp2[:, :], -1e30)

    @block.scalar
    def _(sc):
        sc.wait_ge(sems.h["ld2"], ld2_all)
        ins = sc.activation(sgk[:, 0:2], bkcs[:, :], AF.Sigmoid)
        sc.activation(sgk[:, 2:10], bvcs[:, :], AF.Sigmoid
                      ).then_inc(sems.h["lnA"], 1)
    sems.n["lnA"] += 1

    @block.vector
    def _(v):
        v.wait_ge(sems.h["lnA"], sems.n["lnA"])
        v.tensor_tensor(kpadc[:, :], bks[:, :], sgk[:, 0:2], ALU.mult)
        v.tensor_tensor(vpadc[:, :], bvs[:, :], sgk[:, 2:10], ALU.mult
                        ).then_inc(sems.h["pad"], 1)
    sems.n["pad"] += 1

    @block.tensor
    def _(t):
        sems.w(t, "pad")
        sems.w(t, "ld")
        t.transpose(ps_tr[0:2, 0:128], kpadc[:, :], idnt[:, :])
        t.transpose(ps_tr[0:8, 128:256], vpadc[:, :],
                    idnt[:, :]).then_inc(sems.h["mm"], 1)
    sems.n["mm"] += 1

    @block.vector
    def _(v):
        sems.w(v, "mm")
        v.tensor_copy(padk[:, :], ps_tr[0:2, 0:128])
        v.tensor_copy(padv[:, :], ps_tr[0:8, 128:256]
                      ).then_inc(sems.h["pad"], 1)
    sems.n["pad"] += 1

    @block.sync
    def _(s):
        sems.w(s, "pad")
        s.dma_start(out=pad_s[0:2, :], in_=padk[:, :]).then_inc(sems.h["pad"], 16)
        s.dma_start(out=pad_s[2:10, :], in_=padv[:, :]).then_inc(sems.h["pad"], 16)
    sems.n["pad"] += 32

    @block.sync
    def _(s):
        sems.w(s, "pad")
        kb4 = kbuf.rearrange("p (T c) -> p T c", T=4)
        for Ti in range(4):
            for j in range(4):
                hh = 4 * Ti + j
                srcflat = pad_s.ap().rearrange("a b -> (a b)")[
                    hh * 16: hh * 16 + 16]
                s.dma_start(out=kb4[32 * j:32 * j + 16, Ti, 4096:4097],
                            in_=srcflat.unsqueeze(1)
                            ).then_inc(sems.h["pad"], 16)
                sems.n["pad"] += 16
        s.dma_start(out=vpadrow[:, :],
                    in_=pad_s.ap().rearrange("a b -> (a b)")[
                        256:256 + D].unsqueeze(0)).then_inc(sems.h["pad"], 16)
    sems.n["pad"] += 16
    pad_ready = sems.n["pad"]

    def kv_load(grp):
        par = grp % 2
        sem = "kv" + str(par)
        @block.sync
        def _(s, grp=grp, sem=sem, par=par):
            if grp >= 2:
                s.wait_ge(sems.h["mm"], kv_war[par])
            if grp >= 1:
                o = "kv" + str(1 - par)
                s.wait_ge(sems.h[o], sems.n[o])
            s.dma_start(
                out=vraw.rearrange("p (k c) -> p k c", k=KT),
                in_=v1f.ap().rearrange("(k p) d -> p k d", p=128)[
                    :, :, grp * 128:(grp + 1) * 128],
                ).then_inc(sems.h[sem], 16)
        sems.n[sem] += 16

    def kv_restripe(grp):
        par = grp % 2
        vbuf = vpA if par == 0 else vpB
        sem = "kv" + str(par)
        @block.vector
        def _(v, grp=grp, vbuf=vbuf, sem=sem, par=par):
            v.wait_ge(sems.h[sem], sems.n[sem])
            v.wait_ge(sems.h["pad"], pad_ready)
            if grp >= 2:
                v.wait_ge(sems.h["mm"], kv_war[par])
            b4 = vbuf.rearrange("p (k h c) -> p k h c", k=KT1, h=2)
            ins = v.tensor_copy(
                b4[:, 0:KT, :, 0:64],
                vraw.rearrange("p (k h c) -> p k h c", k=KT, h=2))
            ins = v.tensor_copy(
                b4[0:1, 32, :, 0:64],
                vpadrow[0:1, grp * 128:(grp + 1) * 128].rearrange(
                    "p (h c) -> p h c", h=2))
            ins.then_inc(sems.h[sem], 1)
        sems.n[sem] += 1

    kv_war = [0, 0]
    kv_load(0)
    kv_restripe(0)
    kv_load(1)

    for grp in range(8):
        par = grp % 2
        vbuf = vpA if par == 0 else vpB
        if grp >= 1:
            kv_restripe(grp)
            if grp + 1 < 8:
                kv_load(grp + 1)
        negm_ready = {}
        for h_loc in range(2):
            h = grp * 2 + h_loc
            Ti = h // 4
            jj = h % 4
            rb = 32 * jj
            kt_ap = kbuf.rearrange("p (T c) -> p T c", T=4)
            q_ap = qbuf.rearrange("p (T c) -> p T c", T=4)

            segs = [
                (kt_ap[:, Ti, 0:2048], 4, 0, 2048),
                (kt_ap[:, Ti, 2048:4224], 5, 4, 2176),
                (q_ap[:, Ti, :], 4, 9, 2048),
            ]
            for (srcap0, ncks, ncol, wid) in segs:
                @block.scalar
                def _(sc, srcap0=srcap0, wid=wid):
                    sc.wait_ge(sems.h["ld2"], ld2_all)
                    sc.wait_ge(sems.h["pad"], pad_ready)
                    sc.wait_ge(sems.h["mh"], sems.n["mh"])
                    sc.activation(sq[:, 0:wid], srcap0, AF.Square
                                  ).then_inc(sems.h["mh"], 1)
                sems.n["mh"] += 1

                for j in range(ncks):
                    @block.tensor
                    def _(t, j=j, rb=rb, ncks=ncks, ncol=ncol):
                        t.wait_ge(sems.h["mh"], sems.n["mh"])
                        wdt = 512 if not (ncol == 4 and j == 4) else 128
                        t.matmul(ps_n[0:1, 0:wdt], onesc[rb:rb + 16, 0:1],
                                 sq[rb:rb + 16, j * 512:j * 512 + wdt],
                                 start=True, stop=True, tile_position=(rb, 0)
                                 ).then_inc(sems.h["mm"], 1)
                    sems.n["mm"] += 1

                    @block.vector
                    def _(v, j=j, ncol=ncol):
                        v.wait_ge(sems.h["mm"], sems.n["mm"])
                        wdt = 512 if not (ncol == 4 and j == 4) else 128
                        ins = v.tensor_reduce(nrm[0:1, ncol + j:ncol + j + 1],
                                              ps_n[0:1, 0:wdt],
                                              axis=AX.X, op=ALU.max)
                        ins.then_inc(sems.h["mh"], 1)
                    sems.n["mh"] += 1

            @block.vector
            def _(v, h=h):
                v.wait_ge(sems.h["mh"], sems.n["mh"])
                ins = v.tensor_reduce(mrow[0:1, 0:1], nrm[0:1, 0:9],
                                      axis=AX.X, op=ALU.max)
                sems.ch(ins, v, "vch")
                ins = v.tensor_reduce(mrow[0:1, 1:2], nrm[0:1, 9:13],
                                      axis=AX.X, op=ALU.max)
                sems.ch(ins, v, "vch")
                ins = v.tensor_tensor(mrow[0:1, 2:3], mrow[0:1, 0:1],
                                      mrow[0:1, 1:2], ALU.mult)
                ins.then_inc(sems.h["lnV"], 1)
            sems.n["lnV"] += 1

            @block.scalar
            def _(sc, h=h):
                sc.wait_ge(sems.h["lnV"], sems.n["lnV"])
                ins = sc.activation(mrow[0:1, 3:4], mrow[0:1, 2:3], AF.Ln)
                sems.ch(ins, sc, "ach")
                sc.activation(mrow[0:1, 4:5], mrow[0:1, 3:4], AF.Exp,
                              scale=0.5).then_inc(sems.h["lnA"], 1)
            sems.n["lnA"] += 1

            @block.vector
            def _(v, h=h):
                v.wait_ge(sems.h["lnA"], sems.n["lnA"])
                ins = v.tensor_scalar(mrow[0:1, 5:6], mrow[0:1, 4:5], 1.0,
                                      LN16 + 1.0, ALU.mult, ALU.add)
                sems.ch(ins, v, "vch")
                ins = v.tensor_scalar(mrow[0:1, 7:8], mrow[0:1, 5:6], -1.0,
                                      None, ALU.mult)
                ins.then_inc(sems.h["lnV"], 1)
            sems.n["lnV"] += 1

            @block.tensor
            def _(t, h=h):
                t.wait_ge(sems.h["lnV"], sems.n["lnV"])
                t.wait_ge(sems.h["mh"], sems.n["mh"])
                t.wait_ge(sems.h["exp"], sems.n["exp"])
                t.wait_ge(sems.h["evO"], sems.n["evO"])
                t.matmul(ps_b[:, 0:1], onesr[0:1, :], mrow[0:1, 7:8],
                         start=True, stop=True).then_inc(sems.h["mm"], 1)
            sems.n["mm"] += 1

            @block.vector
            def _(v, h=h, h_loc=h_loc):
                v.wait_ge(sems.h["mm"], sems.n["mm"])
                ins = v.tensor_copy(negM2[:, h_loc:h_loc + 1], ps_b[:, 0:1])
                sems.ch(ins, v, "vch")
                ins = v.tensor_scalar(negMp2[0:1, h_loc:h_loc + 1],
                                      negM2[0:1, h_loc:h_loc + 1], 1.0,
                                      LN16, ALU.mult, ALU.add)
                ins.then_inc(sems.h["mh"], 1)
            sems.n["mh"] += 1
            negm_ready[h_loc] = sems.n["mh"]

        kt_ap = kbuf.rearrange("p (T c) -> p T c", T=4)
        q_ap = qbuf.rearrange("p (T c) -> p T c", T=4)
        for qc in range(4):
            evo_before = sems.n["evO"]
            expb = sems.n["exp"]
            exp_after = {(j, hh): expb + 2 * j + hh + 1
                         for j in range(KT1) for hh in range(2)}
            qk_after = {}
            pv_after = {}

            @block.tensor
            def _(t, qc=qc, vbuf=vbuf, grp=grp):
                for kt in range(KT1 + 1):
                    if kt < KT1:
                        if kt == 0:
                            t.wait_ge(sems.h["kv" + str(grp % 2)],
                                      sems.n["kv" + str(grp % 2)])
                        for hh in range(2):
                            h = grp * 2 + hh
                            Ti = h // 4
                            rb = 32 * (h % 4)
                            if kt >= 2:
                                t.wait_ge(sems.h["exp"],
                                          exp_after[(kt - 2, hh)])
                            t.matmul(sH[hh][kt % 2][:, :],
                                     kt_ap[rb:rb + 16, Ti,
                                           kt * 128:(kt + 1) * 128],
                                     q_ap[rb:rb + 16, Ti,
                                          qc * 512:(qc + 1) * 512],
                                     start=True, stop=True,
                                     tile_position=(rb, 0)
                                     ).then_inc(sems.h["mm"], 1)
                            sems.n["mm"] += 1
                            qk_after[(kt, hh)] = sems.n["mm"]
                    if kt >= 1:
                        j = kt - 1
                        for hh in range(2):
                            t.wait_ge(sems.h["exp"], exp_after[(j, hh)])
                            if j == 0 and hh == 0:
                                t.wait_ge(sems.h["evO"], evo_before)
                            t.matmul(ps_oH[hh][:, :],
                                     vbuf[:, j * 130 + hh * 65:
                                          j * 130 + hh * 65 + 65],
                                     pt[:, (2 * hh + j % 2) * 512:
                                        (2 * hh + j % 2 + 1) * 512],
                                     start=(j == 0), stop=(j == KT1 - 1)
                                     ).then_inc(sems.h["mm"], 1)
                            sems.n["mm"] += 1
                            pv_after[(j, hh)] = sems.n["mm"]

            @block.scalar
            def _(sc, qc=qc):
                for j in range(KT1):
                    for hh in range(2):
                        sc.wait_ge(sems.h["mm"], qk_after[(j, hh)])
                        if j == 0:
                            sc.wait_ge(sems.h["mh"], negm_ready[1])
                        if j >= 2:
                            sc.wait_ge(sems.h["mm"], pv_after[(j - 2, hh)])
                        sl = 2 * hh + j % 2
                        bcol = negMp2 if j == 32 else negM2
                        sc.activation(pt[:, sl * 512:(sl + 1) * 512],
                                      sH[hh][j % 2][:, :], AF.Exp,
                                      bias=bcol[:, hh:hh + 1]
                                      ).then_inc(sems.h["exp"], 1)
                        sems.n["exp"] += 1
                        assert sems.n["exp"] == exp_after[(j, hh)]

            for hh in range(2):
                h = grp * 2 + hh

                @block.vector
                def _(v, h=h, qc=qc, hh=hh):
                    v.wait_ge(sems.h["mm"], pv_after[(KT1 - 1, hh)])
                    v.wait_ge(sems.h["trY"], sems.n["trY"])
                    v.tensor_copy(tb[0:65, :], ps_oH[hh][:, :]).then_inc(
                        sems.h["evO"], 1)
                sems.n["evO"] += 1

                @block.tensor
                def _(t, h=h, qc=qc):
                    t.wait_ge(sems.h["evO"], sems.n["evO"])
                    t.wait_ge(sems.h["uev"], sems.n["uev"])
                    for j in range(4):
                        ins = t.transpose(ps_y[:, j * 66:j * 66 + 65],
                                          tb[0:65, j * 128:(j + 1) * 128],
                                          idnt[0:65, 0:65])
                    ins.then_inc(sems.h["trY"], 1)
                sems.n["trY"] += 1

                @block.vector
                def _(v, h=h, qc=qc):
                    v.wait_ge(sems.h["trY"], sems.n["trY"])
                    dst = U.rearrange("p (t s) -> p t s", t=NT)[
                        :, 4 * qc: 4 * qc + 4, h * 66:h * 66 + 65]
                    srcp = ps_y.rearrange("p (j s) -> p j s", j=4)[
                        :, :, 0:65]
                    v.tensor_copy(dst, srcp).then_inc(sems.h["uev"], 1)
                sems.n["uev"] += 1
        kv_war[grp % 2] = sems.n["mm"]

    def st_col(i, j):
        return stats[:, 16 * i + j: 16 * i + j + 1]

    uev_total = sems.n["uev"]
    x3_done = []
    for tt in range(NT):
        par = tt % 2
        sp = str(par)

        @block.sync
        def _(s, tt=tt, par=par, sp=sp):
            if tt >= 2:
                s.wait_ge(sems.h["ev"], x3_done[tt - 2])
            s.dma_start(out=xt[:, par * D:(par + 1) * D],
                        in_=x2_in[tt * 128:(tt + 1) * 128, :]
                        ).then_inc(sems.h["xd" + sp], 16)
        sems.n["xd" + sp] += 16

        @block.vector
        def _(v, tt=tt, par=par):
            if tt == 0:
                v.wait_ge(sems.h["uev"], uev_total)
            else:
                v.wait_ge(sems.h["lnA"], sems.n["lnA"])
            uv = U.rearrange("p (t h s) -> p t h s", t=NT, h=H0)
            ins = v.reciprocal(rcol[:, 0:16], uv[:, tt, :, 64:65])
            sems.ch(ins, v, "vch")
            for h in range(H0):
                ins = v.tensor_scalar(yt[:, h * 64:(h + 1) * 64],
                                      uv[:, tt, h, 0:64],
                                      rcol[:, h:h + 1], None, ALU.mult)
            sems.ch(ins, v, "vch")
            ins = v.reduce_sum(st_col(tt, 0), yt[:, :], axis=AX.X)
            sems.ch(ins, v, "vch")
            ins = v.tensor_scalar(st_col(tt, 1), st_col(tt, 0), -1.0 / D,
                                  None, ALU.mult)
            ins.then_inc(sems.h["lnV"], 1)
            v.wait_ge(sems.h["lnV"], sems.n["lnV"] + 1)
        sems.n["lnV"] += 1
        yt_ready = sems.n["lnV"]

        @block.scalar
        def _(sc, tt=tt, par=par):
            sc.wait_ge(sems.h["lnV"], yt_ready)
            if tt >= 2:
                sc.wait_ge(sems.h["tr"], sems.n["tr"])
            sc.activation(xn[:, par * D:(par + 1) * D], yt[:, :], AF.Square,
                          accum_out=st_col(tt, 2)).then_inc(sems.h["lnA"], 1)
        sems.n["lnA"] += 1

        @block.vector
        def _(v, tt=tt):
            v.wait_ge(sems.h["lnA"], sems.n["lnA"])
            v.tensor_tensor(st_col(tt, 3), st_col(tt, 1), st_col(tt, 1), ALU.mult)
            ins = v.tensor_scalar(st_col(tt, 4), st_col(tt, 2), 1.0 / D,
                                  None, ALU.mult)
            sems.ch(ins, v, "vch")
            v.tensor_tensor(st_col(tt, 4), st_col(tt, 4), st_col(tt, 3),
                            ALU.subtract).then_inc(sems.h["lnV"], 1)
        sems.n["lnV"] += 1

        @block.scalar
        def _(sc, tt=tt):
            sc.wait_ge(sems.h["lnV"], sems.n["lnV"])
            ins = sc.activation(st_col(tt, 5), st_col(tt, 4), AF.Ln,
                                bias=epsc[:, 0:1])
            sems.ch(ins, sc, "ach")
            sc.activation(st_col(tt, 5), st_col(tt, 5), AF.Exp,
                          scale=-0.5).then_inc(sems.h["lnA"], 1)
        sems.n["lnA"] += 1

        @block.vector
        def _(v, tt=tt, par=par, sp=sp):
            v.wait_ge(sems.h["lnA"], sems.n["lnA"])
            v.wait_ge(sems.h["xd" + sp], 16 * (tt // 2 + 1))
            if tt >= 2:
                v.wait_ge(sems.h["st" + sp], 16 * (tt // 2))
            ins = v.tensor_scalar(x1t[:, par * D:(par + 1) * D], yt[:, :],
                                  st_col(tt, 1), st_col(tt, 5),
                                  ALU.add, ALU.mult)
            sems.ch(ins, v, "vch")
            ins = v.tensor_tensor(x1t[:, par * D:(par + 1) * D],
                                  x1t[:, par * D:(par + 1) * D],
                                  xt[:, par * D:(par + 1) * D], ALU.add)
            ins.then_inc(sems.h["ev"], 1)
            v.wait_ge(sems.h["ev"], sems.n["ev"] + 1)
            ins = v.reduce_sum(st_col(tt, 8), x1t[:, par * D:(par + 1) * D],
                               axis=AX.X)
            sems.ch(ins, v, "vch")
            ins = v.tensor_scalar(st_col(tt, 9), st_col(tt, 8), -1.0 / D,
                                  None, ALU.mult)
            ins.then_inc(sems.h["lnV"], 1)
            v.wait_ge(sems.h["lnV"], sems.n["lnV"] + 1)
        sems.n["ev"] += 1
        sems.n["lnV"] += 1
        x3_done.append(sems.n["ev"])

        @block.sync
        def _(s, tt=tt, par=par, sp=sp):
            s.wait_ge(sems.h["ev"], x3_done[tt])
            s.dma_start(out=x3_s[tt * 128:(tt + 1) * 128, :],
                        in_=x1t[:, par * D:(par + 1) * D]
                        ).then_inc(sems.h["st" + sp], 16)
        sems.n["st" + sp] += 16

        @block.scalar
        def _(sc, tt=tt, par=par):
            sc.wait_ge(sems.h["ev"], x3_done[tt])
            if tt >= 2:
                sc.wait_ge(sems.h["tr"], sems.n["tr"])
            sc.activation(xn[:, par * D:(par + 1) * D],
                          x1t[:, par * D:(par + 1) * D], AF.Square,
                          accum_out=st_col(tt, 10)).then_inc(sems.h["lnA"], 1)
        sems.n["lnA"] += 1

        @block.vector
        def _(v, tt=tt):
            v.wait_ge(sems.h["lnA"], sems.n["lnA"])
            v.tensor_tensor(st_col(tt, 11), st_col(tt, 9), st_col(tt, 9),
                            ALU.mult)
            ins = v.tensor_scalar(st_col(tt, 12), st_col(tt, 10), 1.0 / D,
                                  None, ALU.mult)
            sems.ch(ins, v, "vch")
            v.tensor_tensor(st_col(tt, 12), st_col(tt, 12), st_col(tt, 11),
                            ALU.subtract).then_inc(sems.h["lnV"], 1)
        sems.n["lnV"] += 1

        @block.scalar
        def _(sc, tt=tt):
            sc.wait_ge(sems.h["lnV"], sems.n["lnV"])
            ins = sc.activation(st_col(tt, 13), st_col(tt, 12), AF.Ln,
                                bias=epsc[:, 0:1])
            sems.ch(ins, sc, "ach")
            sc.activation(st_col(tt, 13), st_col(tt, 13), AF.Exp,
                          scale=-0.5).then_inc(sems.h["lnA"], 1)
        sems.n["lnA"] += 1

        @block.vector
        def _(v, tt=tt, par=par):
            v.wait_ge(sems.h["lnA"], sems.n["lnA"])
            if tt >= 2:
                v.wait_ge(sems.h["tr"], sems.n["tr"])
            v.tensor_scalar(xn[:, par * D:(par + 1) * D],
                            x1t[:, par * D:(par + 1) * D],
                            st_col(tt, 9), st_col(tt, 13),
                            ALU.add, ALU.mult).then_inc(sems.h["ev"], 1)
        sems.n["ev"] += 1

        @block.tensor
        def _(t, tt=tt, par=par):
            t.wait_ge(sems.h["ev"], sems.n["ev"])
            if tt >= 1:
                t.wait_ge(sems.h["pe"], sems.n["pe"])
            for c in range(NCH):
                ins = t.transpose(ps_tr[:, c * 128:(c + 1) * 128],
                                  xn[:, par * D + c * 128:
                                     par * D + (c + 1) * 128],
                                  idnt[:, :])
            ins.then_inc(sems.h["tr"], 1)
        sems.n["tr"] += 1

        @block.vector
        def _(v, tt=tt):
            v.wait_ge(sems.h["tr"], sems.n["tr"])
            dst = xn3T.rearrange("p (c t) -> p c t", c=NCH)[
                :, :, tt * 128:(tt + 1) * 128]
            srcp = ps_tr.rearrange("p (c t) -> p c t", c=NCH)
            v.tensor_copy(dst, srcp).then_inc(sems.h["pe"], 1)
        sems.n["pe"] += 1
    xn3_ready = sems.n["pe"]

    yT = xn3T

    h3 = hbuf.rearrange("p (c t) -> p c t", c=NHH)
    u3 = ubuf.rearrange("p (c t) -> p c t", c=NHH)

    ffp = [[psc_viewat(0), psc_viewat(1), psc_viewat(7), psc_viewat(4)],
           [psc_viewat(2), psc_viewat(3), psc_viewat(5), psc_viewat(6)]]
    ff_act = []
    w_half_last = {0: sems.n["mm"], 1: sems.n["mm"]}
    wld_half = {0: sems.n["wld0"], 1: sems.n["wld1"]}
    h3 = hbuf.rearrange("p (c t) -> p c t", c=NHH)
    u3 = ubuf.rearrange("p (c t) -> p c t", c=NHH)
    yT = xn3T
    yT_h = arn.alloc_at(96 * KB, NCH * HTOK, BF16)

    def big_mm_phase(w_dram, scol_off, act_fn, src3, nsrc, dst3, ndt_,
                     src_ready):
        st_guard = sems.n["st"]
        nh = nsrc // 2
        w_half_last[0] = w_half_last[1] = sems.n["mm"]
        for dt in range(ndt_):
            pset = ffp[dt % 2]
            for hf in range(2):
                wnm_h = "wld" + str(hf)
                @block.gpsimd
                def _(g, dt=dt, hf=hf, nh=nh, wl=w_half_last[hf], wnm_h=wnm_h):
                    g.wait_ge(sems.h["mm"], wl)
                    g.dma_start(
                        out=wbA[:, hf * nh * 128:(hf + 1) * nh * 128
                                ].rearrange("p (c o) -> p c o", c=nh),
                        in_=w_dram.ap().rearrange("(c p) o -> p c o", p=128)[
                            :, hf * nh:(hf + 1) * nh,
                            dt * 128:(dt + 1) * 128]
                        ).then_inc(sems.h[wnm_h], 16)
                sems.n[wnm_h] += 16
                wld_half[hf] = sems.n[wnm_h]

            itg = len(ff_act)
            mm_pre = sems.n["mm"]

            @block.tensor
            def _(t, dt=dt, pset=pset, itg=itg, nh=nh, nsrc=nsrc,
                  w0=wld_half[0], w1=wld_half[1]):
                t.wait_ge(sems.h[src_ready[0]], src_ready[1])
                if itg >= 2:
                    wnm, wv = ff_act[itg - 2]
                    t.wait_ge(sems.h[wnm], wv)
                for c in range(nsrc):
                    if c == 0:
                        t.wait_ge(sems.h["wld0"], w0)
                    if c == nh:
                        t.wait_ge(sems.h["wld1"], w1)
                    for qc in range(2):
                        ins = t.matmul(
                            pset[qc][:, :], wbA[:, c * 128:(c + 1) * 128],
                            src3[:, c, qc * 512:(qc + 1) * 512],
                            start=(c == 0), stop=(c == nsrc - 1))
                    if c == nh - 1:
                        ins.then_inc(sems.h["mm"], 1)
                        sems.n["mm"] += 1
                        w_half_last[0] = sems.n["mm"]
                ins.then_inc(sems.h["mm"], 1)
                sems.n["mm"] += 1
                w_half_last[1] = sems.n["mm"]
            mm_done = sems.n["mm"]

            @block.scalar
            def _(sc, dt=dt, pset=pset, mm_done=mm_done, itg=itg,
                  st_guard=st_guard):
                sc.wait_ge(sems.h["mm"], mm_done)
                sc.wait_ge(sems.h["st"], st_guard)
                for qc in range(2):
                    sc.activation(dst3[:, dt, qc * 512:(qc + 1) * 512],
                                  pset[qc][:, :], act_fn,
                                  scale=scl[:, scol_off + dt:scol_off + dt + 1]
                                  ).then_inc(sems.h["ffb"], 1)
                    sems.n["ffb"] += 1
            ff_act.append(("ffb", sems.n["ffb"]))

    xn3T3 = xn3T.rearrange("p (c t) -> p c t", c=NCH)
    yT3f = yT.rearrange("p (c t) -> p c t", c=NCH)
    for half in range(2):
        yh = yT_h.rearrange("p (c t) -> p c t", c=NCH)
        w_half_last[0] = w_half_last[1] = sems.n["mm"]
        for dht in range(NHH):
            pset = ffp[dht % 2]
            for hf in range(2):
                wnm_h = "wld" + str(hf)
                @block.gpsimd
                def _(g, dht=dht, hf=hf, wl=w_half_last[hf], wnm_h=wnm_h):
                    g.wait_ge(sems.h["mm"], wl)
                    g.dma_start(
                        out=wbA[:, hf * 512:hf * 512 + 512].rearrange(
                            "p (c o) -> p c o", c=4),
                        in_=w1_in.ap().rearrange("(c p) o -> p c o", p=128)[
                            :, hf * 4:hf * 4 + 4, dht * 128:(dht + 1) * 128]
                        ).then_inc(sems.h[wnm_h], 16)
                    g.dma_start(
                        out=wbA[:, 1024 + hf * 512:1536 + hf * 512].rearrange(
                            "p (c o) -> p c o", c=4),
                        in_=w2_in.ap().rearrange("(c p) o -> p c o", p=128)[
                            :, hf * 4:hf * 4 + 4, dht * 128:(dht + 1) * 128]
                        ).then_inc(sems.h[wnm_h], 16)
                sems.n[wnm_h] += 32
                wld_half[hf] = sems.n[wnm_h]

            itg = len(ff_act)

            @block.tensor
            def _(t, dht=dht, half=half, pset=pset, itg=itg,
                  w0=wld_half[0], w1=wld_half[1]):
                t.wait_ge(sems.h["pe"], xn3_ready)
                if itg >= 2:
                    wnm, wv = ff_act[itg - 2]
                    t.wait_ge(sems.h[wnm], wv)
                for c in range(NCH):
                    if c == 0:
                        t.wait_ge(sems.h["wld0"], w0)
                    if c == 4:
                        t.wait_ge(sems.h["wld1"], w1)
                    wb_ = (c // 4) * 512 + (c % 4) * 128
                    for qc in range(2):
                        xs = xn3T[:, c * TOK + half * HTOK + qc * 512:
                                  c * TOK + half * HTOK + (qc + 1) * 512]
                        t.matmul(pset[qc][:, :], wbA[:, wb_:wb_ + 128],
                                 xs, start=(c == 0), stop=(c == NCH - 1))
                        ins = t.matmul(pset[2 + qc][:, :],
                                       wbA[:, 1024 + wb_:1024 + wb_ + 128],
                                       xs, start=(c == 0), stop=(c == NCH - 1))
                    if c == 3:
                        ins.then_inc(sems.h["mm"], 1)
                        sems.n["mm"] += 1
                        w_half_last[0] = sems.n["mm"]
                ins.then_inc(sems.h["mm"], 1)
                sems.n["mm"] += 1
                w_half_last[1] = sems.n["mm"]
            mm_done = sems.n["mm"]

            @block.scalar
            def _(sc, dht=dht, pset=pset, mm_done=mm_done, itg=itg):
                sc.wait_ge(sems.h["mm"], mm_done)
                if itg >= 1:
                    wnm, wv = ff_act[itg - 1]
                    sc.wait_ge(sems.h[wnm], wv)
                for qc in range(2):
                    sc.activation(tsb[:, qc * 1024:qc * 1024 + 512],
                                  pset[qc][:, :], AF.Silu,
                                  scale=scl[:, dht:dht + 1])
                    sc.activation(tsb[:, qc * 1024 + 512:qc * 1024 + 1024],
                                  pset[2 + qc][:, :], AF.Identity,
                                  scale=scl[:, NHH + dht:NHH + dht + 1]
                                  ).then_inc(sems.h["ffb"], 1)
                    sems.n["ffb"] += 1

            @block.vector
            def _(v, dht=dht):
                v.wait_ge(sems.h["ffb"], sems.n["ffb"])
                for qc in range(2):
                    v.tensor_tensor(h3[:, dht, qc * 512:(qc + 1) * 512],
                                    tsb[:, qc * 1024:qc * 1024 + 512],
                                    tsb[:, qc * 1024 + 512:qc * 1024 + 1024],
                                    ALU.mult).then_inc(sems.h["ffe"], 1)
                    sems.n["ffe"] += 1
            ff_act.append(("ffe", sems.n["ffe"]))

        big_mm_phase(h1_in, 2 * NHH, AF.Gelu, h3, NHH, u3, NHH,
                     ("ffe", sems.n["ffe"]))
        big_mm_phase(h2_in, 3 * NHH, AF.Gelu, u3, NHH, h3, NHH,
                     ("ffb", sems.n["ffb"]))
        big_mm_phase(w3_in, 4 * NHH, AF.Identity, h3, NHH, yh, NCH,
                     ("ffb", sems.n["ffb"]))

        @block.sync
        def _(s, half=half):
            s.wait_ge(sems.h["ffe"], sems.n["ffe"])
            s.wait_ge(sems.h["ffb"], sems.n["ffb"])
            s.dma_start(out=y_s[:, half * (NCH * HTOK):(half + 1) * (NCH * HTOK)],
                        in_=yT_h).then_inc(sems.h["st"], 16)
        sems.n["st"] += 16
    ffn_done = sems.n["ffe"]
    yst_done = sems.n["st"]

    @block.sync
    def _(s):
        s.wait_ge(sems.h["st"], yst_done)
        for g_ in range(2):
            s.dma_start(
                out=yT.rearrange("p (c t) -> p c t", c=NCH)[
                    :, :, g_ * 1024:(g_ + 1) * 1024],
                in_=y_s.ap().rearrange("p (g c t) -> p g c t", g=2, c=NCH)[
                    :, g_]).then_inc(sems.h["st"], 16)
            sems.n["st"] += 16
    y_ready = sems.n["st"]

    xd_base = {0: sems.n["xd0"], 1: sems.n["xd1"]}
    st_base = {0: sems.n["st0"], 1: sems.n["st1"]}
    x4_done = []
    for tt in range(NT):
        par = tt % 2
        sp = str(par)

        @block.sync
        def _(s, tt=tt, par=par, sp=sp):
            if tt < 2:
                s.wait_ge(sems.h["st0"], st_base[0])
                s.wait_ge(sems.h["st1"], st_base[1])
            if tt >= 2:
                s.wait_ge(sems.h["ev"], x4_done[tt - 2])
            s.dma_start(out=xt[:, par * D:(par + 1) * D],
                        in_=x3_s[tt * 128:(tt + 1) * 128, :]
                        ).then_inc(sems.h["xd" + sp], 16)
        sems.n["xd" + sp] += 16

        @block.tensor
        def _(t, tt=tt, par=par):
            if tt == 0:
                t.wait_ge(sems.h["st"], y_ready)
            if tt >= 1:
                t.wait_ge(sems.h["lnV"], y_done[tt - 1])
            for c in range(NCH):
                ins = t.transpose(ps_tr[:, c * 128:(c + 1) * 128],
                                  yT3f[:, c, tt * 128:(tt + 1) * 128],
                                  idnt[:, :])
            ins.then_inc(sems.h["tr"], 1)
        sems.n["tr"] += 1

        @block.vector
        def _(v, tt=tt):
            v.wait_ge(sems.h["tr"], sems.n["tr"])
            if tt >= 1:
                v.wait_ge(sems.h["lnA"], sems.n["lnA"])
                v.wait_ge(sems.h["ev"], x4_done[tt - 1])
            ins = v.tensor_copy(yt[:, :], ps_tr[:, :])
            sems.ch(ins, v, "vch")
            ins = v.reduce_sum(st_col(tt, 0), yt[:, :], axis=AX.X)
            sems.ch(ins, v, "vch")
            ins = v.tensor_scalar(st_col(tt, 1), st_col(tt, 0), -1.0 / D,
                                  None, ALU.mult)
            ins.then_inc(sems.h["lnV"], 1)
            v.wait_ge(sems.h["lnV"], sems.n["lnV"] + 1)
        sems.n["lnV"] += 1
        y_done = y_done if tt > 0 else {}
        y_done[tt] = sems.n["lnV"]

        @block.scalar
        def _(sc, tt=tt, par=par):
            sc.wait_ge(sems.h["lnV"], y_done[tt])
            sc.activation(xn[:, par * D:(par + 1) * D], yt[:, :], AF.Square,
                          accum_out=st_col(tt, 2)).then_inc(sems.h["lnA"], 1)
        sems.n["lnA"] += 1

        @block.vector
        def _(v, tt=tt):
            v.wait_ge(sems.h["lnA"], sems.n["lnA"])
            v.tensor_tensor(st_col(tt, 3), st_col(tt, 1), st_col(tt, 1), ALU.mult)
            ins = v.tensor_scalar(st_col(tt, 4), st_col(tt, 2), 1.0 / D,
                                  None, ALU.mult)
            sems.ch(ins, v, "vch")
            v.tensor_tensor(st_col(tt, 4), st_col(tt, 4), st_col(tt, 3),
                            ALU.subtract).then_inc(sems.h["lnV"], 1)
        sems.n["lnV"] += 1

        @block.scalar
        def _(sc, tt=tt):
            sc.wait_ge(sems.h["lnV"], sems.n["lnV"])
            ins = sc.activation(st_col(tt, 5), st_col(tt, 4), AF.Ln,
                                bias=epsc[:, 0:1])
            sems.ch(ins, sc, "ach")
            sc.activation(st_col(tt, 5), st_col(tt, 5), AF.Exp,
                          scale=-0.5).then_inc(sems.h["lnA"], 1)
        sems.n["lnA"] += 1

        @block.vector
        def _(v, tt=tt, par=par, sp=sp):
            v.wait_ge(sems.h["lnA"], sems.n["lnA"])
            v.wait_ge(sems.h["xd" + sp], xd_base[par] + 16 * (tt // 2 + 1))
            if tt >= 2:
                v.wait_ge(sems.h["st" + sp], st_base[par] + 16 * (tt // 2))
            ins = v.tensor_scalar(x1t[:, par * D:(par + 1) * D], yt[:, :],
                                  st_col(tt, 1), st_col(tt, 5),
                                  ALU.add, ALU.mult)
            sems.ch(ins, v, "vch")
            ins = v.tensor_tensor(x1t[:, par * D:(par + 1) * D],
                                  x1t[:, par * D:(par + 1) * D],
                                  xt[:, par * D:(par + 1) * D], ALU.add)
            ins.then_inc(sems.h["ev"], 1)
        sems.n["ev"] += 1
        x4_done.append(sems.n["ev"])

        @block.sync
        def _(s, tt=tt, par=par, sp=sp):
            s.wait_ge(sems.h["ev"], x4_done[tt])
            s.dma_start(out=x4_o[tt * 128:(tt + 1) * 128, :],
                        in_=x1t[:, par * D:(par + 1) * D]
                        ).then_inc(sems.h["st" + sp], 16)
        sems.n["st" + sp] += 16

    @block.sync
    def _(s):
        s.wait_ge(sems.h["st0"], sems.n["st0"])
        s.wait_ge(sems.h["st1"], sems.n["st1"])


def build_D():
    nc = bass.Bass("TRN2", target_bir_lowering=False)
    io = dram_D(nc)
    with contextlib.ExitStack() as stk:
        io["_sb"], io["_ps"] = alloc_pools(nc, stk)
        block = stk.enter_context(nc.Block())
        emit_D(nc, block, stk, io)
    return nc


def _toT(a2d, nch=8):
    t = a2d.shape[0]
    return np.ascontiguousarray(
        a2d.T.reshape(nch, 128, t).transpose(1, 0, 2).reshape(128, nch * t))


def _pack_padded(matT, width):
    out = np.zeros((128, 4 * width), dtype=matT.dtype)
    for h in range(16):
        T_, j = h // 4, h % 4
        out[32 * j:32 * j + 16, T_ * width:T_ * width + matT.shape[1]] = \
            matT[h * 16:(h + 1) * 16]
    return out


def kernel(**inputs):
    return kernel_fused(**inputs)


def kernel_unfused(**inputs):
    sh = _prep_shared(inputs)
    x = np.asarray(inputs["L0_attn_x"])
    rel_ext = sh["rel_ext"]
    rel16p = np.ascontiguousarray(np.concatenate([rel_ext, rel_ext[:, -1:]], axis=1))
    rs_by_g = [_prep_rs(rel_ext, 0), _prep_rs(rel_ext, 1)]

    ncA = build_A()
    mapsA = []
    for c in range(8):
        b, g = c // 2, c % 2
        mapsA.append({
            "x_in": np.ascontiguousarray(x[b, g * 2048:(g + 1) * 2048, :]),
            "wq0": sh["wq0"], "wk0": sh["wk0"], "wv0": sh["wv0"],
            "bq0": sh["bq0"], "bk0": sh["bk0"], "bv0": sh["bv0"],
            "ident": sh["ident"], "ones_row": sh["ones_row"],
        })
    resA = run_bass_kernel_spmd(ncA, mapsA, core_ids=list(range(8))).results

    ncB = build_B()
    mapsB = []
    for c in range(8):
        b, g = c // 2, c % 2
        kT0 = np.asarray(resA[2 * b]["kT_o"]).reshape(128, NCH, TOK)
        kT1 = np.asarray(resA[2 * b + 1]["kT_o"]).reshape(128, NCH, TOK)
        kf = np.ascontiguousarray(
            np.concatenate([kT0, kT1], axis=2).reshape(128, NCH * SKEY))
        vf = np.ascontiguousarray(np.concatenate(
            [np.asarray(resA[2 * b]["v_o"]), np.asarray(resA[2 * b + 1]["v_o"])],
            axis=0))
        mapsB.append({
            "x_in": np.ascontiguousarray(x[b, g * 2048:(g + 1) * 2048, :]),
            "qT_i": np.asarray(resA[c]["qT_o"]),
            "kf": kf, "vf": vf, "rs": rs_by_g[g], "rel16": rel16p,
            "gw": sh["gw"], "gb": sh["gb"], "ident": sh["ident"],
            "ones_row": sh["ones_row"], "ones_col": sh["ones_col"],
        })
    resB = run_bass_kernel_spmd(ncB, mapsB, core_ids=list(range(8))).results

    cnt8 = np.ascontiguousarray(np.concatenate(
        [np.asarray(resB[c]["cnt_o"]) for c in range(8)], axis=0))
    ncC = build_C()
    mapsC = []
    for c in range(8):
        mapsC.append({
            "x1_in": np.asarray(resB[c]["x1_o"]),
            "xn1T_i": np.asarray(resB[c]["xn1T_o"]),
            "cnt8": cnt8, "ew": sh["ew"], "be": sh["be"],
            "a1Wq": sh["a1Wq"], "a1Wqc": sh["a1Wqc"], "a1Wk": sh["a1Wk"],
            "a1Wkc": sh["a1Wkc"], "a1Wv": sh["a1Wv"], "a1Wvc": sh["a1Wvc"],
            "a1bq": sh["a1bq"], "a1bqc": sh["a1bqc"], "a1bk": sh["a1bk"],
            "a1bkc": sh["a1bkc"], "a1bv": sh["a1bv"], "a1bvc": sh["a1bvc"],
            "ident": sh["ident"], "ones_row": sh["ones_row"],
            "ones_col": sh["ones_col"],
        })
    resC = run_bass_kernel_spmd(ncC, mapsC, core_ids=list(range(8))).results

    ncD = build_D()
    mapsD = []
    for c in range(8):
        b, g = c // 2, c % 2
        q1T = np.asarray(resC[c]["q1T_o"])
        qfullT = np.ascontiguousarray(
            q1T.reshape(128, 2, TOK).transpose(1, 0, 2).reshape(256, TOK))
        k1T0 = np.asarray(resC[2 * b]["k1T_o"]).reshape(128, 2, TOK)
        k1T1 = np.asarray(resC[2 * b + 1]["k1T_o"]).reshape(128, 2, TOK)
        kfT = np.ascontiguousarray(
            np.concatenate([k1T0, k1T1], axis=2).reshape(128, 2, SKEY
                                                         ).transpose(1, 0, 2
                                                                     ).reshape(256, SKEY))
        vfull = np.ascontiguousarray(np.concatenate(
            [np.asarray(resC[2 * b]["v1_o"]), np.asarray(resC[2 * b + 1]["v1_o"])],
            axis=0))
        mapsD.append({
            "x2_in": np.asarray(resC[c]["x2_o"]),
            "qb_in": _pack_padded(qfullT, TOK),
            "kb_in": _pack_padded(kfT, 4224),
            "v1f": vfull,
            "a1bk": sh["a1bk"], "a1bkc": sh["a1bkc"],
            "a1bv": sh["a1bv"], "a1bvc": sh["a1bvc"],
            "w1T": sh["w1T"], "w2T": sh["w2T"], "h1T": sh["h1T"],
            "h2T": sh["h2T"], "w3T": sh["w3T"],
            "s1": sh["s1"], "s2": sh["s2"], "sh1": sh["sh1"],
            "sh2": sh["sh2"], "s3": sh["s3"],
            "ident": sh["ident"], "ones_row": sh["ones_row"],
            "ones_col": sh["ones_col"],
        })
    resD = run_bass_kernel_spmd(ncD, mapsD, core_ids=list(range(8))).results

    out = np.empty((B, S, D), dtype=np.float32)
    for c in range(8):
        b, g = c // 2, c % 2
        out[b, g * 2048:(g + 1) * 2048, :] = np.asarray(resD[c]["x4_o"])
    return out



PAIRS_RG = [[0, 1], [2, 3], [4, 5], [6, 7]]
ALL8_RG = [list(range(8))]


def emit_barrier(nc, block, gsems, kctr):
    kctr[0] += 1
    k = kctr[0]
    for engname in ("tensor", "vector", "scalar", "gpsimd", "sync"):
        dec = getattr(block, engname)

        @dec
        def _(e):
            e.drain()
            e.sem_inc(gsems.h["bar"], 1)
            e.wait_ge(gsems.h["bar"], 5 * k)


def build_fused():
    nc = bass.Bass("TRN2", target_bir_lowering=False)
    x_in = nc.dram_tensor("x_in", [TOK, D], F32, kind="ExternalInput")
    ext = {}
    for nm, shp in [("wq0", [D, D]), ("wk0", [D, D]), ("wv0", [D, D]),
                    ("bq0", [128, 8]), ("bk0", [128, 8]), ("bv0", [1, D]),
                    ("ident", [128, 128]), ("ones_row", [1, 128]),
                    ("ones_col", [128, 1]),
                    ("rs", [H0 * 128, RSW]), ("rel16", [H0, 8192]),
                    ("gw", [128, NCH * 8]), ("gb", [8, 1]),
                    ("ew", [8 * D, D]), ("be", [128, NCH * 8]),
                    ("a1Wq", [D, 256]), ("a1Wqc", [D, 256]),
                    ("a1Wk", [D, 256]), ("a1Wkc", [D, 256]),
                    ("a1Wv", [D, D]), ("a1Wvc", [D, D]),
                    ("a1bq", [128, 2]), ("a1bqc", [128, 2]),
                    ("a1bk", [128, 2]), ("a1bkc", [128, 2]),
                    ("a1bv", [128, 8]), ("a1bvc", [128, 8]),
                    ("w1T", [D, DH]), ("w2T", [D, DH]),
                    ("h1T", [DH, DH]), ("h2T", [DH, DH]), ("w3T", [DH, D]),
                    ("s1", [128, NHH]), ("s2", [128, NHH]),
                    ("sh1", [128, NHH]), ("sh2", [128, NHH]), ("s3", [128, 8])]:
        ext[nm] = nc.dram_tensor(nm, shp, F32, kind="ExternalInput")
    x4_o = nc.dram_tensor("x4_o", [TOK, D], F32, kind="ExternalOutput")
    qT_d = nc.dram_tensor("qT_d", [128, NCH * TOK], BF16)
    kT_d = nc.dram_tensor("kT_d", [128, NCH * TOK], BF16)
    v_d = nc.dram_tensor("v_d", [TOK, D], BF16)
    kag0 = nc.dram_tensor("kag0", [256, NCH * TOK], BF16)
    vag0 = nc.dram_tensor("vag0", [SKEY, D], BF16)
    x1_d = nc.dram_tensor("x1_d", [TOK, D], F32)
    xn1T_d = nc.dram_tensor("xn1T_d", [128, NCH * TOK], BF16)
    cnt_d = nc.dram_tensor("cnt_d", [1, 8], F32)
    car = nc.dram_tensor("car", [1, 8], F32)
    x2_d = nc.dram_tensor("x2_d", [TOK, D], F32)
    q1T_d = nc.dram_tensor("q1T_d", [128, 2 * TOK], BF16)
    k1T_d = nc.dram_tensor("k1T_d", [128, 2 * TOK], BF16)
    v1_d = nc.dram_tensor("v1_d", [TOK, D], BF16)
    k1ag = nc.dram_tensor("k1ag", [256, 2 * TOK], BF16)
    v1ag = nc.dram_tensor("v1ag", [SKEY, D], BF16)
    x3_s = nc.dram_tensor("x3_s", [TOK, D], F32)
    y_s = nc.dram_tensor("y_s", [128, 2 * NCH * 1024], BF16)
    pad_s = nc.dram_tensor("pad_s", [10, 128], BF16)

    with contextlib.ExitStack() as stk:
        sbc, psc = alloc_pools(nc, stk)
        gsems = Sems(nc, stk, ["bar", "cc"], prefix="G_")
        semsA = Sems(nc, stk, ["ld", "wld", "xd0", "xd1", "lnV", "lnA", "tr",
                               "ev", "mm", "pe", "st", "st0", "st1", "bc",
                               "vch", "ach"], prefix="A_")
        semsB = Sems(nc, stk, [
            "ld", "ld2", "qld", "kv0", "kv1", "kvr", "rs0", "rs1", "bmv",
            "mm", "bias", "exp", "evO", "trY", "uev", "mh",
            "xd0", "xd1", "lnV", "lnA", "vch", "ach", "tr", "pe",
            "st", "st0", "st1", "gmm", "gev", "bmx", "ev"], prefix="B_")
        semsC = Sems(nc, stk, ["ld", "ld2", "ew0", "ew1", "mm", "ev", "lnV",
                               "lnA", "vch", "ach", "tr", "pe", "xd0", "xd1",
                               "st", "st0", "st1", "wc", "qkm", "qke"],
                     prefix="C_")
        semsD = Sems(nc, stk, [
            "ld", "ld2", "kv0", "kv1", "mm", "exp", "evO", "trY", "uev", "mh",
            "xd0", "xd1", "lnV", "lnA", "vch", "ach", "tr", "pe",
            "st", "st0", "st1", "pad", "wld0", "wld1", "ev", "ffe", "ffb"],
            prefix="D_")
        block = stk.enter_context(nc.Block())
        kctr = [0]

        ioA = {"x_in": x_in, "wq": ext["wq0"], "wk": ext["wk0"],
               "wv": ext["wv0"], "bq": ext["bq0"], "bk": ext["bk0"],
               "bv": ext["bv0"], "ident_in": ext["ident"],
               "ones_row_in": ext["ones_row"],
               "qT_o": qT_d, "kT_o": kT_d, "v_o": v_d,
               "agksrc": kT_d, "agkdst": kag0,
               "agvsrc": v_d, "agvdst": vag0,
               "_sb": Carver(sbc.t, sbc.nbytes), "_ps": PsumPool(psc.t),
               "_sems": semsA}
        emit_A(nc, block, stk, ioA, fused=True)

        emit_barrier(nc, block, gsems, kctr)

        ioB = {"x_in": x_in, "qT_i": qT_d, "kf": kag0, "vf": vag0,
               "rs_in": ext["rs"], "rel16": ext["rel16"], "gw_in": ext["gw"],
               "gb_in": ext["gb"], "ident_in": ext["ident"],
               "ones_row_in": ext["ones_row"], "ones_col_in": ext["ones_col"],
               "x1_o": x1_d, "xn1T_o": xn1T_d, "cnt_o": cnt_d,
               "_sb": Carver(sbc.t, sbc.nbytes), "_ps": PsumPool(psc.t),
               "_sems": semsB}
        emit_B(nc, block, stk, ioB, fused=True)

        emit_barrier(nc, block, gsems, kctr)

        @block.gpsimd
        def _(g):
            g.collective_compute(
                "AllReduce", mybir.AluOpType.add, replica_groups=ALL8_RG,
                ins=[cnt_d.ap().opt()], outs=[car.ap().opt()],
            ).then_inc(gsems.h["cc"], 1)
            g.wait_ge(gsems.h["cc"], 1)
        gsems.n["cc"] = 1

        emit_barrier(nc, block, gsems, kctr)

        ioC = {"x1_in": x1_d, "xn1T_i": xn1T_d, "cnt8": car,
               "ew_in": ext["ew"], "be_in": ext["be"],
               "wq_in": ext["a1Wq"], "wqc_in": ext["a1Wqc"],
               "wk_in": ext["a1Wk"], "wkc_in": ext["a1Wkc"],
               "wv_in": ext["a1Wv"], "wvc_in": ext["a1Wvc"],
               "bq_in": ext["a1bq"], "bqc_in": ext["a1bqc"],
               "bk_in": ext["a1bk"], "bkc_in": ext["a1bkc"],
               "bv_in": ext["a1bv"], "bvc_in": ext["a1bvc"],
               "ident_in": ext["ident"], "ones_row_in": ext["ones_row"],
               "ones_col_in": ext["ones_col"],
               "x2_o": x2_d, "q1T_o": q1T_d, "k1T_o": k1T_d, "v1_o": v1_d,
               "agksrc": k1T_d, "agkdst": k1ag,
               "agvsrc": v1_d, "agvdst": v1ag,
               "_sb": Carver(sbc.t, sbc.nbytes), "_ps": PsumPool(psc.t),
               "_sems": semsC}
        emit_C(nc, block, stk, ioC, fused=True)

        emit_barrier(nc, block, gsems, kctr)

        ioD = {"x2_in": x2_d, "qb_in": q1T_d, "kb_in": k1ag, "v1f": v1ag,
               "bk_in": ext["a1bk"], "bkc_in": ext["a1bkc"],
               "bv_in": ext["a1bv"], "bvc_in": ext["a1bvc"],
               "w1_in": ext["w1T"], "w2_in": ext["w2T"],
               "h1_in": ext["h1T"], "h2_in": ext["h2T"], "w3_in": ext["w3T"],
               "s1_in": ext["s1"], "s2_in": ext["s2"], "sh1_in": ext["sh1"],
               "sh2_in": ext["sh2"], "s3_in": ext["s3"],
               "ident_in": ext["ident"], "ones_row_in": ext["ones_row"],
               "ones_col_in": ext["ones_col"],
               "x4_o": x4_o, "x3_s": x3_s, "y_s": y_s, "pad_s": pad_s,
               "_sb": Carver(sbc.t, sbc.nbytes), "_ps": PsumPool(psc.t),
               "_sems": semsD}
        emit_D(nc, block, stk, ioD, fused=True)
    return nc


def kernel_fused(**inputs):
    sh = _prep_shared(inputs)
    x = np.asarray(inputs["L0_attn_x"])
    rel_ext = sh["rel_ext"]
    rel16p = np.ascontiguousarray(
        np.concatenate([rel_ext, rel_ext[:, -1:]], axis=1))
    rs_by_g = [_prep_rs(rel_ext, 0), _prep_rs(rel_ext, 1)]
    nc = build_fused()
    maps = []
    for c in range(8):
        b, g = c // 2, c % 2
        m = {"x_in": np.ascontiguousarray(x[b, g * 2048:(g + 1) * 2048, :]),
             "rs": rs_by_g[g], "rel16": rel16p}
        for nm in ["wq0", "wk0", "wv0", "bq0", "bk0", "bv0", "ident",
                   "ones_row", "ones_col", "gw", "gb", "ew", "be",
                   "a1Wq", "a1Wqc", "a1Wk", "a1Wkc", "a1Wv", "a1Wvc",
                   "a1bq", "a1bqc", "a1bk", "a1bkc", "a1bv", "a1bvc",
                   "w1T", "w2T", "h1T", "h2T", "w3T",
                   "s1", "s2", "sh1", "sh2", "s3"]:
            m[nm] = sh[nm]
        maps.append(m)
    res = run_bass_kernel_spmd(nc, maps, core_ids=list(range(8))).results
    out = np.empty((B, S, D), dtype=np.float32)
    for c in range(8):
        b, g = c // 2, c % 2
        out[b, g * 2048:(g + 1) * 2048, :] = np.asarray(res[c]["x4_o"])
    return out


# version 17
# speedup vs baseline: 1.1333x; 1.1333x over previous
import contextlib
import numpy as np
import ml_dtypes
import concourse.bass as bass
import concourse.mybir as mybir
from concourse.bass_utils import run_bass_kernel_spmd

F32 = mybir.dt.float32
BF16 = mybir.dt.bfloat16
AF = mybir.ActivationFunctionType
ALU = mybir.AluOpType
AX = mybir.AxisListType
BF16NP = ml_dtypes.bfloat16

B, S, D = 4, 4096, 1024
TOK = 2048
NT = TOK // 128
NCH = D // 128
H0, DK0 = 16, 64
H1, QL1, VL1 = 16, 16, 64
DH = 4096
NHH = DH // 128
SKEY = 4096
RSW = 6016
LN16 = float(np.log(16.0))
EPS = 1e-5


def _col8(b):
    return np.ascontiguousarray(b.reshape(-1, 128).T)


def _prep_shared(inp):
    s = {}
    s["wq0"] = inp["L0_attn_Wq"]
    s["wk0"] = inp["L0_attn_Wk"]
    s["wv0"] = inp["L0_attn_Wv"]
    s["bq0"] = _col8(inp["L0_attn_bq"])
    s["bk0"] = _col8(inp["L0_attn_bk"])
    s["bv0"] = inp["L0_attn_bv"].reshape(1, D)
    gwT = inp["L0_ffn_gate_w"].T
    s["gw"] = np.ascontiguousarray(
        gwT.reshape(NCH, 128, 8).transpose(1, 0, 2).reshape(128, NCH * 8))
    s["gb"] = inp["L0_ffn_gate_b"].reshape(8, 1)
    s["ew"] = np.ascontiguousarray(
        inp["L0_ffn_expert_w"].transpose(0, 2, 1)).reshape(8 * D, D)
    s["be"] = np.ascontiguousarray(
        inp["L0_ffn_expert_b"].reshape(8, NCH, 128).transpose(2, 1, 0).reshape(128, NCH * 8))
    for nm in ("Wq", "Wqc", "Wk", "Wkc", "Wv", "Wvc"):
        s["a1" + nm] = inp["L1_attn_" + nm]
    s["a1bq"] = np.ascontiguousarray(inp["L1_attn_bq"].reshape(-1, 128).T)
    s["a1bqc"] = np.ascontiguousarray(inp["L1_attn_bqc"].reshape(-1, 128).T)
    s["a1bk"] = np.ascontiguousarray(inp["L1_attn_bk"].reshape(-1, 128).T)
    s["a1bkc"] = np.ascontiguousarray(inp["L1_attn_bkc"].reshape(-1, 128).T)
    s["a1bv"] = _col8(inp["L1_attn_bv"])
    s["a1bvc"] = _col8(inp["L1_attn_bvc"])
    s["w1T"] = np.ascontiguousarray(inp["L1_ffn_w1_w"].T)
    s["w2T"] = np.ascontiguousarray(inp["L1_ffn_w2_w"].T)
    s["h1T"] = np.ascontiguousarray(inp["L1_ffn_h1_w"].T)
    s["h2T"] = np.ascontiguousarray(inp["L1_ffn_h2_w"].T)
    s["w3T"] = np.ascontiguousarray(inp["L1_ffn_w3_w"].T)
    s["s1"] = np.ascontiguousarray(inp["L1_ffn_w1_s"].reshape(-1, 128).T)
    s["s2"] = np.ascontiguousarray(inp["L1_ffn_w2_s"].reshape(-1, 128).T)
    s["sh1"] = np.ascontiguousarray(inp["L1_ffn_h1_s"].reshape(-1, 128).T)
    s["sh2"] = np.ascontiguousarray(inp["L1_ffn_h2_s"].reshape(-1, 128).T)
    s["s3"] = _col8(inp["L1_ffn_w3_s"])
    s["ident"] = np.eye(128, dtype=np.float32)
    s["ones_row"] = np.ones((1, 128), dtype=np.float32)
    s["ones_col"] = np.ones((128, 1), dtype=np.float32)
    rel = inp["L0_attn_rel_emb"]
    s["rel_ext"] = np.ascontiguousarray(np.concatenate(
        [rel, np.broadcast_to(rel[:, 4094:4095], (H0, 4096))], axis=1))
    return s


def _prep_rs(rel_ext, g):
    out = np.empty((H0, 128, RSW), dtype=np.float32)
    base = g * 2048 + 127
    for h in range(H0):
        arr = rel_ext[h]
        v = np.lib.stride_tricks.as_strided(
            arr[base:], shape=(128, RSW),
            strides=(-arr.strides[0], arr.strides[0]))
        out[h] = v
    return out.reshape(H0 * 128, RSW)



class Sems:
    def __init__(self, nc, stack, names, prefix=""):
        self.h = {}
        self.n = {}
        for nm in names:
            self.h[nm] = stack.enter_context(nc.semaphore(prefix + nm))
            self.n[nm] = 0

    def tick(self, instr, nm, k=1):
        instr.then_inc(self.h[nm], k)
        self.n[nm] += k
        return self.n[nm]

    def w(self, eng, nm, val=None):
        eng.wait_ge(self.h[nm], self.n[nm] if val is None else val)

    def ch(self, instr, eng, nm):
        instr.then_inc(self.h[nm], 1)
        self.n[nm] += 1
        eng.wait_ge(self.h[nm], self.n[nm])


class Carver:
    def __init__(self, tensor, nbytes, base=0):
        self.t = tensor
        self.nbytes = nbytes
        self.base = base
        self.cur = 0

    def alloc(self, cols, dtype, rows=128):
        esz = 4 if dtype == F32 else 2
        al = self.cur % 4
        if al:
            self.cur += 4 - al
        assert self.cur + cols * esz <= self.nbytes, (
            f"arena overflow {self.cur} + {cols}*{esz} > {self.nbytes}")
        off = (self.base + self.cur) // esz
        self.cur += cols * esz
        view = self.t.ap().bitcast(dtype)
        return view[0:rows, off:off + cols]

    def alloc_at(self, byte_off, cols, dtype, rows=128):
        esz = 4 if dtype == F32 else 2
        assert byte_off % 4 == 0 and byte_off + cols * esz <= self.nbytes
        off = (self.base + byte_off) // esz
        view = self.t.ap().bitcast(dtype)
        return view[0:rows, off:off + cols]

    def push(self):
        pass

    def pop(self):
        pass

    def sub(self, kb):
        al = self.cur % 4
        if al:
            self.cur += 4 - al
        s = Carver(self.t, kb * 1024, base=self.base + self.cur)
        self.cur += kb * 1024
        assert self.cur <= self.nbytes
        return s


class PsumPool:
    def __init__(self, pst):
        self.t = pst
        self.cur = 0

    def view_at(self, bank, shape, dtype):
        esz = 4 if dtype == F32 else 2
        rows, cols = shape
        off = bank * 2048 // esz
        view = self.t.ap().bitcast(dtype)
        return view[0:rows, off:off + cols]

    def alloc(self, nm, shape, dtype):
        esz = 4 if dtype == F32 else 2
        rows, cols = shape
        if self.cur % 2048:
            self.cur += 2048 - self.cur % 2048
        need = cols * esz
        assert self.cur + need <= 16384, f"psum overflow at {nm}"
        off = self.cur // esz
        self.cur += need
        view = self.t.ap().bitcast(dtype)
        return view[0:rows, off:off + cols]


def alloc_pools(nc, stk, kb=207):
    arena = stk.enter_context(nc.sbuf_tensor("arena", [128, kb * 512], BF16))
    pspool = stk.enter_context(nc.psum_tensor("pspool", [128, 4096], F32))
    return Carver(arena, kb * 1024), PsumPool(pspool)


def ln_stats_emit(sems, v, sc, x_ap, junk, stats, wait_fn=None):
    F = x_ap.shape[-1]
    inv = 1.0 / F
    v.reduce_sum(stats["s"], x_ap, axis=AX.X)
    v.tensor_scalar(stats["mneg"], stats["s"], -inv, None, ALU.mult)
    sc.activation(junk, x_ap, AF.Square, accum_out=stats["ms2"])



def dram_A(nc):
    io = {}
    io["x_in"] = nc.dram_tensor("x_in", [TOK, D], F32, kind="ExternalInput")
    io["wq"] = nc.dram_tensor("wq0", [D, D], F32, kind="ExternalInput")
    io["wk"] = nc.dram_tensor("wk0", [D, D], F32, kind="ExternalInput")
    io["wv"] = nc.dram_tensor("wv0", [D, D], F32, kind="ExternalInput")
    io["bq"] = nc.dram_tensor("bq0", [128, 8], F32, kind="ExternalInput")
    io["bk"] = nc.dram_tensor("bk0", [128, 8], F32, kind="ExternalInput")
    io["bv"] = nc.dram_tensor("bv0", [1, D], F32, kind="ExternalInput")
    io["ident_in"] = nc.dram_tensor("ident", [128, 128], F32, kind="ExternalInput")
    io["ones_row_in"] = nc.dram_tensor("ones_row", [1, 128], F32, kind="ExternalInput")
    io["qT_o"] = nc.dram_tensor("qT_o", [128, NCH * TOK], BF16, kind="ExternalOutput")
    io["kT_o"] = nc.dram_tensor("kT_o", [128, NCH * TOK], BF16, kind="ExternalOutput")
    io["v_o"] = nc.dram_tensor("v_o", [TOK, D], BF16, kind="ExternalOutput")
    return io


def emit_A(nc, block, stk, io, fused=False):
    nc = bass.Bass("TRN2", target_bir_lowering=False)
    x_in = io["x_in"]
    wq = io["wq"]
    wk = io["wk"]
    wv = io["wv"]
    bq = io["bq"]
    bk = io["bk"]
    bv = io["bv"]
    ident_in = io["ident_in"]
    ones_row_in = io["ones_row_in"]
    qT_o = io["qT_o"]
    kT_o = io["kT_o"]
    v_o = io["v_o"]
    T = lambda nm, shape, dt: io["_sb"].alloc(shape[-1], dt, rows=shape[0])
    PS = lambda nm, shape, dt: io["_ps"].alloc(nm, shape, dt)
    xnT = T("xnT", [128, NCH * TOK], BF16)
    qT = T("qT", [128, NCH * TOK], BF16)
    kT = T("kT", [128, NCH * TOK], BF16)
    wsb = T("wsb", [128, NCH * D], BF16)
    wsb2 = T("wsb2", [128, NCH * D], BF16)
    xt = T("xt", [128, 2 * D], F32)
    xn = T("xn", [128, 2 * D], BF16)
    stats = T("stats", [128, 8 * NT], F32)
    idnt = T("idnt", [128, 128], BF16)
    onesr = T("onesr", [1, 128], F32)
    bqs = T("bqs", [128, 8], F32)
    bks = T("bks", [128, 8], F32)
    bvrow = T("bvrow", [1, D], F32)
    bvb = T("bvb", [128, D], F32)
    vout = T("vout", [128, 2 * D], BF16)
    ps_tr = PS("ps_tr", [128, D], BF16)
    ps_a = PS("ps_a", [128, 512], F32)
    ps_b = PS("ps_b", [128, 512], F32)
    epsc = T("epsc", [128, 1], F32)
    sems = io.get("_sems") or Sems(
        nc, stk, ["ld", "wld", "xd0", "xd1", "lnV", "lnA", "tr",
                  "ev", "mm", "pe", "st", "st0", "st1", "bc",
                  "vch", "ach"], prefix="A_")

    @block.vector
    def _(v):
        v.memset(epsc[:, :], EPS)

    @block.gpsimd
    def _(g):
        g.dma_start(out=idnt[:, :], in_=ident_in[:, :]).then_inc(sems.h["ld"], 16)
        g.dma_start(out=bqs[:, :], in_=bq[:, :]).then_inc(sems.h["ld"], 16)
        g.dma_start(out=bks[:, :], in_=bk[:, :]).then_inc(sems.h["ld"], 16)
        g.dma_start(out=bvrow[:, :], in_=bv[:, :]).then_inc(sems.h["ld"], 16)
        g.dma_start(out=onesr[:, :], in_=ones_row_in[:, :]).then_inc(sems.h["ld"], 16)
        g.dma_start(out=wsb.rearrange("p (c o) -> p c o", c=NCH),
                    in_=wv.ap().rearrange("(c p) o -> p c o", p=128)
                    ).then_inc(sems.h["wld"], 16)
        g.dma_start(out=wsb2.rearrange("p (c o) -> p c o", c=NCH),
                    in_=wk.ap().rearrange("(c p) o -> p c o", p=128)
                    ).then_inc(sems.h["wld"], 16)
    sems.n["ld"] = 16 * 5
    sems.n["wld"] = 32

    @block.tensor
    def _(t):
        sems.w(t, "ld")
        t.matmul(ps_a[:, :], onesr[0:1, :], bvrow[0:1, 0:512],
                 start=True, stop=True).then_inc(sems.h["bc"], 1)
        t.matmul(ps_b[:, :], onesr[0:1, :], bvrow[0:1, 512:1024],
                 start=True, stop=True).then_inc(sems.h["bc"], 1)
    sems.n["bc"] = 2

    @block.vector
    def _(v):
        sems.w(v, "bc")
        v.tensor_copy(bvb[:, 0:512], ps_a[:, :])
        v.tensor_copy(bvb[:, 512:1024], ps_b[:, :]).then_inc(sems.h["bc"], 1)
    sems.n["bc"] = 3

    def st_col(i, j):
        return stats[:, 8 * i + j: 8 * i + j + 1]

    for i in range(NT):
        pb = i % 2

        @block.sync
        def _(s, i=i, pb=pb):
            if i >= 2:
                s.wait_ge(sems.h["tr"], i - 1)
            s.dma_start(out=xt[:, pb * D:(pb + 1) * D],
                        in_=x_in[i * 128:(i + 1) * 128, :]
                        ).then_inc(sems.h["xd" + str(pb)], 16)
        sems.n["xd" + str(pb)] += 16

        @block.vector
        def _(v, i=i, pb=pb):
            v.wait_ge(sems.h["xd" + str(pb)], 16 * (i // 2 + 1))
            xap = xt[:, pb * D:(pb + 1) * D]
            ins = v.reduce_sum(st_col(i, 0), xap, axis=AX.X)
            sems.ch(ins, v, "vch")
            ins = v.tensor_scalar(st_col(i, 1), st_col(i, 0), -1.0 / D, None,
                                  ALU.mult)
            ins.then_inc(sems.h["lnV"], 1)
            v.wait_ge(sems.h["lnV"], sems.n["lnV"] + 1)
        sems.n["lnV"] += 1

        @block.scalar
        def _(sc, i=i, pb=pb):
            sc.wait_ge(sems.h["xd" + str(pb)], 16 * (i // 2 + 1))
            if i >= 2:
                sc.wait_ge(sems.h["tr"], i - 1)
            xap = xt[:, pb * D:(pb + 1) * D]
            sc.activation(xn[:, pb * D:(pb + 1) * D], xap, AF.Square,
                          accum_out=st_col(i, 2)).then_inc(sems.h["lnA"], 1)
        sems.n["lnA"] += 1

        @block.vector
        def _(v, i=i):
            v.wait_ge(sems.h["lnA"], sems.n["lnA"])
            v.tensor_tensor(st_col(i, 3), st_col(i, 1), st_col(i, 1), ALU.mult)
            ins = v.tensor_scalar(st_col(i, 4), st_col(i, 2), 1.0 / D, None,
                                  ALU.mult)
            sems.ch(ins, v, "vch")
            v.tensor_tensor(st_col(i, 4), st_col(i, 4), st_col(i, 3),
                            ALU.subtract).then_inc(sems.h["lnV"], 1)
        sems.n["lnV"] += 1

        @block.scalar
        def _(sc, i=i):
            sc.wait_ge(sems.h["lnV"], sems.n["lnV"])
            ins = sc.activation(st_col(i, 5), st_col(i, 4), AF.Ln,
                                bias=epsc[:, 0:1])
            sems.ch(ins, sc, "ach")
            sc.activation(st_col(i, 5), st_col(i, 5), AF.Exp,
                          scale=-0.5).then_inc(sems.h["lnA"], 1)
        sems.n["lnA"] += 1

        @block.vector
        def _(v, i=i, pb=pb):
            v.wait_ge(sems.h["lnA"], sems.n["lnA"])
            if i >= 2:
                v.wait_ge(sems.h["tr"], i - 1)
            xap = xt[:, pb * D:(pb + 1) * D]
            xnap = xn[:, pb * D:(pb + 1) * D]
            v.tensor_scalar(xnap, xap, st_col(i, 1), st_col(i, 5),
                            ALU.add, ALU.mult).then_inc(sems.h["ev"], 1)
        sems.n["ev"] += 1

        @block.tensor
        def _(t, i=i, pb=pb):
            t.wait_ge(sems.h["ev"], sems.n["ev"])
            if i == 0:
                sems.w(t, "ld")
            if i >= 1:
                t.wait_ge(sems.h["pe"], i)
            for c in range(NCH):
                ins = t.transpose(ps_tr[:, c * 128:(c + 1) * 128],
                                  xn[:, pb * D + c * 128: pb * D + (c + 1) * 128],
                                  idnt[:, :])
            ins.then_inc(sems.h["tr"], 1)
        sems.n["tr"] += 1

        @block.vector
        def _(v, i=i):
            v.wait_ge(sems.h["tr"], sems.n["tr"])
            dst = xnT.rearrange("p (c t) -> p c t", c=NCH)[
                :, :, i * 128:(i + 1) * 128]
            src = ps_tr.rearrange("p (c t) -> p c t", c=NCH)
            v.tensor_copy(dst, src).then_inc(sems.h["pe"], 1)
        sems.n["pe"] += 1

    def proj(name, w_sb, wld_need, bias_ap, out_sb, out_dram, scale,
             evac_engine):
        it = 0
        for dt in range(8):
            for qc in range(4):
                psu = ps_a if it % 2 == 0 else ps_b
                @block.tensor
                def _(t, dt=dt, qc=qc, it=it, psu=psu):
                    t.wait_ge(sems.h["pe"], min(NT, 4 * qc + 4))
                    if it == 0:
                        t.wait_ge(sems.h["wld"], wld_need)
                        t.wait_ge(sems.h["bc"], 3)
                        t.wait_ge(sems.h["ev"], sems.n["ev"])
                    if it >= 2:
                        t.wait_ge(sems.h["ev"], sems.n["ev"] - 1)
                    for c in range(NCH):
                        ins = t.matmul(psu[:, :],
                                 w_sb[:, c * D + dt * 128: c * D + (dt + 1) * 128],
                                 xnT[:, c * TOK + qc * 512: c * TOK + (qc + 1) * 512],
                                 start=(c == 0), stop=(c == NCH - 1))
                    ins.then_inc(sems.h["mm"], 1)
                sems.n["mm"] += 1

                eng = block.scalar if evac_engine == "scalar" else block.vector
                @eng
                def _(e, dt=dt, qc=qc, it=it, psu=psu):
                    e.wait_ge(sems.h["mm"], sems.n["mm"])
                    dst = out_sb[:, dt * TOK + qc * 512: dt * TOK + (qc + 1) * 512]
                    if evac_engine == "scalar":
                        e.activation(dst, psu[:, :], AF.Identity,
                                     bias=bias_ap(dt), scale=scale
                                     ).then_inc(sems.h["ev"], 1)
                    else:
                        e.tensor_tensor(dst, psu[:, :], bias_ap(dt),
                                        ALU.add).then_inc(sems.h["ev"], 1)
                sems.n["ev"] += 1
                it += 1

    bq8 = T("bq8", [128, 8], F32)
    @block.vector
    def _(v):
        sems.w(v, "ld")
        v.tensor_scalar(bq8[:, :], bqs[:, :], 0.125, None, ALU.mult
                        ).then_inc(sems.h["ev"], 1)
    sems.n["ev"] += 1

    it = 0
    for tt in range(NT):
        for dvh in range(2):
            psu = ps_a if it % 2 == 0 else ps_b
            @block.tensor
            def _(t, tt=tt, dvh=dvh, it=it, psu=psu):
                t.wait_ge(sems.h["pe"], min(NT, tt + 1))
                if it == 0:
                    t.wait_ge(sems.h["wld"], 32)
                    t.wait_ge(sems.h["bc"], 3)
                    t.wait_ge(sems.h["ev"], sems.n["ev"])
                if it >= 2:
                    t.wait_ge(sems.h["ev"], sems.n["ev"] - 1)
                for c in range(NCH):
                    ins = t.matmul(psu[:, :],
                             xnT[:, c * TOK + tt * 128: c * TOK + (tt + 1) * 128],
                             wsb[:, c * D + dvh * 512: c * D + (dvh + 1) * 512],
                             start=(c == 0), stop=(c == NCH - 1))
                ins.then_inc(sems.h["mm"], 1)
            sems.n["mm"] += 1

            @block.vector
            def _(v, tt=tt, dvh=dvh, it=it, psu=psu):
                v.wait_ge(sems.h["mm"], sems.n["mm"])
                if tt >= 2 and dvh == 0:
                    v.wait_ge(sems.h["st" + str(tt % 2)], 16 * (tt // 2))
                dst = vout[:, (tt % 2) * D + dvh * 512:
                           (tt % 2) * D + (dvh + 1) * 512]
                v.tensor_tensor(dst, psu[:, :],
                                bvb[:, dvh * 512:(dvh + 1) * 512],
                                ALU.add).then_inc(sems.h["ev"], 1)
            sems.n["ev"] += 1
            it += 1

        @block.sync
        def _(s, tt=tt):
            s.wait_ge(sems.h["ev"], sems.n["ev"])
            s.dma_start(out=v_o[tt * 128:(tt + 1) * 128, :],
                        in_=vout[:, (tt % 2) * D:(tt % 2 + 1) * D]
                        ).then_inc(sems.h["st" + str(tt % 2)], 16)
        sems.n["st" + str(tt % 2)] += 16
    v_stores = (sems.n["st0"], sems.n["st1"])

    @block.gpsimd
    def _(g, vmm=sems.n["mm"]):
        g.wait_ge(sems.h["mm"], vmm)
        g.dma_start(out=wsb.rearrange("p (c o) -> p c o", c=NCH),
                    in_=wq.ap().rearrange("(c p) o -> p c o", p=128)
                    ).then_inc(sems.h["wld"], 16)
    sems.n["wld"] += 16

    if fused:
        @block.gpsimd
        def _(g):
            g.wait_ge(sems.h["st0"], v_stores[0])
            g.wait_ge(sems.h["st1"], v_stores[1])
            g.collective_compute(
                "AllGather", mybir.AluOpType.bypass,
                replica_groups=PAIRS_RG,
                ins=[io["agvsrc"].ap().opt()],
                outs=[io["agvdst"].ap().opt()],
            ).then_inc(sems.h["bc"], 1)
        sems.n["bc"] += 1

    proj("k", wsb2, 32, lambda dt: bks[:, dt:dt + 1], kT, kT_o, 1.0,
         "scalar")
    k_evacs = sems.n["ev"]

    @block.sync
    def _(s):
        s.wait_ge(sems.h["ev"], k_evacs)
        s.dma_start(out=kT_o[:, :], in_=kT[:, :]).then_inc(sems.h["st"], 16)
    sems.n["st"] += 16
    k_store = sems.n["st"]

    if fused:
        @block.gpsimd
        def _(g):
            g.wait_ge(sems.h["st"], k_store)
            g.collective_compute(
                "AllGather", mybir.AluOpType.bypass,
                replica_groups=PAIRS_RG,
                ins=[io["agksrc"].ap().opt()],
                outs=[io["agkdst"].ap().opt()],
            ).then_inc(sems.h["bc"], 1)
        sems.n["bc"] += 1

    proj("q", wsb, sems.n["wld"], lambda dt: bq8[:, dt:dt + 1], qT, qT_o,
         0.125, "scalar")

    if fused:
        @block.gpsimd
        def _(g):
            g.wait_ge(sems.h["bc"], sems.n["bc"])

    @block.sync
    def _(s):
        s.wait_ge(sems.h["ev"], sems.n["ev"])
        s.dma_start(out=qT_o[:, :], in_=qT[:, :]).then_inc(sems.h["st"], 16)
        sems.n["st"] += 16
        s.wait_ge(sems.h["st"], sems.n["st"])
        s.wait_ge(sems.h["st0"], sems.n["st0"])
        s.wait_ge(sems.h["st1"], sems.n["st1"])


def build_A():
    nc = bass.Bass("TRN2", target_bir_lowering=False)
    io = dram_A(nc)
    with contextlib.ExitStack() as stk:
        io["_sb"], io["_ps"] = alloc_pools(nc, stk)
        block = stk.enter_context(nc.Block())
        emit_A(nc, block, stk, io)
    return nc


def dram_B(nc):
    io = {}
    io["x_in"] = nc.dram_tensor("x_in", [TOK, D], F32, kind="ExternalInput")
    io["qT_i"] = nc.dram_tensor("qT_i", [128, NCH * TOK], BF16, kind="ExternalInput")
    io["kf"] = nc.dram_tensor("kf", [128, NCH * SKEY], BF16, kind="ExternalInput")
    io["vf"] = nc.dram_tensor("vf", [SKEY, D], BF16, kind="ExternalInput")
    io["rs_in"] = nc.dram_tensor("rs", [H0 * 128, RSW], F32, kind="ExternalInput")
    io["rel16"] = nc.dram_tensor("rel16", [H0, 8192], F32, kind="ExternalInput")
    io["gw_in"] = nc.dram_tensor("gw", [128, NCH * 8], F32, kind="ExternalInput")
    io["gb_in"] = nc.dram_tensor("gb", [8, 1], F32, kind="ExternalInput")
    io["ident_in"] = nc.dram_tensor("ident", [128, 128], F32, kind="ExternalInput")
    io["ones_row_in"] = nc.dram_tensor("ones_row", [1, 128], F32, kind="ExternalInput")
    io["ones_col_in"] = nc.dram_tensor("ones_col", [128, 1], F32, kind="ExternalInput")
    io["x1_o"] = nc.dram_tensor("x1_o", [TOK, D], F32, kind="ExternalOutput")
    io["xn1T_o"] = nc.dram_tensor("xn1T_o", [128, NCH * TOK], BF16, kind="ExternalOutput")
    io["cnt_o"] = nc.dram_tensor("cnt_o", [1, 8], F32, kind="ExternalOutput")
    return io


def emit_B(nc, block, stk, io, fused=False):
    nc = bass.Bass("TRN2", target_bir_lowering=False)
    KT = SKEY // 128
    x_in = io["x_in"]
    qT_i = io["qT_i"]
    kf = io["kf"]
    vf = io["vf"]
    rs_in = io["rs_in"]
    rel16 = io["rel16"]
    gw_in = io["gw_in"]
    gb_in = io["gb_in"]
    ident_in = io["ident_in"]
    ones_row_in = io["ones_row_in"]
    ones_col_in = io["ones_col_in"]
    x1_o = io["x1_o"]
    xn1T_o = io["xn1T_o"]
    cnt_o = io["cnt_o"]
    T = lambda nm, shape, dt: io["_sb"].alloc(shape[-1], dt, rows=shape[0])
    PS = lambda nm, shape, dt: io["_ps"].alloc(nm, shape, dt)
    qT = T("qT", [128, NCH * TOK], BF16)
    kvA_k = T("kvA_k", [128, SKEY], BF16)
    kvB_k = T("kvB_k", [128, SKEY], BF16)
    kvA_v = T("kvA_v", [128, KT * 130], BF16)
    kvB_v = T("kvB_v", [128, KT * 130], BF16)
    vraw = T("vraw", [128, KT * 128], BF16)
    rsA = T("rsA", [128, RSW], BF16)
    rsB = T("rsB", [128, RSW], BF16)
    U = T("U", [128, NT * H0 * 66], BF16)
    xn1T = io["_sb"].alloc_at(0, NCH * TOK, BF16)
    ssb = T("ssb", [128, 8 * 512], F32)
    pt = T("pt", [128, 8 * 512], BF16)
    sq = T("sq", [128, 1024], BF16)
    nrm = T("nrm", [1, 16], F32)
    mrow = T("mrow", [1, 16], F32)
    bmaxr = T("bmaxr", [1, 16], F32)
    bm16m = T("bm16m", [16, 1], F32)
    bm16t = T("bm16t", [16, 1], F32)
    negM2 = T("negM2", [128, 2], F32)
    idnt = T("idnt", [128, 128], BF16)
    idnt_f = T("idnt_f", [128, 128], F32)
    onesr = T("onesr", [1, 128], F32)
    onesrb = T("onesrb", [1, 128], BF16)
    onesc = T("onesc", [128, 1], BF16)
    epsc = T("epsc", [128, 1], F32)
    yt = T("yt", [128, D], F32)
    rcol = T("rcol", [128, 16], F32)
    xt = T("xt", [128, 2 * D], F32)
    xn = T("xn", [128, 2 * D], BF16)
    x1t = T("x1t", [128, 2 * D], F32)
    stats = T("stats", [128, 16 * NT], F32)
    gwsb = T("gwsb", [128, NCH * 8], BF16)
    gbsb = T("gbsb", [8, 1], F32)
    gTfull = T("gT", [128, TOK], BF16)
    gT = gTfull[0:8, 0:TOK]
    tb = gTfull[0:128, 0:512]
    tb2 = gTfull[0:128, 512:1024]
    gtt = T("gtt", [128, 8 * NT], F32)
    gsel = T("gsel", [128, 8 * NT], F32)
    gm = T("gm", [128, 4 * NT], F32)
    cnt = T("cnt", [1, 8], F32)
    gscr = T("gscr", [128, 8], F32)
    onesc_f = T("onesc_f", [128, 1], F32)
    ps_S0 = PS("ps_S0", [128, 512], F32)
    ps_S1 = PS("ps_S1", [128, 512], F32)
    ps_o = PS("ps_o", [65, 512], F32)
    ps_y = PS("ps_y", [128, 2 * 4 * 66], BF16)
    ps_n = PS("ps_n", [1, 512], F32)
    ps_b = PS("ps_b", [128, 8], F32)
    ps_tr = PS("ps_tr", [128, D], BF16)
    ps_g = PS("ps_g", [8, 512], F32)
    sH = [[io["_ps"].view_at(0, [128, 512], F32),
           io["_ps"].view_at(1, [128, 512], F32)],
          [io["_ps"].view_at(6, [128, 512], F32),
           io["_ps"].view_at(7, [128, 512], F32)]]
    ps_o2 = io["_ps"].view_at(5, [65, 512], F32)
    ps_oH = [ps_o, ps_o2]

    sems = io.get("_sems") or Sems(nc, stk, [
        "ld", "ld2", "qld", "kv0", "kv1", "kvr", "rs0", "rs1", "bmv",
        "mm", "bias", "exp", "evO", "trY", "uev", "mh",
        "xd0", "xd1", "lnV", "lnA", "vch", "ach", "tr", "pe",
        "st", "st0", "st1", "gmm", "gev", "bmx", "ev"], prefix="B_")

    @block.gpsimd
    def _(g):
        g.dma_start(out=idnt[:, :], in_=ident_in[:, :]).then_inc(sems.h["ld"], 16)
        g.dma_start(out=onesrb[:, :], in_=ones_row_in[:, :]).then_inc(sems.h["ld"], 16)
        g.dma_start(out=onesc[:, :], in_=ones_col_in[:, :]).then_inc(sems.h["ld"], 16)
        g.dma_start(out=gwsb[:, :], in_=gw_in[:, :]).then_inc(sems.h["ld"], 16)
    sems.n["ld"] += 64

    @block.sync
    def _(s):
        s.dma_start(out=idnt_f[:, :], in_=ident_in[:, :]).then_inc(sems.h["ld2"], 16)
        s.dma_start(out=onesr[:, :], in_=ones_row_in[:, :]).then_inc(sems.h["ld2"], 16)
        s.dma_start(out=gbsb[:, :], in_=gb_in[:, :]).then_inc(sems.h["ld2"], 16)
        s.dma_start(out=onesc_f[:, :], in_=ones_col_in[:, :]).then_inc(sems.h["ld2"], 16)
        s.dma_start(out=qT[:, :], in_=qT_i[:, :]).then_inc(sems.h["qld"], 16)
    sems.n["ld2"] += 64
    sems.n["qld"] += 16

    @block.vector
    def _(v):
        v.memset(epsc[:, :], EPS)
        for buf in (kvA_v, kvB_v):
            v.memset(buf.rearrange("p (k h c) -> p k h c", k=KT, h=2)[
                :, :, :, 64:65], 1.0)

    @block.gpsimd
    def _(g):
        g.dma_start(out=gTfull[0:16, 0:2048], in_=rel16[:, 0:2048]
                    ).then_inc(sems.h["bmx"], 16)
    sems.n["bmx"] += 16

    for j in range(4):
        @block.vector
        def _(v, j=j):
            v.wait_ge(sems.h["bmx"], sems.n["bmx"])
            op = v.tensor_reduce(bm16m[:, :] if j == 0 else bm16t[:, :],
                                 gTfull[0:16, 0:2048], axis=AX.X, op=ALU.max,
                                 apply_absolute_value=True)
            if j > 0:
                sems.ch(op, v, "vch")
                op = v.tensor_tensor(bm16m[:, :], bm16m[:, :],
                                     bm16t[:, :], ALU.max)
            op.then_inc(sems.h["bmv"], 1)
        sems.n["bmv"] += 1
        if j < 3:
            @block.gpsimd
            def _(g, j=j):
                g.wait_ge(sems.h["bmv"], sems.n["bmv"])
                g.dma_start(out=gTfull[0:16, 0:2048],
                            in_=rel16[:, (j + 1) * 2048:(j + 2) * 2048]
                            ).then_inc(sems.h["bmx"], 16)
            sems.n["bmx"] += 16

    @block.tensor
    def _(t):
        sems.w(t, "bmx")
        sems.w(t, "bmv")
        sems.w(t, "ld")
        sems.w(t, "ld2")
        t.matmul(ps_n[0:1, 0:16], bm16m[0:16, 0:1], idnt_f[0:16, 0:16],
                 start=True, stop=True).then_inc(sems.h["bmv"], 1)
    sems.n["bmv"] += 1

    @block.vector
    def _(v):
        sems.w(v, "bmv")
        v.tensor_copy(bmaxr[:, :], ps_n[0:1, 0:16]).then_inc(sems.h["bmv"], 1)
    sems.n["bmv"] += 1
    bmx_done = sems.n["bmv"]

    def kv_load(grp):
        par = grp % 2
        kbuf = kvA_k if par == 0 else kvB_k
        sem = "kv" + str(par)

        @block.sync
        def _(s, grp=grp, kbuf=kbuf, sem=sem, par=par):
            if grp >= 2:
                s.wait_ge(sems.h["mm"], kv_war[par])
            if grp >= 1:
                o = "kv" + str(1 - par)
                s.wait_ge(sems.h[o], sems.n[o])
            if fused:
                for m_ in range(2):
                    s.dma_start(out=kbuf[:, m_ * TOK:(m_ + 1) * TOK],
                                in_=kf[m_ * 128:(m_ + 1) * 128,
                                       grp * TOK:(grp + 1) * TOK]
                                ).then_inc(sems.h[sem], 16)
            else:
                s.dma_start(out=kbuf[:, :],
                            in_=kf[:, grp * SKEY:(grp + 1) * SKEY]
                            ).then_inc(sems.h[sem], 16)
            s.dma_start(
                out=vraw.rearrange("p (k c) -> p k c", k=KT),
                in_=vf.ap().rearrange("(k p) d -> p k d", p=128)[
                    :, :, grp * 128:(grp + 1) * 128],
                ).then_inc(sems.h[sem], 16)
        sems.n[sem] += 48 if fused else 32

    def kv_restripe(grp):
        par = grp % 2
        vbuf = kvA_v if par == 0 else kvB_v
        sem = "kv" + str(par)

        @block.vector
        def _(v, grp=grp, vbuf=vbuf, sem=sem, par=par):
            v.wait_ge(sems.h[sem], sems.n[sem])
            if grp >= 2:
                v.wait_ge(sems.h["mm"], kv_war[par])
            ins = v.tensor_copy(
                vbuf.rearrange("p (k h c) -> p k h c", k=KT, h=2)[
                    :, :, :, 0:64],
                vraw.rearrange("p (k h c) -> p k h c", k=KT, h=2))
            ins.then_inc(sems.h[sem], 1)
        sems.n[sem] += 1

    kv_war = [0, 0]
    rs_war = [0, 0]
    x1_done = []

    def rs_load(h):
        par = h % 2
        buf = rsA if par == 0 else rsB
        sem = "rs" + str(par)

        @block.gpsimd
        def _(g, h=h, buf=buf, sem=sem, par=par):
            if h >= 2:
                g.wait_ge(sems.h["bias"], rs_war[par])
            g.dma_start(out=buf[:, :],
                        in_=rs_in[h * 128:(h + 1) * 128, :]
                        ).then_inc(sems.h[sem], 16)
        sems.n[sem] += 16

    kv_load(0)
    kv_restripe(0)
    kv_load(1)
    rs_load(0)
    rs_load(1)

    ep_pend = None
    last_try = {0: 0, 1: 0}
    last_uev = {0: 0, 1: 0}
    tbH = [tb, tb2]

    for grp in range(8):
        par = grp % 2
        kbuf = kvA_k if par == 0 else kvB_k
        vbuf = kvA_v if par == 0 else kvB_v
        if grp >= 1:
            kv_restripe(grp)
            if grp + 1 < 8:
                kv_load(grp + 1)
        negm_ready = {}
        for h_loc in range(2):
            h = grp * 2 + h_loc
            hb = h_loc * 64

            segs = [(0, kbuf[hb:hb + 64, i * 1024:(i + 1) * 1024],
                     2, 2 * i) for i in range(4)]
            segs += [(64, qT[hb:hb + 64,
                          (h // 2) * TOK + i * 1024:
                          (h // 2) * TOK + (i + 1) * 1024], 2, 8 + 2 * i)
                     for i in range(2)]
            for (rb, srcap0, ncks, ncol) in segs:
                @block.scalar
                def _(sc, h=h, rb=rb, srcap0=srcap0):
                    sc.wait_ge(sems.h["kv" + str(grp % 2)],
                               sems.n["kv" + str(grp % 2)])
                    sc.wait_ge(sems.h["qld"], 16)
                    sc.wait_ge(sems.h["bmv"], bmx_done)
                    sc.wait_ge(sems.h["mh"], sems.n["mh"])
                    sc.activation(sq[rb:rb + 64, 0:1024], srcap0,
                                  AF.Square).then_inc(sems.h["mh"], 1)
                sems.n["mh"] += 1

                for j in range(ncks):
                    @block.tensor
                    def _(t, j=j, rb=rb):
                        t.wait_ge(sems.h["mh"], sems.n["mh"])
                        t.wait_ge(sems.h["ld"], 64)
                        t.matmul(ps_n[0:1, 0:512], onesc[rb:rb + 64, 0:1],
                                 sq[rb:rb + 64, j * 512:(j + 1) * 512],
                                 start=True, stop=True,
                                 tile_position=(rb, 0)
                                 ).then_inc(sems.h["mm"], 1)
                    sems.n["mm"] += 1

                    @block.vector
                    def _(v, j=j, ncol=ncol):
                        v.wait_ge(sems.h["mm"], sems.n["mm"])
                        ins = v.tensor_reduce(nrm[0:1, ncol + j:ncol + j + 1],
                                              ps_n[0:1, 0:512],
                                              axis=AX.X, op=ALU.max)
                        ins.then_inc(sems.h["mh"], 1)
                    sems.n["mh"] += 1

            @block.vector
            def _(v, h=h):
                v.wait_ge(sems.h["mh"], sems.n["mh"])
                ins = v.tensor_reduce(mrow[0:1, 0:1], nrm[0:1, 0:8],
                                      axis=AX.X, op=ALU.max)
                sems.ch(ins, v, "vch")
                ins = v.tensor_reduce(mrow[0:1, 1:2],
                                      nrm[0:1, 8:12],
                                      axis=AX.X, op=ALU.max)
                sems.ch(ins, v, "vch")
                ins = v.tensor_tensor(mrow[0:1, 2:3], mrow[0:1, 0:1],
                                      mrow[0:1, 1:2], ALU.mult)
                ins.then_inc(sems.h["lnV"], 1)
            sems.n["lnV"] += 1

            @block.scalar
            def _(sc, h=h):
                sc.wait_ge(sems.h["lnV"], sems.n["lnV"])
                ins = sc.activation(mrow[0:1, 3:4], mrow[0:1, 2:3], AF.Ln)
                sems.ch(ins, sc, "ach")
                sc.activation(mrow[0:1, 4:5], mrow[0:1, 3:4], AF.Exp,
                              scale=0.5).then_inc(sems.h["lnA"], 1)
            sems.n["lnA"] += 1

            @block.vector
            def _(v, h=h):
                v.wait_ge(sems.h["lnA"], sems.n["lnA"])
                v.wait_ge(sems.h["bmv"], bmx_done)
                ins = v.tensor_scalar(mrow[0:1, 5:6], mrow[0:1, 4:5], 0.125,
                                      1.0, ALU.mult, ALU.add)
                sems.ch(ins, v, "vch")
                ins = v.tensor_tensor(mrow[0:1, 6:7], mrow[0:1, 5:6],
                                      bmaxr[0:1, h:h + 1], ALU.add)
                sems.ch(ins, v, "vch")
                ins = v.tensor_scalar(mrow[0:1, 7:8], mrow[0:1, 6:7], -1.0,
                                      None, ALU.mult)
                ins.then_inc(sems.h["lnV"], 1)
            sems.n["lnV"] += 1

            @block.tensor
            def _(t, h=h):
                t.wait_ge(sems.h["lnV"], sems.n["lnV"])
                t.wait_ge(sems.h["mh"], sems.n["mh"])
                t.wait_ge(sems.h["exp"], sems.n["exp"])
                t.wait_ge(sems.h["evO"], sems.n["evO"])
                t.wait_ge(sems.h["ld2"], 64)
                t.matmul(ps_b[:, 0:1], onesr[0:1, :], mrow[0:1, 7:8],
                         start=True, stop=True).then_inc(sems.h["mm"], 1)
            sems.n["mm"] += 1

            @block.vector
            def _(v, h=h, h_loc=h_loc):
                v.wait_ge(sems.h["mm"], sems.n["mm"])
                v.tensor_copy(negM2[:, h_loc:h_loc + 1], ps_b[:, 0:1]
                              ).then_inc(sems.h["mh"], 1)
            sems.n["mh"] += 1
            negm_ready[h_loc] = sems.n["mh"]

        exp_q = []
        for qc in range(4):
            biasb = sems.n["bias"]
            expb = sems.n["exp"]
            bias_after = {(j, hh): biasb + 2 * j + hh + 1
                          for j in range(KT) for hh in range(2)}
            exp_after = {(j, hh): expb + 2 * j + hh + 1
                         for j in range(KT) for hh in range(2)}
            mmc = sems.n["mm"]
            qk_after = {}
            pv_after = {}
            for kt in range(KT + 1):
                if kt < KT:
                    for hh in range(2):
                        mmc += 1
                        qk_after[(kt, hh)] = mmc
                if kt >= 1:
                    for hh in range(2):
                        mmc += 1
                        pv_after[(kt - 1, hh)] = mmc
            pend = ep_pend
            if pend is not None:
                pend["evo"] = {0: sems.n["evO"] + 1, 1: sems.n["evO"] + 2}
                trb = sems.n["trY"]
                pend["try"] = {0: trb + 1, 1: trb + 2}
                pend["uev"] = {0: sems.n["uev"] + 1, 1: sems.n["uev"] + 2}
                pend["tb_free"] = dict(last_try)
                pend["psy_free"] = dict(last_uev)
                last_try.update(pend["try"])
                last_uev.update(pend["uev"])

            @block.vector
            def _(v, qc=qc, grp=grp, pend=pend, bias_after=bias_after,
                  exp_after=exp_after, qk_after=qk_after):
                if pend is not None:
                    for hh in range(2):
                        v.wait_ge(sems.h["mm"], pend["pv31"][hh])
                        v.wait_ge(sems.h["trY"], pend["tb_free"][hh])
                        v.tensor_copy(tbH[hh][0:65, :], ps_oH[hh][:, :]
                                      ).then_inc(sems.h["evO"], 1)
                        sems.n["evO"] += 1
                        assert sems.n["evO"] == pend["evo"][hh]
                for j in range(KT):
                    for hh in range(2):
                        v.wait_ge(sems.h["mm"], qk_after[(j, hh)])
                        if j == 0:
                            v.wait_ge(sems.h["rs" + str(hh)],
                                      sems.n["rs" + str(hh)])
                        if j >= 4:
                            v.wait_ge(sems.h["exp"], exp_after[(j - 4, hh)])
                        c0 = 3968 + qc * 512 - j * 128
                        rsbuf_ = rsA if hh == 0 else rsB
                        sl = 4 * hh + j % 4
                        v.tensor_tensor(ssb[:, sl * 512:(sl + 1) * 512],
                                        sH[hh][j % 2][:, :],
                                        rsbuf_[:, c0:c0 + 512],
                                        ALU.add).then_inc(sems.h["bias"], 1)
                        sems.n["bias"] += 1
                        assert sems.n["bias"] == bias_after[(j, hh)]
                    if pend is not None and j in (2, 6):
                        hh2 = 0 if j == 2 else 1
                        v.wait_ge(sems.h["trY"], pend["try"][hh2])
                        dst = U.rearrange("p (t h s) -> p t h s",
                                          t=NT, h=H0)[
                            :, 4 * pend["qc"]: 4 * pend["qc"] + 4,
                            pend["h"][hh2], 0:65]
                        srcp = ps_y.rearrange("p (g j s) -> p g j s",
                                              g=2, j=4)[:, hh2, :, 0:65]
                        v.tensor_copy(dst, srcp).then_inc(sems.h["uev"], 1)
                        sems.n["uev"] += 1
                        assert sems.n["uev"] == pend["uev"][hh2]

            if pend is not None:
                @block.tensor
                def _(t, pend=pend):
                    for hh in range(2):
                        t.wait_ge(sems.h["evO"], pend["evo"][hh])
                        t.wait_ge(sems.h["uev"], pend["psy_free"][hh])
                        t.wait_ge(sems.h["ld"], 64)
                        t.wait_ge(sems.h["ld2"], 64)
                        for jj in range(4):
                            ins = t.transpose(
                                ps_y[:, hh * 264 + jj * 66:
                                     hh * 264 + jj * 66 + 65],
                                tbH[hh][0:65, jj * 128:(jj + 1) * 128],
                                idnt[0:65, 0:65])
                        ins.then_inc(sems.h["trY"], 1)
                        sems.n["trY"] += 1
                        assert sems.n["trY"] == pend["try"][hh]

            evo_before = sems.n["evO"]

            @block.tensor
            def _(t, qc=qc, kbuf=kbuf, vbuf=vbuf, grp=grp, pend=pend,
                  evo_before=evo_before, bias_after=bias_after,
                  exp_after=exp_after, qk_after=qk_after, pv_after=pv_after):
                for kt in range(KT + 1):
                    if kt < KT:
                        if kt == 0:
                            t.wait_ge(sems.h["kv" + str(grp % 2)],
                                      sems.n["kv" + str(grp % 2)])
                        for hh in range(2):
                            if kt >= 2:
                                t.wait_ge(sems.h["bias"],
                                          bias_after[(kt - 2, hh)])
                            hb = hh * 64
                            h = grp * 2 + hh
                            t.matmul(sH[hh][kt % 2][:, :],
                                     kbuf[hb:hb + 64,
                                          kt * 128:(kt + 1) * 128],
                                     qT[hb:hb + 64,
                                        (h // 2) * TOK + qc * 512:
                                        (h // 2) * TOK + (qc + 1) * 512],
                                     start=True, stop=True,
                                     tile_position=(hb, 0)
                                     ).then_inc(sems.h["mm"], 1)
                            sems.n["mm"] += 1
                            assert sems.n["mm"] == qk_after[(kt, hh)]
                    if kt >= 1:
                        j = kt - 1
                        for hh in range(2):
                            t.wait_ge(sems.h["exp"], exp_after[(j, hh)])
                            if j == 0 and hh == 0:
                                t.wait_ge(sems.h["evO"], evo_before)
                            t.matmul(ps_oH[hh][:, :],
                                     vbuf[:, j * 130 + hh * 65:
                                          j * 130 + hh * 65 + 65],
                                     pt[:, (4 * hh + j % 4) * 512:
                                        (4 * hh + j % 4 + 1) * 512],
                                     start=(j == 0), stop=(j == KT - 1)
                                     ).then_inc(sems.h["mm"], 1)
                            sems.n["mm"] += 1
                            assert sems.n["mm"] == pv_after[(j, hh)]

            @block.scalar
            def _(sc, qc=qc, bias_after=bias_after, exp_after=exp_after,
                  pv_after=pv_after, negm_ready=negm_ready):
                for j in range(KT):
                    for hh in range(2):
                        sc.wait_ge(sems.h["bias"], bias_after[(j, hh)])
                        if j == 0 and hh == 0:
                            sc.wait_ge(sems.h["mh"], negm_ready[1])
                        if j >= 4:
                            sc.wait_ge(sems.h["mm"], pv_after[(j - 4, hh)])
                        sl = 4 * hh + j % 4
                        sc.activation(pt[:, sl * 512:(sl + 1) * 512],
                                      ssb[:, sl * 512:(sl + 1) * 512],
                                      AF.Exp, bias=negM2[:, hh:hh + 1]
                                      ).then_inc(sems.h["exp"], 1)
                        sems.n["exp"] += 1
                        assert sems.n["exp"] == exp_after[(j, hh)]

            ep_pend = {"qc": qc, "grp": grp,
                       "h": {0: grp * 2, 1: grp * 2 + 1},
                       "pv31": {hh: pv_after[(KT - 1, hh)]
                                for hh in range(2)}}

        pend = ep_pend
        ep_pend = None
        pend["evo"] = {0: sems.n["evO"] + 1, 1: sems.n["evO"] + 2}
        trb = sems.n["trY"]
        pend["try"] = {0: trb + 1, 1: trb + 2}
        pend["uev"] = {0: sems.n["uev"] + 1, 1: sems.n["uev"] + 2}
        pend["tb_free"] = dict(last_try)
        pend["psy_free"] = dict(last_uev)
        last_try.update(pend["try"])
        last_uev.update(pend["uev"])

        @block.vector
        def _(v, pend=pend):
            for hh in range(2):
                v.wait_ge(sems.h["mm"], pend["pv31"][hh])
                v.wait_ge(sems.h["trY"], pend["tb_free"][hh])
                v.tensor_copy(tbH[hh][0:65, :], ps_oH[hh][:, :]
                              ).then_inc(sems.h["evO"], 1)
                sems.n["evO"] += 1
                assert sems.n["evO"] == pend["evo"][hh]

        @block.tensor
        def _(t, pend=pend):
            for hh in range(2):
                t.wait_ge(sems.h["evO"], pend["evo"][hh])
                t.wait_ge(sems.h["uev"], pend["psy_free"][hh])
                t.wait_ge(sems.h["ld"], 64)
                t.wait_ge(sems.h["ld2"], 64)
                for jj in range(4):
                    ins = t.transpose(
                        ps_y[:, hh * 264 + jj * 66:hh * 264 + jj * 66 + 65],
                        tbH[hh][0:65, jj * 128:(jj + 1) * 128],
                        idnt[0:65, 0:65])
                ins.then_inc(sems.h["trY"], 1)
                sems.n["trY"] += 1
                assert sems.n["trY"] == pend["try"][hh]

        @block.vector
        def _(v, pend=pend):
            for hh in range(2):
                v.wait_ge(sems.h["trY"], pend["try"][hh])
                dst = U.rearrange("p (t h s) -> p t h s", t=NT, h=H0)[
                    :, 4 * pend["qc"]: 4 * pend["qc"] + 4,
                    pend["h"][hh], 0:65]
                srcp = ps_y.rearrange("p (g j s) -> p g j s",
                                      g=2, j=4)[:, hh, :, 0:65]
                v.tensor_copy(dst, srcp).then_inc(sems.h["uev"], 1)
                sems.n["uev"] += 1
                assert sems.n["uev"] == pend["uev"][hh]
        rs_war[0] = sems.n["bias"]
        rs_war[1] = sems.n["bias"]
        if grp + 1 < 8:
            rs_load(2 * grp + 2)
            rs_load(2 * grp + 3)
        kv_war[grp % 2] = sems.n["mm"]


    def st_col(i, j):
        return stats[:, 16 * i + j: 16 * i + j + 1]

    uev_total = sems.n["uev"]

    for tt in range(NT):
        par = tt % 2
        sp = str(par)

        @block.sync
        def _(s, tt=tt, par=par, sp=sp):
            if tt >= 2:
                s.wait_ge(sems.h["ev"], x1_done[tt - 2])
            s.dma_start(out=xt[:, par * D:(par + 1) * D],
                        in_=x_in[tt * 128:(tt + 1) * 128, :]
                        ).then_inc(sems.h["xd" + sp], 16)
        sems.n["xd" + sp] += 16

        @block.vector
        def _(v, tt=tt, par=par):
            if tt == 0:
                v.wait_ge(sems.h["uev"], uev_total)
            else:
                v.wait_ge(sems.h["lnA"], sems.n["lnA"])
            uv = U.rearrange("p (t h s) -> p t h s", t=NT, h=H0)
            ins = v.reciprocal(rcol[:, 0:16], uv[:, tt, :, 64:65])
            sems.ch(ins, v, "vch")
            for h in range(H0):
                ins = v.tensor_scalar(yt[:, h * 64:(h + 1) * 64],
                                      uv[:, tt, h, 0:64],
                                      rcol[:, h:h + 1], None, ALU.mult)
            sems.ch(ins, v, "vch")
            ins = v.reduce_sum(st_col(tt, 0), yt[:, :], axis=AX.X)
            sems.ch(ins, v, "vch")
            ins = v.tensor_scalar(st_col(tt, 1), st_col(tt, 0), -1.0 / D,
                                  None, ALU.mult)
            ins.then_inc(sems.h["lnV"], 1)
            v.wait_ge(sems.h["lnV"], sems.n["lnV"] + 1)
        sems.n["lnV"] += 1
        yt_ready = sems.n["lnV"]

        @block.scalar
        def _(sc, tt=tt, par=par):
            sc.wait_ge(sems.h["lnV"], yt_ready)
            if tt >= 2:
                sc.wait_ge(sems.h["tr"], sems.n["tr"])
            sc.activation(xn[:, par * D:(par + 1) * D], yt[:, :], AF.Square,
                          accum_out=st_col(tt, 2)).then_inc(sems.h["lnA"], 1)
        sems.n["lnA"] += 1

        @block.vector
        def _(v, tt=tt):
            v.wait_ge(sems.h["lnA"], sems.n["lnA"])
            v.tensor_tensor(st_col(tt, 3), st_col(tt, 1), st_col(tt, 1), ALU.mult)
            ins = v.tensor_scalar(st_col(tt, 4), st_col(tt, 2), 1.0 / D,
                                  None, ALU.mult)
            sems.ch(ins, v, "vch")
            v.tensor_tensor(st_col(tt, 4), st_col(tt, 4), st_col(tt, 3),
                            ALU.subtract).then_inc(sems.h["lnV"], 1)
        sems.n["lnV"] += 1

        @block.scalar
        def _(sc, tt=tt):
            sc.wait_ge(sems.h["lnV"], sems.n["lnV"])
            ins = sc.activation(st_col(tt, 5), st_col(tt, 4), AF.Ln,
                                bias=epsc[:, 0:1])
            sems.ch(ins, sc, "ach")
            sc.activation(st_col(tt, 5), st_col(tt, 5), AF.Exp,
                          scale=-0.5).then_inc(sems.h["lnA"], 1)
        sems.n["lnA"] += 1

        @block.vector
        def _(v, tt=tt, par=par, sp=sp):
            v.wait_ge(sems.h["lnA"], sems.n["lnA"])
            v.wait_ge(sems.h["xd" + sp], 16 * (tt // 2 + 1))
            if tt >= 2:
                v.wait_ge(sems.h["st" + sp], 16 * (tt // 2))
            ins = v.tensor_scalar(x1t[:, par * D:(par + 1) * D], yt[:, :],
                                  st_col(tt, 1), st_col(tt, 5),
                                  ALU.add, ALU.mult)
            sems.ch(ins, v, "vch")
            ins = v.tensor_tensor(x1t[:, par * D:(par + 1) * D],
                                  x1t[:, par * D:(par + 1) * D],
                                  xt[:, par * D:(par + 1) * D], ALU.add)
            ins.then_inc(sems.h["ev"], 1)
            v.wait_ge(sems.h["ev"], sems.n["ev"] + 1)
            ins = v.reduce_sum(st_col(tt, 8), x1t[:, par * D:(par + 1) * D],
                               axis=AX.X)
            sems.ch(ins, v, "vch")
            ins = v.tensor_scalar(st_col(tt, 9), st_col(tt, 8), -1.0 / D,
                                  None, ALU.mult)
            ins.then_inc(sems.h["lnV"], 1)
            v.wait_ge(sems.h["lnV"], sems.n["lnV"] + 1)
        sems.n["ev"] += 1
        sems.n["lnV"] += 1
        x1_done.append(0)
        x1_done[tt] = sems.n["ev"]

        @block.sync
        def _(s, tt=tt, par=par, sp=sp):
            s.wait_ge(sems.h["ev"], x1_done[tt])
            s.dma_start(out=x1_o[tt * 128:(tt + 1) * 128, :],
                        in_=x1t[:, par * D:(par + 1) * D]
                        ).then_inc(sems.h["st" + sp], 16)
        sems.n["st" + sp] += 16

        @block.scalar
        def _(sc, tt=tt, par=par):
            sc.wait_ge(sems.h["ev"], x1_done[tt])
            if tt >= 2:
                sc.wait_ge(sems.h["tr"], sems.n["tr"])
            sc.activation(xn[:, par * D:(par + 1) * D],
                          x1t[:, par * D:(par + 1) * D], AF.Square,
                          accum_out=st_col(tt, 10)).then_inc(sems.h["lnA"], 1)
        sems.n["lnA"] += 1

        @block.vector
        def _(v, tt=tt):
            v.wait_ge(sems.h["lnA"], sems.n["lnA"])
            v.tensor_tensor(st_col(tt, 11), st_col(tt, 9), st_col(tt, 9),
                            ALU.mult)
            ins = v.tensor_scalar(st_col(tt, 12), st_col(tt, 10), 1.0 / D,
                                  None, ALU.mult)
            sems.ch(ins, v, "vch")
            v.tensor_tensor(st_col(tt, 12), st_col(tt, 12), st_col(tt, 11),
                            ALU.subtract).then_inc(sems.h["lnV"], 1)
        sems.n["lnV"] += 1

        @block.scalar
        def _(sc, tt=tt):
            sc.wait_ge(sems.h["lnV"], sems.n["lnV"])
            ins = sc.activation(st_col(tt, 13), st_col(tt, 12), AF.Ln,
                                bias=epsc[:, 0:1])
            sems.ch(ins, sc, "ach")
            sc.activation(st_col(tt, 13), st_col(tt, 13), AF.Exp,
                          scale=-0.5).then_inc(sems.h["lnA"], 1)
        sems.n["lnA"] += 1

        @block.vector
        def _(v, tt=tt, par=par):
            v.wait_ge(sems.h["lnA"], sems.n["lnA"])
            if tt >= 2:
                v.wait_ge(sems.h["tr"], sems.n["tr"])
            v.tensor_scalar(xn[:, par * D:(par + 1) * D],
                            x1t[:, par * D:(par + 1) * D],
                            st_col(tt, 9), st_col(tt, 13),
                            ALU.add, ALU.mult).then_inc(sems.h["ev"], 1)
        sems.n["ev"] += 1

        @block.tensor
        def _(t, tt=tt, par=par):
            t.wait_ge(sems.h["ev"], sems.n["ev"])
            if tt >= 1:
                t.wait_ge(sems.h["pe"], sems.n["pe"])
            for c in range(NCH):
                ins = t.transpose(ps_tr[:, c * 128:(c + 1) * 128],
                                  xn[:, par * D + c * 128:
                                     par * D + (c + 1) * 128],
                                  idnt[:, :])
            ins.then_inc(sems.h["tr"], 1)
        sems.n["tr"] += 1

        @block.vector
        def _(v, tt=tt):
            v.wait_ge(sems.h["tr"], sems.n["tr"])
            dst = xn1T.rearrange("p (c t) -> p c t", c=NCH)[
                :, :, tt * 128:(tt + 1) * 128]
            srcp = ps_tr.rearrange("p (c t) -> p c t", c=NCH)
            v.tensor_copy(dst, srcp).then_inc(sems.h["pe"], 1)
        sems.n["pe"] += 1

    gate_ev = []
    for qc in range(4):
        @block.tensor
        def _(t, qc=qc):
            t.wait_ge(sems.h["pe"], sems.n["pe"])
            t.wait_ge(sems.h["ld"], 64)
            t.wait_ge(sems.h["ld2"], 64)
            if qc >= 1:
                t.wait_ge(sems.h["gev"], sems.n["gev"])
            for c in range(NCH):
                ins = t.matmul(ps_g[:, :],
                               gwsb[:, c * 8:(c + 1) * 8],
                               xn1T[:, c * TOK + qc * 512:
                                    c * TOK + (qc + 1) * 512],
                               start=(c == 0), stop=(c == NCH - 1))
            ins.then_inc(sems.h["gmm"], 1)
        sems.n["gmm"] += 1

        @block.scalar
        def _(sc, qc=qc):
            sc.wait_ge(sems.h["gmm"], sems.n["gmm"])
            sc.activation(gT[:, qc * 512:(qc + 1) * 512], ps_g[:, :],
                          AF.Identity, bias=gbsb[:, 0:1]
                          ).then_inc(sems.h["gev"], 1)
        sems.n["gev"] += 1

    for tt in range(NT):
        @block.tensor
        def _(t, tt=tt):
            t.wait_ge(sems.h["gev"], sems.n["gev"])
            t.wait_ge(sems.h["uev"], sems.n["uev"])
            ins = t.transpose(ps_y[:, 0:8], gT[0:8, tt * 128:(tt + 1) * 128],
                              idnt[0:8, 0:8])
            ins.then_inc(sems.h["gmm"], 1)
        sems.n["gmm"] += 1

        @block.vector
        def _(v, tt=tt):
            v.wait_ge(sems.h["gmm"], sems.n["gmm"])
            v.tensor_copy(gtt[:, tt * 8:(tt + 1) * 8], ps_y[:, 0:8]
                          ).then_inc(sems.h["gev"], 1)
        sems.n["gev"] += 1

    for tt in range(NT):
        @block.vector
        def _(v, tt=tt):
            gt_ = gtt[:, tt * 8:(tt + 1) * 8]
            gs_ = gsel[:, tt * 8:(tt + 1) * 8]
            m1 = gm[:, tt * 4: tt * 4 + 1]
            m2 = gm[:, tt * 4 + 1: tt * 4 + 2]
            ins = v.tensor_reduce(m1, gt_, axis=AX.X, op=ALU.max)
            sems.ch(ins, v, "vch")
            ins = v.tensor_scalar(gs_, gt_, m1, None, ALU.is_ge)
            sems.ch(ins, v, "vch")
            ins = v.scalar_tensor_tensor(gscr[:, :], gs_, -1e30, gt_,
                                         ALU.mult, ALU.add)
            sems.ch(ins, v, "vch")
            ins = v.tensor_reduce(m2, gscr[:, :], axis=AX.X, op=ALU.max)
            sems.ch(ins, v, "vch")
            v.tensor_scalar(gs_, gt_, m2, None, ALU.is_ge
                            ).then_inc(sems.h["gev"], 1)
        sems.n["gev"] += 1

    @block.tensor
    def _(t):
        t.wait_ge(sems.h["gev"], sems.n["gev"])
        for tt in range(NT):
            ins = t.matmul(ps_n[0:1, 0:8], onesc_f[:, :],
                           gsel[:, tt * 8:(tt + 1) * 8],
                           start=(tt == 0), stop=(tt == NT - 1))
        ins.then_inc(sems.h["gmm"], 1)
    sems.n["gmm"] += 1

    @block.vector
    def _(v):
        v.wait_ge(sems.h["gmm"], sems.n["gmm"])
        v.tensor_copy(cnt[:, :], ps_n[0:1, 0:8]).then_inc(sems.h["gev"], 1)
    sems.n["gev"] += 1

    @block.sync
    def _(s):
        s.wait_ge(sems.h["pe"], sems.n["pe"])
        s.dma_start(out=xn1T_o[:, :], in_=xn1T[:, :]).then_inc(sems.h["st"], 16)
        sems.n["st"] += 16
        s.wait_ge(sems.h["gev"], sems.n["gev"])
        s.dma_start(out=cnt_o[:, :], in_=cnt[:, :]).then_inc(sems.h["st"], 16)
        sems.n["st"] += 16
        s.wait_ge(sems.h["st"], sems.n["st"])
        s.wait_ge(sems.h["st0"], sems.n["st0"])
        s.wait_ge(sems.h["st1"], sems.n["st1"])


def build_B():
    nc = bass.Bass("TRN2", target_bir_lowering=False)
    io = dram_B(nc)
    with contextlib.ExitStack() as stk:
        io["_sb"], io["_ps"] = alloc_pools(nc, stk)
        block = stk.enter_context(nc.Block())
        emit_B(nc, block, stk, io)
    return nc


def dram_C(nc):
    io = {}
    io["x1_in"] = nc.dram_tensor("x1_in", [TOK, D], F32, kind="ExternalInput")
    io["xn1T_i"] = nc.dram_tensor("xn1T_i", [128, NCH * TOK], BF16, kind="ExternalInput")
    io["cnt8"] = nc.dram_tensor("cnt8", [8, 8], F32, kind="ExternalInput")
    io["ew_in"] = nc.dram_tensor("ew", [8 * D, D], F32, kind="ExternalInput")
    io["be_in"] = nc.dram_tensor("be", [128, NCH * 8], F32, kind="ExternalInput")
    io["wq_in"] = nc.dram_tensor("a1Wq", [D, 256], F32, kind="ExternalInput")
    io["wqc_in"] = nc.dram_tensor("a1Wqc", [D, 256], F32, kind="ExternalInput")
    io["wk_in"] = nc.dram_tensor("a1Wk", [D, 256], F32, kind="ExternalInput")
    io["wkc_in"] = nc.dram_tensor("a1Wkc", [D, 256], F32, kind="ExternalInput")
    io["wv_in"] = nc.dram_tensor("a1Wv", [D, D], F32, kind="ExternalInput")
    io["wvc_in"] = nc.dram_tensor("a1Wvc", [D, D], F32, kind="ExternalInput")
    io["bq_in"] = nc.dram_tensor("a1bq", [128, 2], F32, kind="ExternalInput")
    io["bqc_in"] = nc.dram_tensor("a1bqc", [128, 2], F32, kind="ExternalInput")
    io["bk_in"] = nc.dram_tensor("a1bk", [128, 2], F32, kind="ExternalInput")
    io["bkc_in"] = nc.dram_tensor("a1bkc", [128, 2], F32, kind="ExternalInput")
    io["bv_in"] = nc.dram_tensor("a1bv", [128, 8], F32, kind="ExternalInput")
    io["bvc_in"] = nc.dram_tensor("a1bvc", [128, 8], F32, kind="ExternalInput")
    io["ident_in"] = nc.dram_tensor("ident", [128, 128], F32, kind="ExternalInput")
    io["ones_row_in"] = nc.dram_tensor("ones_row", [1, 128], F32, kind="ExternalInput")
    io["ones_col_in"] = nc.dram_tensor("ones_col", [128, 1], F32, kind="ExternalInput")
    io["x2_o"] = nc.dram_tensor("x2_o", [TOK, D], F32, kind="ExternalOutput")
    io["q1T_o"] = nc.dram_tensor("q1T_o", [128, 2 * TOK], BF16, kind="ExternalOutput")
    io["k1T_o"] = nc.dram_tensor("k1T_o", [128, 2 * TOK], BF16, kind="ExternalOutput")
    io["v1_o"] = nc.dram_tensor("v1_o", [TOK, D], BF16, kind="ExternalOutput")
    return io


def emit_C(nc, block, stk, io, fused=False):
    nc = bass.Bass("TRN2", target_bir_lowering=False)
    x1_in = io["x1_in"]
    xn1T_i = io["xn1T_i"]
    cnt8 = io["cnt8"]
    ew_in = io["ew_in"]
    be_in = io["be_in"]
    wq_in = io["wq_in"]
    wqc_in = io["wqc_in"]
    wk_in = io["wk_in"]
    wkc_in = io["wkc_in"]
    wv_in = io["wv_in"]
    wvc_in = io["wvc_in"]
    bq_in = io["bq_in"]
    bqc_in = io["bqc_in"]
    bk_in = io["bk_in"]
    bkc_in = io["bkc_in"]
    bv_in = io["bv_in"]
    bvc_in = io["bvc_in"]
    ident_in = io["ident_in"]
    ones_row_in = io["ones_row_in"]
    ones_col_in = io["ones_col_in"]
    x2_o = io["x2_o"]
    q1T_o = io["q1T_o"]
    k1T_o = io["k1T_o"]
    v1_o = io["v1_o"]
    T = lambda nm, shape, dt: io["_sb"].alloc(shape[-1], dt, rows=shape[0])
    PS = lambda nm, shape, dt: io["_ps"].alloc(nm, shape, dt)
    psc_viewat = lambda b, w=512: io["_ps"].view_at(b, [128, w], F32)
    arn = io["_sb"].sub(150)
    xn1T = arn.alloc(NCH * TOK, BF16)
    acc = arn.alloc(NCH * TOK, F32)
    ewA = arn.alloc(NCH * D, BF16)
    ewB = arn.alloc(NCH * D, BF16)
    xn2T = xn1T
    relu = T("relu", [128, 2 * 1024], BF16)
    wcol = T("wcol", [128, 8], F32)
    wrow = T("wrow", [1, 8], F32)
    c8 = T("c8", [8, 8], F32)
    besb = T("besb", [128, NCH * 8], F32)
    idnt = T("idnt", [128, 128], BF16)
    idnt_f = T("idnt_f", [128, 128], F32)
    onesr = T("onesr", [1, 128], F32)
    onesc = T("onesc", [128, 1], F32)
    epsc = T("epsc", [128, 1], F32)
    yt = T("yt", [128, D], F32)
    xt = T("xt", [128, 2 * D], F32)
    xn = T("xn", [128, 2 * D], BF16)
    x2t = T("x2t", [128, 2 * D], F32)
    stats = T("stats", [128, 16 * NT], F32)
    bq8 = T("bq8", [128, 2], F32)
    bqs = T("bqs", [128, 2], F32)
    bqcs = T("bqcs", [128, 2], F32)
    bks = T("bks", [128, 2], F32)
    bkcs = T("bkcs", [128, 2], F32)
    bvs = T("bvs", [128, 8], F32)
    bvcs = T("bvcs", [128, 8], F32)
    sgt = T("sgt", [128, 2 * 512], BF16)
    amt = T("amt", [128, 2 * 512], BF16)
    ps_a = PS("ps_a", [128, 512], F32)
    ps_b = PS("ps_b", [128, 512], F32)
    ps_c = PS("ps_c", [128, 512], F32)
    ps_d = PS("ps_d", [128, 512], F32)
    ps_tr = PS("ps_tr", [128, D], BF16)
    ps_trf = PS("ps_trf", [128, D], F32)
    ps_w = PS("ps_w", [128, 8], F32)

    sems = io.get("_sems") or Sems(
        nc, stk, ["ld", "ld2", "ew0", "ew1", "mm", "ev", "lnV",
                  "lnA", "vch", "ach", "tr", "pe", "xd0", "xd1",
                  "st", "st0", "st1", "wc", "qkm", "qke"], prefix="C_")

    @block.gpsimd
    def _(g):
        g.dma_start(out=idnt[:, :], in_=ident_in[:, :]).then_inc(sems.h["ld"], 16)
        g.dma_start(out=xn1T[:, :], in_=xn1T_i[:, :]).then_inc(sems.h["ld"], 16)
        g.dma_start(out=ewA[:, :].rearrange("p (c o) -> p c o", c=NCH),
                    in_=ew_in.ap().rearrange("(e c p) o -> p (e c) o", p=128, c=NCH)[
                        :, 0:NCH, :]).then_inc(sems.h["ew0"], 16)
    sems.n["ld"] += 32
    sems.n["ew0"] += 16

    @block.sync
    def _(s):
        s.dma_start(out=idnt_f[:, :], in_=ident_in[:, :]).then_inc(sems.h["ld2"], 16)
        if fused:
            s.dma_start(out=c8[0:1, 0:8], in_=cnt8[:, :]).then_inc(sems.h["ld2"], 16)
        else:
            s.dma_start(out=c8[:, :], in_=cnt8[:, :]).then_inc(sems.h["ld2"], 16)
        s.dma_start(out=besb[:, :], in_=be_in[:, :]).then_inc(sems.h["ld2"], 16)
        s.dma_start(out=onesr[:, :], in_=ones_row_in[:, :]).then_inc(sems.h["ld2"], 16)
        s.dma_start(out=onesc[:, :], in_=ones_col_in[:, :]).then_inc(sems.h["ld2"], 16)
        s.dma_start(out=bqs[:, :], in_=bq_in[:, :]).then_inc(sems.h["ld2"], 16)
        s.dma_start(out=bqcs[:, :], in_=bqc_in[:, :]).then_inc(sems.h["ld2"], 16)
        s.dma_start(out=bks[:, :], in_=bk_in[:, :]).then_inc(sems.h["ld2"], 16)
        s.dma_start(out=bkcs[:, :], in_=bkc_in[:, :]).then_inc(sems.h["ld2"], 16)
        s.dma_start(out=bvs[:, :], in_=bv_in[:, :]).then_inc(sems.h["ld2"], 16)
        s.dma_start(out=bvcs[:, :], in_=bvc_in[:, :]).then_inc(sems.h["ld2"], 16)
    sems.n["ld2"] += 176

    @block.vector
    def _(v):
        v.memset(epsc[:, :], EPS)
        sems.w(v, "ld2")
        v.tensor_scalar(bq8[:, :], bqs[:, :], 0.125, None, ALU.mult
                        ).then_inc(sems.h["ev"], 1)
    sems.n["ev"] += 1

    if fused:
        @block.vector
        def _(v):
            sems.w(v, "ld2")
            ins = v.tensor_scalar(wrow[:, :], c8[0:1, 0:8], 1.0 / 32768.0,
                                  None, ALU.mult)
            ins.then_inc(sems.h["wc"], 1)
            v.wait_ge(sems.h["wc"], sems.n["wc"] + 1)
        sems.n["wc"] += 1
    else:
        @block.tensor
        def _(t):
            sems.w(t, "ld2")
            t.matmul(ps_w[0:1, 0:8], onesc[0:8, 0:1], c8[:, :], start=True,
                     stop=True).then_inc(sems.h["mm"], 1)
        sems.n["mm"] += 1

        @block.vector
        def _(v):
            sems.w(v, "mm")
            ins = v.tensor_scalar(wrow[:, :], ps_w[0:1, 0:8], 1.0 / 32768.0,
                                  None, ALU.mult)
            ins.then_inc(sems.h["wc"], 1)
            v.wait_ge(sems.h["wc"], sems.n["wc"] + 1)
        sems.n["wc"] += 1

    @block.tensor
    def _(t):
        sems.w(t, "wc")
        t.matmul(ps_w[:, 0:8], onesr[0:1, :], wrow[0:1, :], start=True,
                 stop=True).then_inc(sems.h["mm"], 1)
    sems.n["mm"] += 1

    @block.vector
    def _(v):
        sems.w(v, "mm")
        v.tensor_copy(wcol[:, :], ps_w[:, 0:8]).then_inc(sems.h["wc"], 1)
    sems.n["wc"] += 1
    wc_ready = sems.n["wc"]

    mset = [[psc_viewat(0), psc_viewat(1), psc_viewat(2), psc_viewat(3)],
            [psc_viewat(4), psc_viewat(5), psc_viewat(6), psc_viewat(7)]]
    mpair = [[psc_viewat(0, 1024), psc_viewat(2, 1024)],
             [psc_viewat(4, 1024), psc_viewat(6, 1024)]]
    ew_war = {0: 0, 1: 0}
    acc3 = acc.rearrange("p (c t) -> p c t", c=NCH)
    moe_act = []
    for e in range(8):
        par = e % 2
        ewb = ewA if par == 0 else ewB
        esem = "ew" + str(par)
        if e + 1 < 8:
            npar = (e + 1) % 2
            nbuf = ewA if npar == 0 else ewB
            @block.gpsimd
            def _(g, e=e, nbuf=nbuf, npar=npar):
                if e >= 1:
                    g.wait_ge(sems.h["mm"], ew_war[npar])
                g.dma_start(
                    out=nbuf[:, :].rearrange("p (c o) -> p c o", c=NCH),
                    in_=ew_in.ap().rearrange("(e c p) o -> p (e c) o",
                                             p=128, c=NCH)[
                        :, (e + 1) * NCH:(e + 2) * NCH, :]
                    ).then_inc(sems.h["ew" + str(npar)], 16)
            sems.n["ew" + str(npar)] += 16

        for dt in range(8):
            itg = len(moe_act)
            pset = mset[itg % 2]
            ppair = mpair[itg % 2]
            @block.tensor
            def _(t, e=e, dt=dt, itg=itg, pset=pset, ewb=ewb, esem=esem):
                if dt == 0:
                    t.wait_ge(sems.h[esem], sems.n[esem])
                    t.wait_ge(sems.h["ld"], 32)
                if itg >= 2:
                    wnm, wv = moe_act[itg - 2]
                    t.wait_ge(sems.h[wnm], wv)
                if itg < 2:
                    t.wait_ge(sems.h["wc"], wc_ready)
                for c in range(NCH):
                    for qc in range(4):
                        ins = t.matmul(
                            pset[qc][:, :],
                            ewb[:, c * D + dt * 128: c * D + (dt + 1) * 128],
                            xn1T[:, c * TOK + qc * 512: c * TOK + (qc + 1) * 512],
                            start=(c == 0), stop=(c == NCH - 1))
                ins.then_inc(sems.h["mm"], 1)
            sems.n["mm"] += 1
            mm_done = sems.n["mm"]

            @block.scalar
            def _(sc, e=e, dt=dt, itg=itg, ppair=ppair, mm_done=mm_done):
                sc.wait_ge(sems.h["mm"], mm_done)
                if itg >= 1:
                    wnm, wv = moe_act[itg - 1]
                    sc.wait_ge(sems.h[wnm], wv)
                for half in range(2):
                    sc.activation(relu[:, half * 1024:(half + 1) * 1024],
                                  ppair[half][:, :], AF.Relu,
                                  bias=besb[:, dt * 8 + e: dt * 8 + e + 1]
                                  ).then_inc(sems.h["lnA"], 1)
                    sems.n["lnA"] += 1

            @block.vector
            def _(v, e=e, dt=dt, itg=itg):
                v.wait_ge(sems.h["lnA"], sems.n["lnA"])
                if e == 0 and itg == 0:
                    v.wait_ge(sems.h["wc"], wc_ready)
                for half in range(2):
                    dst = acc3[:, dt, half * 1024:(half + 1) * 1024]
                    rl = relu[:, half * 1024:(half + 1) * 1024]
                    if e == 0:
                        ins = v.tensor_scalar(dst, rl, wcol[:, 0:1], None,
                                              ALU.mult)
                    else:
                        ins = v.scalar_tensor_tensor(dst, rl,
                                                     wcol[:, e:e + 1], dst,
                                                     ALU.mult, ALU.add)
                    ins.then_inc(sems.h["ev"], 1)
                    sems.n["ev"] += 1
            moe_act.append(("ev", sems.n["ev"]))
        ew_war[par] = sems.n["mm"]
    moe_done = sems.n["ev"]

    def st_col(i, j):
        return stats[:, 16 * i + j: 16 * i + j + 1]

    x2_done = []
    yt_done = {}
    for tt in range(NT):
        par = tt % 2
        sp = str(par)

        @block.sync
        def _(s, tt=tt, par=par, sp=sp):
            if tt >= 2:
                s.wait_ge(sems.h["ev"], x2_done[tt - 2])
            s.dma_start(out=xt[:, par * D:(par + 1) * D],
                        in_=x1_in[tt * 128:(tt + 1) * 128, :]
                        ).then_inc(sems.h["xd" + sp], 16)
        sems.n["xd" + sp] += 16

        @block.tensor
        def _(t, tt=tt):
            if tt == 0:
                t.wait_ge(sems.h["ev"], moe_done)
                t.wait_ge(sems.h["ld2"], 176)
            if tt >= 1:
                t.wait_ge(sems.h["lnV"], yt_done[tt - 1])
            for c in range(NCH):
                ins = t.transpose(ps_trf[:, c * 128:(c + 1) * 128],
                                  acc3[:, c, tt * 128:(tt + 1) * 128],
                                  idnt_f[:, :])
            ins.then_inc(sems.h["tr"], 1)
        sems.n["tr"] += 1

        @block.vector
        def _(v, tt=tt):
            v.wait_ge(sems.h["tr"], sems.n["tr"])
            if tt >= 1:
                v.wait_ge(sems.h["lnA"], sems.n["lnA"])
                v.wait_ge(sems.h["ev"], x2_done[tt - 1])
            ins = v.tensor_copy(yt[:, :], ps_trf[:, :])
            sems.ch(ins, v, "vch")
            ins = v.reduce_sum(st_col(tt, 0), yt[:, :], axis=AX.X)
            sems.ch(ins, v, "vch")
            ins = v.tensor_scalar(st_col(tt, 1), st_col(tt, 0), -1.0 / D,
                                  None, ALU.mult)
            ins.then_inc(sems.h["lnV"], 1)
            v.wait_ge(sems.h["lnV"], sems.n["lnV"] + 1)
        sems.n["lnV"] += 1
        yt_done[tt] = sems.n["lnV"]

        @block.scalar
        def _(sc, tt=tt, par=par):
            sc.wait_ge(sems.h["lnV"], sems.n["lnV"])
            if tt >= 2:
                sc.wait_ge(sems.h["tr"], sems.n["tr"])
            sc.activation(xn[:, par * D:(par + 1) * D], yt[:, :], AF.Square,
                          accum_out=st_col(tt, 2)).then_inc(sems.h["lnA"], 1)
        sems.n["lnA"] += 1

        @block.vector
        def _(v, tt=tt):
            v.wait_ge(sems.h["lnA"], sems.n["lnA"])
            v.tensor_tensor(st_col(tt, 3), st_col(tt, 1), st_col(tt, 1), ALU.mult)
            ins = v.tensor_scalar(st_col(tt, 4), st_col(tt, 2), 1.0 / D,
                                  None, ALU.mult)
            sems.ch(ins, v, "vch")
            v.tensor_tensor(st_col(tt, 4), st_col(tt, 4), st_col(tt, 3),
                            ALU.subtract).then_inc(sems.h["lnV"], 1)
        sems.n["lnV"] += 1

        @block.scalar
        def _(sc, tt=tt):
            sc.wait_ge(sems.h["lnV"], sems.n["lnV"])
            ins = sc.activation(st_col(tt, 5), st_col(tt, 4), AF.Ln,
                                bias=epsc[:, 0:1])
            sems.ch(ins, sc, "ach")
            sc.activation(st_col(tt, 5), st_col(tt, 5), AF.Exp,
                          scale=-0.5).then_inc(sems.h["lnA"], 1)
        sems.n["lnA"] += 1

        @block.vector
        def _(v, tt=tt, par=par, sp=sp):
            v.wait_ge(sems.h["lnA"], sems.n["lnA"])
            v.wait_ge(sems.h["xd" + sp], 16 * (tt // 2 + 1))
            if tt >= 2:
                v.wait_ge(sems.h["st" + sp], 16 * (tt // 2))
            ins = v.tensor_scalar(x2t[:, par * D:(par + 1) * D], yt[:, :],
                                  st_col(tt, 1), st_col(tt, 5),
                                  ALU.add, ALU.mult)
            sems.ch(ins, v, "vch")
            ins = v.tensor_tensor(x2t[:, par * D:(par + 1) * D],
                                  x2t[:, par * D:(par + 1) * D],
                                  xt[:, par * D:(par + 1) * D], ALU.add)
            ins.then_inc(sems.h["ev"], 1)
            v.wait_ge(sems.h["ev"], sems.n["ev"] + 1)
            ins = v.reduce_sum(st_col(tt, 8), x2t[:, par * D:(par + 1) * D],
                               axis=AX.X)
            sems.ch(ins, v, "vch")
            ins = v.tensor_scalar(st_col(tt, 9), st_col(tt, 8), -1.0 / D,
                                  None, ALU.mult)
            ins.then_inc(sems.h["lnV"], 1)
            v.wait_ge(sems.h["lnV"], sems.n["lnV"] + 1)
        sems.n["ev"] += 1
        sems.n["lnV"] += 1
        x2_done.append(sems.n["ev"])

        @block.sync
        def _(s, tt=tt, par=par, sp=sp):
            s.wait_ge(sems.h["ev"], x2_done[tt])
            s.dma_start(out=x2_o[tt * 128:(tt + 1) * 128, :],
                        in_=x2t[:, par * D:(par + 1) * D]
                        ).then_inc(sems.h["st" + sp], 16)
        sems.n["st" + sp] += 16

        @block.scalar
        def _(sc, tt=tt, par=par):
            sc.wait_ge(sems.h["ev"], x2_done[tt])
            if tt >= 2:
                sc.wait_ge(sems.h["tr"], sems.n["tr"])
            sc.activation(xn[:, par * D:(par + 1) * D],
                          x2t[:, par * D:(par + 1) * D], AF.Square,
                          accum_out=st_col(tt, 10)).then_inc(sems.h["lnA"], 1)
        sems.n["lnA"] += 1

        @block.vector
        def _(v, tt=tt):
            v.wait_ge(sems.h["lnA"], sems.n["lnA"])
            v.tensor_tensor(st_col(tt, 11), st_col(tt, 9), st_col(tt, 9),
                            ALU.mult)
            ins = v.tensor_scalar(st_col(tt, 12), st_col(tt, 10), 1.0 / D,
                                  None, ALU.mult)
            sems.ch(ins, v, "vch")
            v.tensor_tensor(st_col(tt, 12), st_col(tt, 12), st_col(tt, 11),
                            ALU.subtract).then_inc(sems.h["lnV"], 1)
        sems.n["lnV"] += 1

        @block.scalar
        def _(sc, tt=tt):
            sc.wait_ge(sems.h["lnV"], sems.n["lnV"])
            ins = sc.activation(st_col(tt, 13), st_col(tt, 12), AF.Ln,
                                bias=epsc[:, 0:1])
            sems.ch(ins, sc, "ach")
            sc.activation(st_col(tt, 13), st_col(tt, 13), AF.Exp,
                          scale=-0.5).then_inc(sems.h["lnA"], 1)
        sems.n["lnA"] += 1

        @block.vector
        def _(v, tt=tt, par=par):
            v.wait_ge(sems.h["lnA"], sems.n["lnA"])
            if tt >= 2:
                v.wait_ge(sems.h["tr"], sems.n["tr"])
            v.tensor_scalar(xn[:, par * D:(par + 1) * D],
                            x2t[:, par * D:(par + 1) * D],
                            st_col(tt, 9), st_col(tt, 13),
                            ALU.add, ALU.mult).then_inc(sems.h["ev"], 1)
        sems.n["ev"] += 1

        @block.tensor
        def _(t, tt=tt, par=par):
            t.wait_ge(sems.h["ev"], sems.n["ev"])
            if tt >= 1:
                t.wait_ge(sems.h["pe"], sems.n["pe"])
            for c in range(NCH):
                ins = t.transpose(ps_tr[:, c * 128:(c + 1) * 128],
                                  xn[:, par * D + c * 128:
                                     par * D + (c + 1) * 128],
                                  idnt[:, :])
            ins.then_inc(sems.h["tr"], 1)
        sems.n["tr"] += 1

        @block.vector
        def _(v, tt=tt):
            v.wait_ge(sems.h["tr"], sems.n["tr"])
            dst = xn2T.rearrange("p (c t) -> p c t", c=NCH)[
                :, :, tt * 128:(tt + 1) * 128]
            srcp = ps_tr.rearrange("p (c t) -> p c t", c=NCH)
            v.tensor_copy(dst, srcp).then_inc(sems.h["pe"], 1)
        sems.n["pe"] += 1
    xn2_ready = sems.n["pe"]

    arn.push()
    wbA = ewA
    wbB = ewB
    vT = acc.bitcast(BF16)[:, 0:NCH * TOK]

    @block.gpsimd
    def _(g):
        g.wait_ge(sems.h["ev"], moe_done)
        g.wait_ge(sems.h["mm"], sems.n["mm"])
        g.dma_start(out=wbA[:, 0:NCH * 256].rearrange("p (c o) -> p c o", c=NCH),
                    in_=wq_in.ap().rearrange("(c p) o -> p c o", p=128)
                    ).then_inc(sems.h["ew0"], 16)
        g.dma_start(out=wbA[:, NCH * 256:NCH * 512].rearrange("p (c o) -> p c o", c=NCH),
                    in_=wqc_in.ap().rearrange("(c p) o -> p c o", p=128)
                    ).then_inc(sems.h["ew0"], 16)
        g.dma_start(out=wbA[:, NCH * 512:NCH * 768].rearrange("p (c o) -> p c o", c=NCH),
                    in_=wk_in.ap().rearrange("(c p) o -> p c o", p=128)
                    ).then_inc(sems.h["ew0"], 16)
        g.dma_start(out=wbA[:, NCH * 768:NCH * 1024].rearrange("p (c o) -> p c o", c=NCH),
                    in_=wkc_in.ap().rearrange("(c p) o -> p c o", p=128)
                    ).then_inc(sems.h["ew0"], 16)
        g.dma_start(out=wbB[:, :].rearrange("p (c o) -> p c o", c=NCH),
                    in_=wv_in.ap().rearrange("(c p) o -> p c o", p=128)
                    ).then_inc(sems.h["ew1"], 16)
    sems.n["ew0"] += 64
    sems.n["ew1"] += 16

    qk_sb = T("qk_sb", [128, 4 * TOK], BF16)

    css_act_after = []

    def css_proj(woff, gwoff, bmain, bgate, scale_main, out_ap, ndt,
                 wbuf, wsem, wneed, dt0=0):
        for dt in range(ndt):
            for qc in range(4):
                it_g = len(css_act_after)
                pmain = ps_a if (dt * 4 + qc) % 2 == 0 else ps_c
                pgate = ps_b if (dt * 4 + qc) % 2 == 0 else ps_d
                @block.tensor
                def _(t, dt=dt, qc=qc, pmain=pmain, pgate=pgate, wbuf=wbuf,
                      woff=woff, gwoff=gwoff, wsem=wsem, wneed=wneed,
                      it_g=it_g):
                    t.wait_ge(sems.h[wsem], wneed)
                    t.wait_ge(sems.h["pe"], xn2_ready)
                    if it_g >= 2:
                        t.wait_ge(sems.h["lnA"], css_act_after[it_g - 2])
                    for c in range(NCH):
                        t.matmul(pmain[:, :],
                                 wbuf[:, woff + c * (ndt * 128) + dt * 128:
                                      woff + c * (ndt * 128) + (dt + 1) * 128],
                                 xn2T[:, c * TOK + qc * 512: c * TOK + (qc + 1) * 512],
                                 start=(c == 0), stop=(c == NCH - 1))
                    for c in range(NCH):
                        ins = t.matmul(pgate[:, :],
                                 wbuf[:, gwoff + c * (ndt * 128) + dt * 128:
                                      gwoff + c * (ndt * 128) + (dt + 1) * 128],
                                 xn2T[:, c * TOK + qc * 512: c * TOK + (qc + 1) * 512],
                                 start=(c == 0), stop=(c == NCH - 1))
                    ins.then_inc(sems.h["qkm"], 1)
                sems.n["qkm"] += 1

                @block.scalar
                def _(sc, dt=dt, qc=qc, pmain=pmain, pgate=pgate,
                      bmain=bmain, bgate=bgate, scale_main=scale_main,
                      dt0=dt0):
                    sc.wait_ge(sems.h["qkm"], sems.n["qkm"])
                    if sems.n["qke"] >= 1:
                        sc.wait_ge(sems.h["qke"], sems.n["qke"] - 1)
                    i2 = (dt * 4 + qc) % 2
                    ins = sc.activation(sgt[:, i2 * 512:(i2 + 1) * 512],
                                        pgate[:, :], AF.Sigmoid,
                                        bias=bgate[:, dt0 + dt:dt0 + dt + 1])
                    sc.activation(amt[:, i2 * 512:(i2 + 1) * 512],
                                  pmain[:, :], AF.Identity,
                                  bias=bmain[:, dt0 + dt:dt0 + dt + 1],
                                  scale=scale_main).then_inc(sems.h["lnA"], 1)
                sems.n["lnA"] += 1
                css_act_after.append(sems.n["lnA"])

                @block.vector
                def _(v, dt=dt, qc=qc, out_ap=out_ap):
                    v.wait_ge(sems.h["lnA"], sems.n["lnA"])
                    i2 = (dt * 4 + qc) % 2
                    v.tensor_tensor(out_ap[:, dt * TOK + qc * 512:
                                           dt * TOK + (qc + 1) * 512],
                                    amt[:, i2 * 512:(i2 + 1) * 512],
                                    sgt[:, i2 * 512:(i2 + 1) * 512],
                                    ALU.mult).then_inc(sems.h["qke"], 1)
                sems.n["qke"] += 1

    css_proj(0, NCH * 256, bq8, bqcs, 0.125, qk_sb[:, 0:2 * TOK], 2,
             wbA, "ew0", sems.n["ew0"])
    css_proj(NCH * 512, NCH * 768, bks, bkcs, 1.0, qk_sb[:, 2 * TOK:4 * TOK],
             2, wbA, "ew0", sems.n["ew0"])
    k1_done = sems.n["qke"]

    @block.sync
    def _(s):
        s.wait_ge(sems.h["qke"], k1_done)
        s.dma_start(out=k1T_o[:, :], in_=qk_sb[:, 2 * TOK:4 * TOK]
                    ).then_inc(sems.h["st"], 16)
    sems.n["st"] += 16
    k1_store = sems.n["st"]

    if fused:
        @block.gpsimd
        def _(g):
            g.wait_ge(sems.h["st"], k1_store)
            g.collective_compute(
                "AllGather", mybir.AluOpType.bypass,
                replica_groups=PAIRS_RG,
                ins=[io["agksrc"].ap().opt()],
                outs=[io["agkdst"].ap().opt()],
            ).then_inc(sems.h["wc"], 1)
        sems.n["wc"] += 1

    @block.gpsimd
    def _(g):
        g.wait_ge(sems.h["qkm"], sems.n["qkm"])
        g.dma_start(out=wbA[:, :].rearrange("p (c o) -> p c o", c=NCH),
                    in_=wvc_in.ap().rearrange("(c p) o -> p c o", p=128)
                    ).then_inc(sems.h["ew0"], 16)
    sems.n["ew0"] += 16

    for dt in range(8):
        for qc in range(4):
            it_g = len(css_act_after)
            pmain = ps_a if (dt * 4 + qc) % 2 == 0 else ps_c
            pgate = ps_b if (dt * 4 + qc) % 2 == 0 else ps_d
            @block.tensor
            def _(t, dt=dt, qc=qc, pmain=pmain, pgate=pgate, it_g=it_g):
                t.wait_ge(sems.h["ew0"], sems.n["ew0"])
                t.wait_ge(sems.h["ew1"], sems.n["ew1"])
                if it_g >= 2:
                    t.wait_ge(sems.h["lnA"], css_act_after[it_g - 2])
                for c in range(NCH):
                    t.matmul(pmain[:, :],
                             wbB[:, c * D + dt * 128:c * D + (dt + 1) * 128],
                             xn2T[:, c * TOK + qc * 512: c * TOK + (qc + 1) * 512],
                             start=(c == 0), stop=(c == NCH - 1))
                for c in range(NCH):
                    ins = t.matmul(pgate[:, :],
                             wbA[:, c * D + dt * 128:c * D + (dt + 1) * 128],
                             xn2T[:, c * TOK + qc * 512: c * TOK + (qc + 1) * 512],
                             start=(c == 0), stop=(c == NCH - 1))
                ins.then_inc(sems.h["qkm"], 1)
            sems.n["qkm"] += 1

            @block.scalar
            def _(sc, dt=dt, qc=qc, pmain=pmain, pgate=pgate):
                sc.wait_ge(sems.h["qkm"], sems.n["qkm"])
                sc.wait_ge(sems.h["qke"], sems.n["qke"] - 1)
                i2 = (dt * 4 + qc) % 2
                ins = sc.activation(sgt[:, i2 * 512:(i2 + 1) * 512],
                                    pgate[:, :], AF.Sigmoid,
                                    bias=bvcs[:, dt:dt + 1])
                sc.activation(amt[:, i2 * 512:(i2 + 1) * 512],
                              pmain[:, :], AF.Identity,
                              bias=bvs[:, dt:dt + 1]).then_inc(sems.h["lnA"], 1)
            sems.n["lnA"] += 1
            css_act_after.append(sems.n["lnA"])

            @block.vector
            def _(v, dt=dt, qc=qc):
                v.wait_ge(sems.h["lnA"], sems.n["lnA"])
                i2 = (dt * 4 + qc) % 2
                v.tensor_tensor(vT[:, dt * TOK + qc * 512:
                                   dt * TOK + (qc + 1) * 512],
                                amt[:, i2 * 512:(i2 + 1) * 512],
                                sgt[:, i2 * 512:(i2 + 1) * 512],
                                ALU.mult).then_inc(sems.h["qke"], 1)
            sems.n["qke"] += 1

    vt_ready = sems.n["qke"]

    vtile = T("vtile", [128, 2 * D], BF16)
    vT3 = vT.rearrange("p (c t) -> p c t", c=NCH)
    for tt in range(NT):
        par = tt % 2
        @block.tensor
        def _(t, tt=tt):
            t.wait_ge(sems.h["qke"], vt_ready)
            if tt >= 1:
                t.wait_ge(sems.h["pe"], sems.n["pe"])
            for c in range(NCH):
                ins = t.transpose(ps_tr[:, c * 128:(c + 1) * 128],
                                  vT3[:, c, tt * 128:(tt + 1) * 128],
                                  idnt[:, :])
            ins.then_inc(sems.h["tr"], 1)
        sems.n["tr"] += 1

        @block.vector
        def _(v, tt=tt, par=par):
            v.wait_ge(sems.h["tr"], sems.n["tr"])
            if tt >= 2:
                v.wait_ge(sems.h["st" + str(par)], 16 * (NT // 2 + tt // 2))
            v.tensor_copy(vtile[:, par * D:(par + 1) * D], ps_tr[:, :]
                          ).then_inc(sems.h["pe"], 1)
        sems.n["pe"] += 1

        @block.sync
        def _(s, tt=tt, par=par):
            s.wait_ge(sems.h["pe"], sems.n["pe"])
            s.dma_start(out=v1_o[tt * 128:(tt + 1) * 128, :],
                        in_=vtile[:, par * D:(par + 1) * D]
                        ).then_inc(sems.h["st" + str(par)], 16)
        sems.n["st" + str(par)] += 16

    @block.sync
    def _(s):
        s.wait_ge(sems.h["qke"], sems.n["qke"])
        s.dma_start(out=q1T_o[:, :], in_=qk_sb[:, 0:2 * TOK]).then_inc(sems.h["st"], 16)
        sems.n["st"] += 16
        s.wait_ge(sems.h["st"], sems.n["st"])
        s.wait_ge(sems.h["st0"], sems.n["st0"])
        s.wait_ge(sems.h["st1"], sems.n["st1"])

    if fused:
        @block.gpsimd
        def _(g):
            g.wait_ge(sems.h["st0"], sems.n["st0"])
            g.wait_ge(sems.h["st1"], sems.n["st1"])
            g.collective_compute(
                "AllGather", mybir.AluOpType.bypass,
                replica_groups=PAIRS_RG,
                ins=[io["agvsrc"].ap().opt()],
                outs=[io["agvdst"].ap().opt()],
            ).then_inc(sems.h["wc"], 1)
            g.wait_ge(sems.h["wc"], sems.n["wc"] + 1)
        sems.n["wc"] += 1


def build_C():
    nc = bass.Bass("TRN2", target_bir_lowering=False)
    io = dram_C(nc)
    with contextlib.ExitStack() as stk:
        io["_sb"], io["_ps"] = alloc_pools(nc, stk)
        block = stk.enter_context(nc.Block())
        emit_C(nc, block, stk, io)
    return nc


def dram_D(nc):
    io = {}
    io["x2_in"] = nc.dram_tensor("x2_in", [TOK, D], F32, kind="ExternalInput")
    io["qb_in"] = nc.dram_tensor("qb_in", [128, 4 * TOK], BF16, kind="ExternalInput")
    io["kb_in"] = nc.dram_tensor("kb_in", [128, 4 * 4224], BF16, kind="ExternalInput")
    io["v1f"] = nc.dram_tensor("v1f", [SKEY, D], BF16, kind="ExternalInput")
    io["bk_in"] = nc.dram_tensor("a1bk", [128, 2], F32, kind="ExternalInput")
    io["bkc_in"] = nc.dram_tensor("a1bkc", [128, 2], F32, kind="ExternalInput")
    io["bv_in"] = nc.dram_tensor("a1bv", [128, 8], F32, kind="ExternalInput")
    io["bvc_in"] = nc.dram_tensor("a1bvc", [128, 8], F32, kind="ExternalInput")
    io["w1_in"] = nc.dram_tensor("w1T", [D, DH], F32, kind="ExternalInput")
    io["w2_in"] = nc.dram_tensor("w2T", [D, DH], F32, kind="ExternalInput")
    io["h1_in"] = nc.dram_tensor("h1T", [DH, DH], F32, kind="ExternalInput")
    io["h2_in"] = nc.dram_tensor("h2T", [DH, DH], F32, kind="ExternalInput")
    io["w3_in"] = nc.dram_tensor("w3T", [DH, D], F32, kind="ExternalInput")
    io["s1_in"] = nc.dram_tensor("s1", [128, NHH], F32, kind="ExternalInput")
    io["s2_in"] = nc.dram_tensor("s2", [128, NHH], F32, kind="ExternalInput")
    io["sh1_in"] = nc.dram_tensor("sh1", [128, NHH], F32, kind="ExternalInput")
    io["sh2_in"] = nc.dram_tensor("sh2", [128, NHH], F32, kind="ExternalInput")
    io["s3_in"] = nc.dram_tensor("s3", [128, 8], F32, kind="ExternalInput")
    io["ident_in"] = nc.dram_tensor("ident", [128, 128], F32, kind="ExternalInput")
    io["ones_row_in"] = nc.dram_tensor("ones_row", [1, 128], F32, kind="ExternalInput")
    io["ones_col_in"] = nc.dram_tensor("ones_col", [128, 1], F32, kind="ExternalInput")
    io["x4_o"] = nc.dram_tensor("x4_o", [TOK, D], F32, kind="ExternalOutput")
    io["x3_s"] = nc.dram_tensor("x3_s", [TOK, D], F32)
    io["y_s"] = nc.dram_tensor("y_s", [128, 2 * NCH * 1024], BF16)
    io["pad_s"] = nc.dram_tensor("pad_s", [10, 128], BF16)
    return io


def emit_D(nc, block, stk, io, fused=False):
    nc = bass.Bass("TRN2", target_bir_lowering=False)
    KT1 = 33
    KT = 32
    x2_in = io["x2_in"]
    qb_in = io["qb_in"]
    kb_in = io["kb_in"]
    v1f = io["v1f"]
    bk_in = io["bk_in"]
    bkc_in = io["bkc_in"]
    bv_in = io["bv_in"]
    bvc_in = io["bvc_in"]
    w1_in = io["w1_in"]
    w2_in = io["w2_in"]
    h1_in = io["h1_in"]
    h2_in = io["h2_in"]
    w3_in = io["w3_in"]
    s1_in = io["s1_in"]
    s2_in = io["s2_in"]
    sh1_in = io["sh1_in"]
    sh2_in = io["sh2_in"]
    s3_in = io["s3_in"]
    ident_in = io["ident_in"]
    ones_row_in = io["ones_row_in"]
    ones_col_in = io["ones_col_in"]
    x4_o = io["x4_o"]
    x3_s = io["x3_s"]
    y_s = io["y_s"]
    pad_s = io["pad_s"]
    T = lambda nm, shape, dt: io["_sb"].alloc(shape[-1], dt, rows=shape[0])
    PS = lambda nm, shape, dt: io["_ps"].alloc(nm, shape, dt)
    psc_viewat = lambda b: io["_ps"].view_at(b, [128, 512], F32)
    arn = io["_sb"].sub(168)
    KB = 1024
    xn3T = arn.alloc_at(0, NCH * TOK, BF16)
    qbuf = arn.alloc_at(32 * KB, 4 * TOK, BF16)
    kbuf = arn.alloc_at(48 * KB, 4 * 4224, BF16)
    vpA = arn.alloc_at(81 * KB, KT1 * 130, BF16)
    vpB = arn.alloc_at(90 * KB, KT1 * 130, BF16)
    vraw = arn.alloc_at(99 * KB, KT * 128, BF16)
    U = arn.alloc_at(107 * KB, NT * H0 * 66, BF16)
    sq = arn.alloc_at(140 * KB, 2176, BF16)
    HTOK = TOK // 2
    hbuf = arn.alloc_at(32 * KB, NHH * HTOK, BF16)
    ubuf = arn.alloc_at(96 * KB, NHH * HTOK, BF16)
    wbA = arn.alloc_at(160 * KB, 4096, BF16)
    wbB = wbA
    pt = T("pt", [128, 4 * 512], BF16)
    tb = T("tb", [128, 512], BF16)
    nrm = T("nrm", [1, 16], F32)
    mrow = T("mrow", [1, 16], F32)
    negM2 = T("negM2", [128, 2], F32)
    negMp2 = T("negMp2", [128, 2], F32)
    idnt = T("idnt", [128, 128], BF16)
    onesr = T("onesr", [1, 128], F32)
    onesc = T("onesc", [128, 1], BF16)
    epsc = T("epsc", [128, 1], F32)
    yt = T("yt", [128, D], F32)
    rcol = T("rcol", [128, 16], F32)
    xt = T("xt", [128, 2 * D], F32)
    xn = T("xn", [128, 2 * D], BF16)
    x1t = T("x1t", [128, 2 * D], F32)
    stats = T("stats", [128, 16 * NT], F32)
    bks = T("bks", [128, 2], F32)
    bkcs = T("bkcs", [128, 2], F32)
    bvs = T("bvs", [128, 8], F32)
    bvcs = T("bvcs", [128, 8], F32)
    sgk = T("sgk", [128, 10], F32)
    kpadc = T("kpadc", [128, 2], BF16)
    vpadc = T("vpadc", [128, 8], BF16)
    padk = T("padk", [2, 128], BF16)
    padv = T("padv", [8, 128], BF16)
    vpadrow = T("vpadrow", [1, D], BF16)
    scl = T("scl", [128, 4 * NHH + 8], F32)
    tsb = T("tsb", [128, 4 * 512], BF16)
    ps_S0 = PS("ps_S0", [128, 512], F32)
    ps_S1 = PS("ps_S1", [128, 512], F32)
    ps_o = PS("ps_o", [65, 512], F32)
    ps_y = PS("ps_y", [128, 4 * 66], BF16)
    ps_n = PS("ps_n", [128, 512], F32)
    ps_b = PS("ps_b", [128, 8], F32)
    ps_tr = PS("ps_tr", [128, D], BF16)
    ps_g = PS("ps_g", [128, 512], F32)
    sH = [[io["_ps"].view_at(0, [128, 512], F32),
           io["_ps"].view_at(1, [128, 512], F32)],
          [io["_ps"].view_at(6, [128, 512], F32),
           io["_ps"].view_at(7, [128, 512], F32)]]
    ps_o2 = io["_ps"].view_at(5, [65, 512], F32)
    ps_oH = [ps_o, ps_o2]

    sems = io.get("_sems") or Sems(nc, stk, [
        "ld", "ld2", "kv0", "kv1", "mm", "exp", "evO", "trY", "uev", "mh",
        "xd0", "xd1", "lnV", "lnA", "vch", "ach", "tr", "pe",
        "st", "st0", "st1", "pad", "wld0", "wld1", "ev", "ffe", "ffb"],
        prefix="D_")

    @block.gpsimd
    def _(g):
        g.dma_start(out=idnt[:, :], in_=ident_in[:, :]).then_inc(sems.h["ld"], 16)
        g.dma_start(out=onesc[:, :], in_=ones_col_in[:, :]).then_inc(sems.h["ld"], 16)
    sems.n["ld"] += 32

    @block.sync
    def _(s):
        s.dma_start(out=onesr[:, :], in_=ones_row_in[:, :]).then_inc(sems.h["ld2"], 16)
        s.dma_start(out=bks[:, :], in_=bk_in[:, :]).then_inc(sems.h["ld2"], 16)
        s.dma_start(out=bkcs[:, :], in_=bkc_in[:, :]).then_inc(sems.h["ld2"], 16)
        s.dma_start(out=bvs[:, :], in_=bv_in[:, :]).then_inc(sems.h["ld2"], 16)
        s.dma_start(out=bvcs[:, :], in_=bvc_in[:, :]).then_inc(sems.h["ld2"], 16)
        s.dma_start(out=scl[:, 0:NHH], in_=s1_in[:, :]).then_inc(sems.h["ld2"], 16)
        s.dma_start(out=scl[:, NHH:2 * NHH], in_=s2_in[:, :]).then_inc(sems.h["ld2"], 16)
        s.dma_start(out=scl[:, 2 * NHH:3 * NHH], in_=sh1_in[:, :]).then_inc(sems.h["ld2"], 16)
        s.dma_start(out=scl[:, 3 * NHH:4 * NHH], in_=sh2_in[:, :]).then_inc(sems.h["ld2"], 16)
        s.dma_start(out=scl[:, 4 * NHH:4 * NHH + 8], in_=s3_in[:, :]).then_inc(sems.h["ld2"], 16)
        if not fused:
            s.dma_start(out=qbuf, in_=qb_in[:, :]).then_inc(sems.h["ld2"], 16)
            s.dma_start(out=kbuf, in_=kb_in[:, :]).then_inc(sems.h["ld2"], 16)
    sems.n["ld2"] += 192 if not fused else 160

    if fused:
        @block.vector
        def _(v):
            ins = v.memset(qbuf, 0.0)
            ins = v.memset(kbuf, 0.0)
            ins.then_inc(sems.h["pad"], 1)
        sems.n["pad"] += 1
        mz = sems.n["pad"]

        @block.sync
        def _(s):
            s.wait_ge(sems.h["pad"], mz)
            kb4 = kbuf.rearrange("p (T c) -> p T c", T=4)
            qb4 = qbuf.rearrange("p (T c) -> p T c", T=4)
            for h in range(16):
                Ti, j = h // 4, h % 4
                cc, p0 = (h * 16) // 128, (h * 16) % 128
                s.dma_start(out=qb4[32 * j:32 * j + 16, Ti, :],
                            in_=qb_in[p0:p0 + 16, cc * TOK:(cc + 1) * TOK]
                            ).then_inc(sems.h["ld2"], 16)
                sems.n["ld2"] += 16
                for m_ in range(2):
                    s.dma_start(
                        out=kb4[32 * j:32 * j + 16, Ti,
                                m_ * TOK:(m_ + 1) * TOK],
                        in_=kb_in[m_ * 128 + p0:m_ * 128 + p0 + 16,
                                  cc * TOK:(cc + 1) * TOK]
                        ).then_inc(sems.h["ld2"], 16)
                    sems.n["ld2"] += 16
    ld2_all = sems.n["ld2"]

    @block.vector
    def _(v):
        v.memset(epsc[:, :], EPS)
        for buf in (vpA, vpB):
            b4 = buf.rearrange("p (k h c) -> p k h c", k=KT1, h=2)
            v.memset(b4[:, :, :, 64:65], 1.0)
            v.memset(b4[:, 32, :, 0:64], 0.0)
        v.memset(negMp2[:, :], -1e30)

    @block.scalar
    def _(sc):
        sc.wait_ge(sems.h["ld2"], ld2_all)
        ins = sc.activation(sgk[:, 0:2], bkcs[:, :], AF.Sigmoid)
        sc.activation(sgk[:, 2:10], bvcs[:, :], AF.Sigmoid
                      ).then_inc(sems.h["lnA"], 1)
    sems.n["lnA"] += 1

    @block.vector
    def _(v):
        v.wait_ge(sems.h["lnA"], sems.n["lnA"])
        v.tensor_tensor(kpadc[:, :], bks[:, :], sgk[:, 0:2], ALU.mult)
        v.tensor_tensor(vpadc[:, :], bvs[:, :], sgk[:, 2:10], ALU.mult
                        ).then_inc(sems.h["pad"], 1)
    sems.n["pad"] += 1

    @block.tensor
    def _(t):
        sems.w(t, "pad")
        sems.w(t, "ld")
        t.transpose(ps_tr[0:2, 0:128], kpadc[:, :], idnt[:, :])
        t.transpose(ps_tr[0:8, 128:256], vpadc[:, :],
                    idnt[:, :]).then_inc(sems.h["mm"], 1)
    sems.n["mm"] += 1

    @block.vector
    def _(v):
        sems.w(v, "mm")
        v.tensor_copy(padk[:, :], ps_tr[0:2, 0:128])
        v.tensor_copy(padv[:, :], ps_tr[0:8, 128:256]
                      ).then_inc(sems.h["pad"], 1)
    sems.n["pad"] += 1

    @block.sync
    def _(s):
        sems.w(s, "pad")
        s.dma_start(out=pad_s[0:2, :], in_=padk[:, :]).then_inc(sems.h["pad"], 16)
        s.dma_start(out=pad_s[2:10, :], in_=padv[:, :]).then_inc(sems.h["pad"], 16)
    sems.n["pad"] += 32

    @block.sync
    def _(s):
        sems.w(s, "pad")
        kb4 = kbuf.rearrange("p (T c) -> p T c", T=4)
        for Ti in range(4):
            for j in range(4):
                hh = 4 * Ti + j
                srcflat = pad_s.ap().rearrange("a b -> (a b)")[
                    hh * 16: hh * 16 + 16]
                s.dma_start(out=kb4[32 * j:32 * j + 16, Ti, 4096:4097],
                            in_=srcflat.unsqueeze(1)
                            ).then_inc(sems.h["pad"], 16)
                sems.n["pad"] += 16
        s.dma_start(out=vpadrow[:, :],
                    in_=pad_s.ap().rearrange("a b -> (a b)")[
                        256:256 + D].unsqueeze(0)).then_inc(sems.h["pad"], 16)
    sems.n["pad"] += 16
    pad_ready = sems.n["pad"]

    def kv_load(grp):
        par = grp % 2
        sem = "kv" + str(par)
        @block.sync
        def _(s, grp=grp, sem=sem, par=par):
            if grp >= 2:
                s.wait_ge(sems.h["mm"], kv_war[par])
            if grp >= 1:
                o = "kv" + str(1 - par)
                s.wait_ge(sems.h[o], sems.n[o])
            s.dma_start(
                out=vraw.rearrange("p (k c) -> p k c", k=KT),
                in_=v1f.ap().rearrange("(k p) d -> p k d", p=128)[
                    :, :, grp * 128:(grp + 1) * 128],
                ).then_inc(sems.h[sem], 16)
        sems.n[sem] += 16

    def kv_restripe(grp):
        par = grp % 2
        vbuf = vpA if par == 0 else vpB
        sem = "kv" + str(par)
        @block.vector
        def _(v, grp=grp, vbuf=vbuf, sem=sem, par=par):
            v.wait_ge(sems.h[sem], sems.n[sem])
            v.wait_ge(sems.h["pad"], pad_ready)
            if grp >= 2:
                v.wait_ge(sems.h["mm"], kv_war[par])
            b4 = vbuf.rearrange("p (k h c) -> p k h c", k=KT1, h=2)
            ins = v.tensor_copy(
                b4[:, 0:KT, :, 0:64],
                vraw.rearrange("p (k h c) -> p k h c", k=KT, h=2))
            ins = v.tensor_copy(
                b4[0:1, 32, :, 0:64],
                vpadrow[0:1, grp * 128:(grp + 1) * 128].rearrange(
                    "p (h c) -> p h c", h=2))
            ins.then_inc(sems.h[sem], 1)
        sems.n[sem] += 1

    kv_war = [0, 0]
    kv_load(0)
    kv_restripe(0)
    kv_load(1)

    for grp in range(8):
        par = grp % 2
        vbuf = vpA if par == 0 else vpB
        if grp >= 1:
            kv_restripe(grp)
            if grp + 1 < 8:
                kv_load(grp + 1)
        negm_ready = {}
        for h_loc in range(2):
            h = grp * 2 + h_loc
            Ti = h // 4
            jj = h % 4
            rb = 32 * jj
            kt_ap = kbuf.rearrange("p (T c) -> p T c", T=4)
            q_ap = qbuf.rearrange("p (T c) -> p T c", T=4)

            segs = [
                (kt_ap[:, Ti, 0:2048], 4, 0, 2048),
                (kt_ap[:, Ti, 2048:4224], 5, 4, 2176),
                (q_ap[:, Ti, :], 4, 9, 2048),
            ]
            for (srcap0, ncks, ncol, wid) in segs:
                @block.scalar
                def _(sc, srcap0=srcap0, wid=wid):
                    sc.wait_ge(sems.h["ld2"], ld2_all)
                    sc.wait_ge(sems.h["pad"], pad_ready)
                    sc.wait_ge(sems.h["mh"], sems.n["mh"])
                    sc.activation(sq[:, 0:wid], srcap0, AF.Square
                                  ).then_inc(sems.h["mh"], 1)
                sems.n["mh"] += 1

                for j in range(ncks):
                    @block.tensor
                    def _(t, j=j, rb=rb, ncks=ncks, ncol=ncol):
                        t.wait_ge(sems.h["mh"], sems.n["mh"])
                        wdt = 512 if not (ncol == 4 and j == 4) else 128
                        t.matmul(ps_n[0:1, 0:wdt], onesc[rb:rb + 16, 0:1],
                                 sq[rb:rb + 16, j * 512:j * 512 + wdt],
                                 start=True, stop=True, tile_position=(rb, 0)
                                 ).then_inc(sems.h["mm"], 1)
                    sems.n["mm"] += 1

                    @block.vector
                    def _(v, j=j, ncol=ncol):
                        v.wait_ge(sems.h["mm"], sems.n["mm"])
                        wdt = 512 if not (ncol == 4 and j == 4) else 128
                        ins = v.tensor_reduce(nrm[0:1, ncol + j:ncol + j + 1],
                                              ps_n[0:1, 0:wdt],
                                              axis=AX.X, op=ALU.max)
                        ins.then_inc(sems.h["mh"], 1)
                    sems.n["mh"] += 1

            @block.vector
            def _(v, h=h):
                v.wait_ge(sems.h["mh"], sems.n["mh"])
                ins = v.tensor_reduce(mrow[0:1, 0:1], nrm[0:1, 0:9],
                                      axis=AX.X, op=ALU.max)
                sems.ch(ins, v, "vch")
                ins = v.tensor_reduce(mrow[0:1, 1:2], nrm[0:1, 9:13],
                                      axis=AX.X, op=ALU.max)
                sems.ch(ins, v, "vch")
                ins = v.tensor_tensor(mrow[0:1, 2:3], mrow[0:1, 0:1],
                                      mrow[0:1, 1:2], ALU.mult)
                ins.then_inc(sems.h["lnV"], 1)
            sems.n["lnV"] += 1

            @block.scalar
            def _(sc, h=h):
                sc.wait_ge(sems.h["lnV"], sems.n["lnV"])
                ins = sc.activation(mrow[0:1, 3:4], mrow[0:1, 2:3], AF.Ln)
                sems.ch(ins, sc, "ach")
                sc.activation(mrow[0:1, 4:5], mrow[0:1, 3:4], AF.Exp,
                              scale=0.5).then_inc(sems.h["lnA"], 1)
            sems.n["lnA"] += 1

            @block.vector
            def _(v, h=h):
                v.wait_ge(sems.h["lnA"], sems.n["lnA"])
                ins = v.tensor_scalar(mrow[0:1, 5:6], mrow[0:1, 4:5], 1.0,
                                      LN16 + 1.0, ALU.mult, ALU.add)
                sems.ch(ins, v, "vch")
                ins = v.tensor_scalar(mrow[0:1, 7:8], mrow[0:1, 5:6], -1.0,
                                      None, ALU.mult)
                ins.then_inc(sems.h["lnV"], 1)
            sems.n["lnV"] += 1

            @block.tensor
            def _(t, h=h):
                t.wait_ge(sems.h["lnV"], sems.n["lnV"])
                t.wait_ge(sems.h["mh"], sems.n["mh"])
                t.wait_ge(sems.h["exp"], sems.n["exp"])
                t.wait_ge(sems.h["evO"], sems.n["evO"])
                t.matmul(ps_b[:, 0:1], onesr[0:1, :], mrow[0:1, 7:8],
                         start=True, stop=True).then_inc(sems.h["mm"], 1)
            sems.n["mm"] += 1

            @block.vector
            def _(v, h=h, h_loc=h_loc):
                v.wait_ge(sems.h["mm"], sems.n["mm"])
                ins = v.tensor_copy(negM2[:, h_loc:h_loc + 1], ps_b[:, 0:1])
                sems.ch(ins, v, "vch")
                ins = v.tensor_scalar(negMp2[0:1, h_loc:h_loc + 1],
                                      negM2[0:1, h_loc:h_loc + 1], 1.0,
                                      LN16, ALU.mult, ALU.add)
                ins.then_inc(sems.h["mh"], 1)
            sems.n["mh"] += 1
            negm_ready[h_loc] = sems.n["mh"]

        kt_ap = kbuf.rearrange("p (T c) -> p T c", T=4)
        q_ap = qbuf.rearrange("p (T c) -> p T c", T=4)
        for qc in range(4):
            evo_before = sems.n["evO"]
            expb = sems.n["exp"]
            exp_after = {(j, hh): expb + 2 * j + hh + 1
                         for j in range(KT1) for hh in range(2)}
            qk_after = {}
            pv_after = {}

            @block.tensor
            def _(t, qc=qc, vbuf=vbuf, grp=grp):
                for kt in range(KT1 + 1):
                    if kt < KT1:
                        if kt == 0:
                            t.wait_ge(sems.h["kv" + str(grp % 2)],
                                      sems.n["kv" + str(grp % 2)])
                        for hh in range(2):
                            h = grp * 2 + hh
                            Ti = h // 4
                            rb = 32 * (h % 4)
                            if kt >= 2:
                                t.wait_ge(sems.h["exp"],
                                          exp_after[(kt - 2, hh)])
                            t.matmul(sH[hh][kt % 2][:, :],
                                     kt_ap[rb:rb + 16, Ti,
                                           kt * 128:(kt + 1) * 128],
                                     q_ap[rb:rb + 16, Ti,
                                          qc * 512:(qc + 1) * 512],
                                     start=True, stop=True,
                                     tile_position=(rb, 0)
                                     ).then_inc(sems.h["mm"], 1)
                            sems.n["mm"] += 1
                            qk_after[(kt, hh)] = sems.n["mm"]
                    if kt >= 1:
                        j = kt - 1
                        for hh in range(2):
                            t.wait_ge(sems.h["exp"], exp_after[(j, hh)])
                            if j == 0 and hh == 0:
                                t.wait_ge(sems.h["evO"], evo_before)
                            t.matmul(ps_oH[hh][:, :],
                                     vbuf[:, j * 130 + hh * 65:
                                          j * 130 + hh * 65 + 65],
                                     pt[:, (2 * hh + j % 2) * 512:
                                        (2 * hh + j % 2 + 1) * 512],
                                     start=(j == 0), stop=(j == KT1 - 1)
                                     ).then_inc(sems.h["mm"], 1)
                            sems.n["mm"] += 1
                            pv_after[(j, hh)] = sems.n["mm"]

            @block.scalar
            def _(sc, qc=qc):
                for j in range(KT1):
                    for hh in range(2):
                        sc.wait_ge(sems.h["mm"], qk_after[(j, hh)])
                        if j == 0:
                            sc.wait_ge(sems.h["mh"], negm_ready[1])
                        if j >= 2:
                            sc.wait_ge(sems.h["mm"], pv_after[(j - 2, hh)])
                        sl = 2 * hh + j % 2
                        bcol = negMp2 if j == 32 else negM2
                        sc.activation(pt[:, sl * 512:(sl + 1) * 512],
                                      sH[hh][j % 2][:, :], AF.Exp,
                                      bias=bcol[:, hh:hh + 1]
                                      ).then_inc(sems.h["exp"], 1)
                        sems.n["exp"] += 1
                        assert sems.n["exp"] == exp_after[(j, hh)]

            for hh in range(2):
                h = grp * 2 + hh

                @block.vector
                def _(v, h=h, qc=qc, hh=hh):
                    v.wait_ge(sems.h["mm"], pv_after[(KT1 - 1, hh)])
                    v.wait_ge(sems.h["trY"], sems.n["trY"])
                    v.tensor_copy(tb[0:65, :], ps_oH[hh][:, :]).then_inc(
                        sems.h["evO"], 1)
                sems.n["evO"] += 1

                @block.tensor
                def _(t, h=h, qc=qc):
                    t.wait_ge(sems.h["evO"], sems.n["evO"])
                    t.wait_ge(sems.h["uev"], sems.n["uev"])
                    for j in range(4):
                        ins = t.transpose(ps_y[:, j * 66:j * 66 + 65],
                                          tb[0:65, j * 128:(j + 1) * 128],
                                          idnt[0:65, 0:65])
                    ins.then_inc(sems.h["trY"], 1)
                sems.n["trY"] += 1

                @block.vector
                def _(v, h=h, qc=qc):
                    v.wait_ge(sems.h["trY"], sems.n["trY"])
                    dst = U.rearrange("p (t s) -> p t s", t=NT)[
                        :, 4 * qc: 4 * qc + 4, h * 66:h * 66 + 65]
                    srcp = ps_y.rearrange("p (j s) -> p j s", j=4)[
                        :, :, 0:65]
                    v.tensor_copy(dst, srcp).then_inc(sems.h["uev"], 1)
                sems.n["uev"] += 1
        kv_war[grp % 2] = sems.n["mm"]

    def st_col(i, j):
        return stats[:, 16 * i + j: 16 * i + j + 1]

    uev_total = sems.n["uev"]
    x3_done = []
    for tt in range(NT):
        par = tt % 2
        sp = str(par)

        @block.sync
        def _(s, tt=tt, par=par, sp=sp):
            if tt >= 2:
                s.wait_ge(sems.h["ev"], x3_done[tt - 2])
            s.dma_start(out=xt[:, par * D:(par + 1) * D],
                        in_=x2_in[tt * 128:(tt + 1) * 128, :]
                        ).then_inc(sems.h["xd" + sp], 16)
        sems.n["xd" + sp] += 16

        @block.vector
        def _(v, tt=tt, par=par):
            if tt == 0:
                v.wait_ge(sems.h["uev"], uev_total)
            else:
                v.wait_ge(sems.h["lnA"], sems.n["lnA"])
            uv = U.rearrange("p (t h s) -> p t h s", t=NT, h=H0)
            ins = v.reciprocal(rcol[:, 0:16], uv[:, tt, :, 64:65])
            sems.ch(ins, v, "vch")
            for h in range(H0):
                ins = v.tensor_scalar(yt[:, h * 64:(h + 1) * 64],
                                      uv[:, tt, h, 0:64],
                                      rcol[:, h:h + 1], None, ALU.mult)
            sems.ch(ins, v, "vch")
            ins = v.reduce_sum(st_col(tt, 0), yt[:, :], axis=AX.X)
            sems.ch(ins, v, "vch")
            ins = v.tensor_scalar(st_col(tt, 1), st_col(tt, 0), -1.0 / D,
                                  None, ALU.mult)
            ins.then_inc(sems.h["lnV"], 1)
            v.wait_ge(sems.h["lnV"], sems.n["lnV"] + 1)
        sems.n["lnV"] += 1
        yt_ready = sems.n["lnV"]

        @block.scalar
        def _(sc, tt=tt, par=par):
            sc.wait_ge(sems.h["lnV"], yt_ready)
            if tt >= 2:
                sc.wait_ge(sems.h["tr"], sems.n["tr"])
            sc.activation(xn[:, par * D:(par + 1) * D], yt[:, :], AF.Square,
                          accum_out=st_col(tt, 2)).then_inc(sems.h["lnA"], 1)
        sems.n["lnA"] += 1

        @block.vector
        def _(v, tt=tt):
            v.wait_ge(sems.h["lnA"], sems.n["lnA"])
            v.tensor_tensor(st_col(tt, 3), st_col(tt, 1), st_col(tt, 1), ALU.mult)
            ins = v.tensor_scalar(st_col(tt, 4), st_col(tt, 2), 1.0 / D,
                                  None, ALU.mult)
            sems.ch(ins, v, "vch")
            v.tensor_tensor(st_col(tt, 4), st_col(tt, 4), st_col(tt, 3),
                            ALU.subtract).then_inc(sems.h["lnV"], 1)
        sems.n["lnV"] += 1

        @block.scalar
        def _(sc, tt=tt):
            sc.wait_ge(sems.h["lnV"], sems.n["lnV"])
            ins = sc.activation(st_col(tt, 5), st_col(tt, 4), AF.Ln,
                                bias=epsc[:, 0:1])
            sems.ch(ins, sc, "ach")
            sc.activation(st_col(tt, 5), st_col(tt, 5), AF.Exp,
                          scale=-0.5).then_inc(sems.h["lnA"], 1)
        sems.n["lnA"] += 1

        @block.vector
        def _(v, tt=tt, par=par, sp=sp):
            v.wait_ge(sems.h["lnA"], sems.n["lnA"])
            v.wait_ge(sems.h["xd" + sp], 16 * (tt // 2 + 1))
            if tt >= 2:
                v.wait_ge(sems.h["st" + sp], 16 * (tt // 2))
            ins = v.tensor_scalar(x1t[:, par * D:(par + 1) * D], yt[:, :],
                                  st_col(tt, 1), st_col(tt, 5),
                                  ALU.add, ALU.mult)
            sems.ch(ins, v, "vch")
            ins = v.tensor_tensor(x1t[:, par * D:(par + 1) * D],
                                  x1t[:, par * D:(par + 1) * D],
                                  xt[:, par * D:(par + 1) * D], ALU.add)
            ins.then_inc(sems.h["ev"], 1)
            v.wait_ge(sems.h["ev"], sems.n["ev"] + 1)
            ins = v.reduce_sum(st_col(tt, 8), x1t[:, par * D:(par + 1) * D],
                               axis=AX.X)
            sems.ch(ins, v, "vch")
            ins = v.tensor_scalar(st_col(tt, 9), st_col(tt, 8), -1.0 / D,
                                  None, ALU.mult)
            ins.then_inc(sems.h["lnV"], 1)
            v.wait_ge(sems.h["lnV"], sems.n["lnV"] + 1)
        sems.n["ev"] += 1
        sems.n["lnV"] += 1
        x3_done.append(sems.n["ev"])

        @block.sync
        def _(s, tt=tt, par=par, sp=sp):
            s.wait_ge(sems.h["ev"], x3_done[tt])
            s.dma_start(out=x3_s[tt * 128:(tt + 1) * 128, :],
                        in_=x1t[:, par * D:(par + 1) * D]
                        ).then_inc(sems.h["st" + sp], 16)
        sems.n["st" + sp] += 16

        @block.scalar
        def _(sc, tt=tt, par=par):
            sc.wait_ge(sems.h["ev"], x3_done[tt])
            if tt >= 2:
                sc.wait_ge(sems.h["tr"], sems.n["tr"])
            sc.activation(xn[:, par * D:(par + 1) * D],
                          x1t[:, par * D:(par + 1) * D], AF.Square,
                          accum_out=st_col(tt, 10)).then_inc(sems.h["lnA"], 1)
        sems.n["lnA"] += 1

        @block.vector
        def _(v, tt=tt):
            v.wait_ge(sems.h["lnA"], sems.n["lnA"])
            v.tensor_tensor(st_col(tt, 11), st_col(tt, 9), st_col(tt, 9),
                            ALU.mult)
            ins = v.tensor_scalar(st_col(tt, 12), st_col(tt, 10), 1.0 / D,
                                  None, ALU.mult)
            sems.ch(ins, v, "vch")
            v.tensor_tensor(st_col(tt, 12), st_col(tt, 12), st_col(tt, 11),
                            ALU.subtract).then_inc(sems.h["lnV"], 1)
        sems.n["lnV"] += 1

        @block.scalar
        def _(sc, tt=tt):
            sc.wait_ge(sems.h["lnV"], sems.n["lnV"])
            ins = sc.activation(st_col(tt, 13), st_col(tt, 12), AF.Ln,
                                bias=epsc[:, 0:1])
            sems.ch(ins, sc, "ach")
            sc.activation(st_col(tt, 13), st_col(tt, 13), AF.Exp,
                          scale=-0.5).then_inc(sems.h["lnA"], 1)
        sems.n["lnA"] += 1

        @block.vector
        def _(v, tt=tt, par=par):
            v.wait_ge(sems.h["lnA"], sems.n["lnA"])
            if tt >= 2:
                v.wait_ge(sems.h["tr"], sems.n["tr"])
            v.tensor_scalar(xn[:, par * D:(par + 1) * D],
                            x1t[:, par * D:(par + 1) * D],
                            st_col(tt, 9), st_col(tt, 13),
                            ALU.add, ALU.mult).then_inc(sems.h["ev"], 1)
        sems.n["ev"] += 1

        @block.tensor
        def _(t, tt=tt, par=par):
            t.wait_ge(sems.h["ev"], sems.n["ev"])
            if tt >= 1:
                t.wait_ge(sems.h["pe"], sems.n["pe"])
            for c in range(NCH):
                ins = t.transpose(ps_tr[:, c * 128:(c + 1) * 128],
                                  xn[:, par * D + c * 128:
                                     par * D + (c + 1) * 128],
                                  idnt[:, :])
            ins.then_inc(sems.h["tr"], 1)
        sems.n["tr"] += 1

        @block.vector
        def _(v, tt=tt):
            v.wait_ge(sems.h["tr"], sems.n["tr"])
            dst = xn3T.rearrange("p (c t) -> p c t", c=NCH)[
                :, :, tt * 128:(tt + 1) * 128]
            srcp = ps_tr.rearrange("p (c t) -> p c t", c=NCH)
            v.tensor_copy(dst, srcp).then_inc(sems.h["pe"], 1)
        sems.n["pe"] += 1
    xn3_ready = sems.n["pe"]

    yT = xn3T

    h3 = hbuf.rearrange("p (c t) -> p c t", c=NHH)
    u3 = ubuf.rearrange("p (c t) -> p c t", c=NHH)

    ffp = [[psc_viewat(0), psc_viewat(1), psc_viewat(7), psc_viewat(4)],
           [psc_viewat(2), psc_viewat(3), psc_viewat(5), psc_viewat(6)]]
    ff_act = []
    w_half_last = {0: sems.n["mm"], 1: sems.n["mm"]}
    wld_half = {0: sems.n["wld0"], 1: sems.n["wld1"]}
    h3 = hbuf.rearrange("p (c t) -> p c t", c=NHH)
    u3 = ubuf.rearrange("p (c t) -> p c t", c=NHH)
    yT = xn3T
    yT_h = arn.alloc_at(96 * KB, NCH * HTOK, BF16)

    def big_mm_phase(w_dram, scol_off, act_fn, src3, nsrc, dst3, ndt_,
                     src_ready):
        st_guard = sems.n["st"]
        nh = nsrc // 2
        w_half_last[0] = w_half_last[1] = sems.n["mm"]
        for dt in range(ndt_):
            pset = ffp[dt % 2]
            for hf in range(2):
                wnm_h = "wld" + str(hf)
                @block.gpsimd
                def _(g, dt=dt, hf=hf, nh=nh, wl=w_half_last[hf], wnm_h=wnm_h):
                    g.wait_ge(sems.h["mm"], wl)
                    g.dma_start(
                        out=wbA[:, hf * nh * 128:(hf + 1) * nh * 128
                                ].rearrange("p (c o) -> p c o", c=nh),
                        in_=w_dram.ap().rearrange("(c p) o -> p c o", p=128)[
                            :, hf * nh:(hf + 1) * nh,
                            dt * 128:(dt + 1) * 128]
                        ).then_inc(sems.h[wnm_h], 16)
                sems.n[wnm_h] += 16
                wld_half[hf] = sems.n[wnm_h]

            itg = len(ff_act)
            mm_pre = sems.n["mm"]

            @block.tensor
            def _(t, dt=dt, pset=pset, itg=itg, nh=nh, nsrc=nsrc,
                  w0=wld_half[0], w1=wld_half[1]):
                t.wait_ge(sems.h[src_ready[0]], src_ready[1])
                if itg >= 2:
                    wnm, wv = ff_act[itg - 2]
                    t.wait_ge(sems.h[wnm], wv)
                for c in range(nsrc):
                    if c == 0:
                        t.wait_ge(sems.h["wld0"], w0)
                    if c == nh:
                        t.wait_ge(sems.h["wld1"], w1)
                    for qc in range(2):
                        ins = t.matmul(
                            pset[qc][:, :], wbA[:, c * 128:(c + 1) * 128],
                            src3[:, c, qc * 512:(qc + 1) * 512],
                            start=(c == 0), stop=(c == nsrc - 1))
                    if c == nh - 1:
                        ins.then_inc(sems.h["mm"], 1)
                        sems.n["mm"] += 1
                        w_half_last[0] = sems.n["mm"]
                ins.then_inc(sems.h["mm"], 1)
                sems.n["mm"] += 1
                w_half_last[1] = sems.n["mm"]
            mm_done = sems.n["mm"]

            @block.scalar
            def _(sc, dt=dt, pset=pset, mm_done=mm_done, itg=itg,
                  st_guard=st_guard):
                sc.wait_ge(sems.h["mm"], mm_done)
                sc.wait_ge(sems.h["st"], st_guard)
                for qc in range(2):
                    sc.activation(dst3[:, dt, qc * 512:(qc + 1) * 512],
                                  pset[qc][:, :], act_fn,
                                  scale=scl[:, scol_off + dt:scol_off + dt + 1]
                                  ).then_inc(sems.h["ffb"], 1)
                    sems.n["ffb"] += 1
            ff_act.append(("ffb", sems.n["ffb"]))

    xn3T3 = xn3T.rearrange("p (c t) -> p c t", c=NCH)
    yT3f = yT.rearrange("p (c t) -> p c t", c=NCH)
    for half in range(2):
        yh = yT_h.rearrange("p (c t) -> p c t", c=NCH)
        w_half_last[0] = w_half_last[1] = sems.n["mm"]
        for dht in range(NHH):
            pset = ffp[dht % 2]
            for hf in range(2):
                wnm_h = "wld" + str(hf)
                @block.gpsimd
                def _(g, dht=dht, hf=hf, wl=w_half_last[hf], wnm_h=wnm_h):
                    g.wait_ge(sems.h["mm"], wl)
                    g.dma_start(
                        out=wbA[:, hf * 512:hf * 512 + 512].rearrange(
                            "p (c o) -> p c o", c=4),
                        in_=w1_in.ap().rearrange("(c p) o -> p c o", p=128)[
                            :, hf * 4:hf * 4 + 4, dht * 128:(dht + 1) * 128]
                        ).then_inc(sems.h[wnm_h], 16)
                    g.dma_start(
                        out=wbA[:, 1024 + hf * 512:1536 + hf * 512].rearrange(
                            "p (c o) -> p c o", c=4),
                        in_=w2_in.ap().rearrange("(c p) o -> p c o", p=128)[
                            :, hf * 4:hf * 4 + 4, dht * 128:(dht + 1) * 128]
                        ).then_inc(sems.h[wnm_h], 16)
                sems.n[wnm_h] += 32
                wld_half[hf] = sems.n[wnm_h]

            itg = len(ff_act)

            @block.tensor
            def _(t, dht=dht, half=half, pset=pset, itg=itg,
                  w0=wld_half[0], w1=wld_half[1]):
                t.wait_ge(sems.h["pe"], xn3_ready)
                if itg >= 2:
                    wnm, wv = ff_act[itg - 2]
                    t.wait_ge(sems.h[wnm], wv)
                for c in range(NCH):
                    if c == 0:
                        t.wait_ge(sems.h["wld0"], w0)
                    if c == 4:
                        t.wait_ge(sems.h["wld1"], w1)
                    wb_ = (c // 4) * 512 + (c % 4) * 128
                    for qc in range(2):
                        xs = xn3T[:, c * TOK + half * HTOK + qc * 512:
                                  c * TOK + half * HTOK + (qc + 1) * 512]
                        t.matmul(pset[qc][:, :], wbA[:, wb_:wb_ + 128],
                                 xs, start=(c == 0), stop=(c == NCH - 1))
                        ins = t.matmul(pset[2 + qc][:, :],
                                       wbA[:, 1024 + wb_:1024 + wb_ + 128],
                                       xs, start=(c == 0), stop=(c == NCH - 1))
                    if c == 3:
                        ins.then_inc(sems.h["mm"], 1)
                        sems.n["mm"] += 1
                        w_half_last[0] = sems.n["mm"]
                ins.then_inc(sems.h["mm"], 1)
                sems.n["mm"] += 1
                w_half_last[1] = sems.n["mm"]
            mm_done = sems.n["mm"]

            @block.scalar
            def _(sc, dht=dht, pset=pset, mm_done=mm_done, itg=itg):
                sc.wait_ge(sems.h["mm"], mm_done)
                if itg >= 1:
                    wnm, wv = ff_act[itg - 1]
                    sc.wait_ge(sems.h[wnm], wv)
                for qc in range(2):
                    sc.activation(tsb[:, qc * 1024:qc * 1024 + 512],
                                  pset[qc][:, :], AF.Silu,
                                  scale=scl[:, dht:dht + 1])
                    sc.activation(tsb[:, qc * 1024 + 512:qc * 1024 + 1024],
                                  pset[2 + qc][:, :], AF.Identity,
                                  scale=scl[:, NHH + dht:NHH + dht + 1]
                                  ).then_inc(sems.h["ffb"], 1)
                    sems.n["ffb"] += 1

            @block.vector
            def _(v, dht=dht):
                v.wait_ge(sems.h["ffb"], sems.n["ffb"])
                for qc in range(2):
                    v.tensor_tensor(h3[:, dht, qc * 512:(qc + 1) * 512],
                                    tsb[:, qc * 1024:qc * 1024 + 512],
                                    tsb[:, qc * 1024 + 512:qc * 1024 + 1024],
                                    ALU.mult).then_inc(sems.h["ffe"], 1)
                    sems.n["ffe"] += 1
            ff_act.append(("ffe", sems.n["ffe"]))

        big_mm_phase(h1_in, 2 * NHH, AF.Gelu, h3, NHH, u3, NHH,
                     ("ffe", sems.n["ffe"]))
        big_mm_phase(h2_in, 3 * NHH, AF.Gelu, u3, NHH, h3, NHH,
                     ("ffb", sems.n["ffb"]))
        big_mm_phase(w3_in, 4 * NHH, AF.Identity, h3, NHH, yh, NCH,
                     ("ffb", sems.n["ffb"]))

        @block.sync
        def _(s, half=half):
            s.wait_ge(sems.h["ffe"], sems.n["ffe"])
            s.wait_ge(sems.h["ffb"], sems.n["ffb"])
            s.dma_start(out=y_s[:, half * (NCH * HTOK):(half + 1) * (NCH * HTOK)],
                        in_=yT_h).then_inc(sems.h["st"], 16)
        sems.n["st"] += 16
    ffn_done = sems.n["ffe"]
    yst_done = sems.n["st"]

    @block.sync
    def _(s):
        s.wait_ge(sems.h["st"], yst_done)
        for g_ in range(2):
            s.dma_start(
                out=yT.rearrange("p (c t) -> p c t", c=NCH)[
                    :, :, g_ * 1024:(g_ + 1) * 1024],
                in_=y_s.ap().rearrange("p (g c t) -> p g c t", g=2, c=NCH)[
                    :, g_]).then_inc(sems.h["st"], 16)
            sems.n["st"] += 16
    y_ready = sems.n["st"]

    xd_base = {0: sems.n["xd0"], 1: sems.n["xd1"]}
    st_base = {0: sems.n["st0"], 1: sems.n["st1"]}
    x4_done = []
    for tt in range(NT):
        par = tt % 2
        sp = str(par)

        @block.sync
        def _(s, tt=tt, par=par, sp=sp):
            if tt < 2:
                s.wait_ge(sems.h["st0"], st_base[0])
                s.wait_ge(sems.h["st1"], st_base[1])
            if tt >= 2:
                s.wait_ge(sems.h["ev"], x4_done[tt - 2])
            s.dma_start(out=xt[:, par * D:(par + 1) * D],
                        in_=x3_s[tt * 128:(tt + 1) * 128, :]
                        ).then_inc(sems.h["xd" + sp], 16)
        sems.n["xd" + sp] += 16

        @block.tensor
        def _(t, tt=tt, par=par):
            if tt == 0:
                t.wait_ge(sems.h["st"], y_ready)
            if tt >= 1:
                t.wait_ge(sems.h["lnV"], y_done[tt - 1])
            for c in range(NCH):
                ins = t.transpose(ps_tr[:, c * 128:(c + 1) * 128],
                                  yT3f[:, c, tt * 128:(tt + 1) * 128],
                                  idnt[:, :])
            ins.then_inc(sems.h["tr"], 1)
        sems.n["tr"] += 1

        @block.vector
        def _(v, tt=tt):
            v.wait_ge(sems.h["tr"], sems.n["tr"])
            if tt >= 1:
                v.wait_ge(sems.h["lnA"], sems.n["lnA"])
                v.wait_ge(sems.h["ev"], x4_done[tt - 1])
            ins = v.tensor_copy(yt[:, :], ps_tr[:, :])
            sems.ch(ins, v, "vch")
            ins = v.reduce_sum(st_col(tt, 0), yt[:, :], axis=AX.X)
            sems.ch(ins, v, "vch")
            ins = v.tensor_scalar(st_col(tt, 1), st_col(tt, 0), -1.0 / D,
                                  None, ALU.mult)
            ins.then_inc(sems.h["lnV"], 1)
            v.wait_ge(sems.h["lnV"], sems.n["lnV"] + 1)
        sems.n["lnV"] += 1
        y_done = y_done if tt > 0 else {}
        y_done[tt] = sems.n["lnV"]

        @block.scalar
        def _(sc, tt=tt, par=par):
            sc.wait_ge(sems.h["lnV"], y_done[tt])
            sc.activation(xn[:, par * D:(par + 1) * D], yt[:, :], AF.Square,
                          accum_out=st_col(tt, 2)).then_inc(sems.h["lnA"], 1)
        sems.n["lnA"] += 1

        @block.vector
        def _(v, tt=tt):
            v.wait_ge(sems.h["lnA"], sems.n["lnA"])
            v.tensor_tensor(st_col(tt, 3), st_col(tt, 1), st_col(tt, 1), ALU.mult)
            ins = v.tensor_scalar(st_col(tt, 4), st_col(tt, 2), 1.0 / D,
                                  None, ALU.mult)
            sems.ch(ins, v, "vch")
            v.tensor_tensor(st_col(tt, 4), st_col(tt, 4), st_col(tt, 3),
                            ALU.subtract).then_inc(sems.h["lnV"], 1)
        sems.n["lnV"] += 1

        @block.scalar
        def _(sc, tt=tt):
            sc.wait_ge(sems.h["lnV"], sems.n["lnV"])
            ins = sc.activation(st_col(tt, 5), st_col(tt, 4), AF.Ln,
                                bias=epsc[:, 0:1])
            sems.ch(ins, sc, "ach")
            sc.activation(st_col(tt, 5), st_col(tt, 5), AF.Exp,
                          scale=-0.5).then_inc(sems.h["lnA"], 1)
        sems.n["lnA"] += 1

        @block.vector
        def _(v, tt=tt, par=par, sp=sp):
            v.wait_ge(sems.h["lnA"], sems.n["lnA"])
            v.wait_ge(sems.h["xd" + sp], xd_base[par] + 16 * (tt // 2 + 1))
            if tt >= 2:
                v.wait_ge(sems.h["st" + sp], st_base[par] + 16 * (tt // 2))
            ins = v.tensor_scalar(x1t[:, par * D:(par + 1) * D], yt[:, :],
                                  st_col(tt, 1), st_col(tt, 5),
                                  ALU.add, ALU.mult)
            sems.ch(ins, v, "vch")
            ins = v.tensor_tensor(x1t[:, par * D:(par + 1) * D],
                                  x1t[:, par * D:(par + 1) * D],
                                  xt[:, par * D:(par + 1) * D], ALU.add)
            ins.then_inc(sems.h["ev"], 1)
        sems.n["ev"] += 1
        x4_done.append(sems.n["ev"])

        @block.sync
        def _(s, tt=tt, par=par, sp=sp):
            s.wait_ge(sems.h["ev"], x4_done[tt])
            s.dma_start(out=x4_o[tt * 128:(tt + 1) * 128, :],
                        in_=x1t[:, par * D:(par + 1) * D]
                        ).then_inc(sems.h["st" + sp], 16)
        sems.n["st" + sp] += 16

    @block.sync
    def _(s):
        s.wait_ge(sems.h["st0"], sems.n["st0"])
        s.wait_ge(sems.h["st1"], sems.n["st1"])


def build_D():
    nc = bass.Bass("TRN2", target_bir_lowering=False)
    io = dram_D(nc)
    with contextlib.ExitStack() as stk:
        io["_sb"], io["_ps"] = alloc_pools(nc, stk)
        block = stk.enter_context(nc.Block())
        emit_D(nc, block, stk, io)
    return nc


def _toT(a2d, nch=8):
    t = a2d.shape[0]
    return np.ascontiguousarray(
        a2d.T.reshape(nch, 128, t).transpose(1, 0, 2).reshape(128, nch * t))


def _pack_padded(matT, width):
    out = np.zeros((128, 4 * width), dtype=matT.dtype)
    for h in range(16):
        T_, j = h // 4, h % 4
        out[32 * j:32 * j + 16, T_ * width:T_ * width + matT.shape[1]] = \
            matT[h * 16:(h + 1) * 16]
    return out


def kernel(**inputs):
    return kernel_fused(**inputs)


def kernel_unfused(**inputs):
    sh = _prep_shared(inputs)
    x = np.asarray(inputs["L0_attn_x"])
    rel_ext = sh["rel_ext"]
    rel16p = np.ascontiguousarray(np.concatenate([rel_ext, rel_ext[:, -1:]], axis=1))
    rs_by_g = [_prep_rs(rel_ext, 0), _prep_rs(rel_ext, 1)]

    ncA = build_A()
    mapsA = []
    for c in range(8):
        b, g = c // 2, c % 2
        mapsA.append({
            "x_in": np.ascontiguousarray(x[b, g * 2048:(g + 1) * 2048, :]),
            "wq0": sh["wq0"], "wk0": sh["wk0"], "wv0": sh["wv0"],
            "bq0": sh["bq0"], "bk0": sh["bk0"], "bv0": sh["bv0"],
            "ident": sh["ident"], "ones_row": sh["ones_row"],
        })
    resA = run_bass_kernel_spmd(ncA, mapsA, core_ids=list(range(8))).results

    ncB = build_B()
    mapsB = []
    for c in range(8):
        b, g = c // 2, c % 2
        kT0 = np.asarray(resA[2 * b]["kT_o"]).reshape(128, NCH, TOK)
        kT1 = np.asarray(resA[2 * b + 1]["kT_o"]).reshape(128, NCH, TOK)
        kf = np.ascontiguousarray(
            np.concatenate([kT0, kT1], axis=2).reshape(128, NCH * SKEY))
        vf = np.ascontiguousarray(np.concatenate(
            [np.asarray(resA[2 * b]["v_o"]), np.asarray(resA[2 * b + 1]["v_o"])],
            axis=0))
        mapsB.append({
            "x_in": np.ascontiguousarray(x[b, g * 2048:(g + 1) * 2048, :]),
            "qT_i": np.asarray(resA[c]["qT_o"]),
            "kf": kf, "vf": vf, "rs": rs_by_g[g], "rel16": rel16p,
            "gw": sh["gw"], "gb": sh["gb"], "ident": sh["ident"],
            "ones_row": sh["ones_row"], "ones_col": sh["ones_col"],
        })
    resB = run_bass_kernel_spmd(ncB, mapsB, core_ids=list(range(8))).results

    cnt8 = np.ascontiguousarray(np.concatenate(
        [np.asarray(resB[c]["cnt_o"]) for c in range(8)], axis=0))
    ncC = build_C()
    mapsC = []
    for c in range(8):
        mapsC.append({
            "x1_in": np.asarray(resB[c]["x1_o"]),
            "xn1T_i": np.asarray(resB[c]["xn1T_o"]),
            "cnt8": cnt8, "ew": sh["ew"], "be": sh["be"],
            "a1Wq": sh["a1Wq"], "a1Wqc": sh["a1Wqc"], "a1Wk": sh["a1Wk"],
            "a1Wkc": sh["a1Wkc"], "a1Wv": sh["a1Wv"], "a1Wvc": sh["a1Wvc"],
            "a1bq": sh["a1bq"], "a1bqc": sh["a1bqc"], "a1bk": sh["a1bk"],
            "a1bkc": sh["a1bkc"], "a1bv": sh["a1bv"], "a1bvc": sh["a1bvc"],
            "ident": sh["ident"], "ones_row": sh["ones_row"],
            "ones_col": sh["ones_col"],
        })
    resC = run_bass_kernel_spmd(ncC, mapsC, core_ids=list(range(8))).results

    ncD = build_D()
    mapsD = []
    for c in range(8):
        b, g = c // 2, c % 2
        q1T = np.asarray(resC[c]["q1T_o"])
        qfullT = np.ascontiguousarray(
            q1T.reshape(128, 2, TOK).transpose(1, 0, 2).reshape(256, TOK))
        k1T0 = np.asarray(resC[2 * b]["k1T_o"]).reshape(128, 2, TOK)
        k1T1 = np.asarray(resC[2 * b + 1]["k1T_o"]).reshape(128, 2, TOK)
        kfT = np.ascontiguousarray(
            np.concatenate([k1T0, k1T1], axis=2).reshape(128, 2, SKEY
                                                         ).transpose(1, 0, 2
                                                                     ).reshape(256, SKEY))
        vfull = np.ascontiguousarray(np.concatenate(
            [np.asarray(resC[2 * b]["v1_o"]), np.asarray(resC[2 * b + 1]["v1_o"])],
            axis=0))
        mapsD.append({
            "x2_in": np.asarray(resC[c]["x2_o"]),
            "qb_in": _pack_padded(qfullT, TOK),
            "kb_in": _pack_padded(kfT, 4224),
            "v1f": vfull,
            "a1bk": sh["a1bk"], "a1bkc": sh["a1bkc"],
            "a1bv": sh["a1bv"], "a1bvc": sh["a1bvc"],
            "w1T": sh["w1T"], "w2T": sh["w2T"], "h1T": sh["h1T"],
            "h2T": sh["h2T"], "w3T": sh["w3T"],
            "s1": sh["s1"], "s2": sh["s2"], "sh1": sh["sh1"],
            "sh2": sh["sh2"], "s3": sh["s3"],
            "ident": sh["ident"], "ones_row": sh["ones_row"],
            "ones_col": sh["ones_col"],
        })
    resD = run_bass_kernel_spmd(ncD, mapsD, core_ids=list(range(8))).results

    out = np.empty((B, S, D), dtype=np.float32)
    for c in range(8):
        b, g = c // 2, c % 2
        out[b, g * 2048:(g + 1) * 2048, :] = np.asarray(resD[c]["x4_o"])
    return out



PAIRS_RG = [[0, 1], [2, 3], [4, 5], [6, 7]]
ALL8_RG = [list(range(8))]


def emit_barrier(nc, block, gsems, kctr):
    kctr[0] += 1
    k = kctr[0]
    for engname in ("tensor", "vector", "scalar", "gpsimd", "sync"):
        dec = getattr(block, engname)

        @dec
        def _(e):
            e.drain()
            e.sem_inc(gsems.h["bar"], 1)
            e.wait_ge(gsems.h["bar"], 5 * k)


def build_fused():
    nc = bass.Bass("TRN2", target_bir_lowering=False)
    x_in = nc.dram_tensor("x_in", [TOK, D], F32, kind="ExternalInput")
    ext = {}
    for nm, shp in [("wq0", [D, D]), ("wk0", [D, D]), ("wv0", [D, D]),
                    ("bq0", [128, 8]), ("bk0", [128, 8]), ("bv0", [1, D]),
                    ("ident", [128, 128]), ("ones_row", [1, 128]),
                    ("ones_col", [128, 1]),
                    ("rs", [H0 * 128, RSW]), ("rel16", [H0, 8192]),
                    ("gw", [128, NCH * 8]), ("gb", [8, 1]),
                    ("ew", [8 * D, D]), ("be", [128, NCH * 8]),
                    ("a1Wq", [D, 256]), ("a1Wqc", [D, 256]),
                    ("a1Wk", [D, 256]), ("a1Wkc", [D, 256]),
                    ("a1Wv", [D, D]), ("a1Wvc", [D, D]),
                    ("a1bq", [128, 2]), ("a1bqc", [128, 2]),
                    ("a1bk", [128, 2]), ("a1bkc", [128, 2]),
                    ("a1bv", [128, 8]), ("a1bvc", [128, 8]),
                    ("w1T", [D, DH]), ("w2T", [D, DH]),
                    ("h1T", [DH, DH]), ("h2T", [DH, DH]), ("w3T", [DH, D]),
                    ("s1", [128, NHH]), ("s2", [128, NHH]),
                    ("sh1", [128, NHH]), ("sh2", [128, NHH]), ("s3", [128, 8])]:
        ext[nm] = nc.dram_tensor(nm, shp, F32, kind="ExternalInput")
    x4_o = nc.dram_tensor("x4_o", [TOK, D], F32, kind="ExternalOutput")
    qT_d = nc.dram_tensor("qT_d", [128, NCH * TOK], BF16)
    kT_d = nc.dram_tensor("kT_d", [128, NCH * TOK], BF16)
    v_d = nc.dram_tensor("v_d", [TOK, D], BF16)
    kag0 = nc.dram_tensor("kag0", [256, NCH * TOK], BF16)
    vag0 = nc.dram_tensor("vag0", [SKEY, D], BF16)
    x1_d = nc.dram_tensor("x1_d", [TOK, D], F32)
    xn1T_d = nc.dram_tensor("xn1T_d", [128, NCH * TOK], BF16)
    cnt_d = nc.dram_tensor("cnt_d", [1, 8], F32)
    car = nc.dram_tensor("car", [1, 8], F32)
    x2_d = nc.dram_tensor("x2_d", [TOK, D], F32)
    q1T_d = nc.dram_tensor("q1T_d", [128, 2 * TOK], BF16)
    k1T_d = nc.dram_tensor("k1T_d", [128, 2 * TOK], BF16)
    v1_d = nc.dram_tensor("v1_d", [TOK, D], BF16)
    k1ag = nc.dram_tensor("k1ag", [256, 2 * TOK], BF16)
    v1ag = nc.dram_tensor("v1ag", [SKEY, D], BF16)
    x3_s = nc.dram_tensor("x3_s", [TOK, D], F32)
    y_s = nc.dram_tensor("y_s", [128, 2 * NCH * 1024], BF16)
    pad_s = nc.dram_tensor("pad_s", [10, 128], BF16)

    with contextlib.ExitStack() as stk:
        sbc, psc = alloc_pools(nc, stk)
        gsems = Sems(nc, stk, ["bar", "cc"], prefix="G_")
        semsA = Sems(nc, stk, ["ld", "wld", "xd0", "xd1", "lnV", "lnA", "tr",
                               "ev", "mm", "pe", "st", "st0", "st1", "bc",
                               "vch", "ach"], prefix="A_")
        semsB = Sems(nc, stk, [
            "ld", "ld2", "qld", "kv0", "kv1", "kvr", "rs0", "rs1", "bmv",
            "mm", "bias", "exp", "evO", "trY", "uev", "mh",
            "xd0", "xd1", "lnV", "lnA", "vch", "ach", "tr", "pe",
            "st", "st0", "st1", "gmm", "gev", "bmx", "ev"], prefix="B_")
        semsC = Sems(nc, stk, ["ld", "ld2", "ew0", "ew1", "mm", "ev", "lnV",
                               "lnA", "vch", "ach", "tr", "pe", "xd0", "xd1",
                               "st", "st0", "st1", "wc", "qkm", "qke"],
                     prefix="C_")
        semsD = Sems(nc, stk, [
            "ld", "ld2", "kv0", "kv1", "mm", "exp", "evO", "trY", "uev", "mh",
            "xd0", "xd1", "lnV", "lnA", "vch", "ach", "tr", "pe",
            "st", "st0", "st1", "pad", "wld0", "wld1", "ev", "ffe", "ffb"],
            prefix="D_")
        block = stk.enter_context(nc.Block())
        kctr = [0]

        ioA = {"x_in": x_in, "wq": ext["wq0"], "wk": ext["wk0"],
               "wv": ext["wv0"], "bq": ext["bq0"], "bk": ext["bk0"],
               "bv": ext["bv0"], "ident_in": ext["ident"],
               "ones_row_in": ext["ones_row"],
               "qT_o": qT_d, "kT_o": kT_d, "v_o": v_d,
               "agksrc": kT_d, "agkdst": kag0,
               "agvsrc": v_d, "agvdst": vag0,
               "_sb": Carver(sbc.t, sbc.nbytes), "_ps": PsumPool(psc.t),
               "_sems": semsA}
        emit_A(nc, block, stk, ioA, fused=True)

        emit_barrier(nc, block, gsems, kctr)

        ioB = {"x_in": x_in, "qT_i": qT_d, "kf": kag0, "vf": vag0,
               "rs_in": ext["rs"], "rel16": ext["rel16"], "gw_in": ext["gw"],
               "gb_in": ext["gb"], "ident_in": ext["ident"],
               "ones_row_in": ext["ones_row"], "ones_col_in": ext["ones_col"],
               "x1_o": x1_d, "xn1T_o": xn1T_d, "cnt_o": cnt_d,
               "_sb": Carver(sbc.t, sbc.nbytes), "_ps": PsumPool(psc.t),
               "_sems": semsB}
        emit_B(nc, block, stk, ioB, fused=True)

        emit_barrier(nc, block, gsems, kctr)

        @block.gpsimd
        def _(g):
            g.collective_compute(
                "AllReduce", mybir.AluOpType.add, replica_groups=ALL8_RG,
                ins=[cnt_d.ap().opt()], outs=[car.ap().opt()],
            ).then_inc(gsems.h["cc"], 1)
            g.wait_ge(gsems.h["cc"], 1)
        gsems.n["cc"] = 1

        emit_barrier(nc, block, gsems, kctr)

        ioC = {"x1_in": x1_d, "xn1T_i": xn1T_d, "cnt8": car,
               "ew_in": ext["ew"], "be_in": ext["be"],
               "wq_in": ext["a1Wq"], "wqc_in": ext["a1Wqc"],
               "wk_in": ext["a1Wk"], "wkc_in": ext["a1Wkc"],
               "wv_in": ext["a1Wv"], "wvc_in": ext["a1Wvc"],
               "bq_in": ext["a1bq"], "bqc_in": ext["a1bqc"],
               "bk_in": ext["a1bk"], "bkc_in": ext["a1bkc"],
               "bv_in": ext["a1bv"], "bvc_in": ext["a1bvc"],
               "ident_in": ext["ident"], "ones_row_in": ext["ones_row"],
               "ones_col_in": ext["ones_col"],
               "x2_o": x2_d, "q1T_o": q1T_d, "k1T_o": k1T_d, "v1_o": v1_d,
               "agksrc": k1T_d, "agkdst": k1ag,
               "agvsrc": v1_d, "agvdst": v1ag,
               "_sb": Carver(sbc.t, sbc.nbytes), "_ps": PsumPool(psc.t),
               "_sems": semsC}
        emit_C(nc, block, stk, ioC, fused=True)

        emit_barrier(nc, block, gsems, kctr)

        ioD = {"x2_in": x2_d, "qb_in": q1T_d, "kb_in": k1ag, "v1f": v1ag,
               "bk_in": ext["a1bk"], "bkc_in": ext["a1bkc"],
               "bv_in": ext["a1bv"], "bvc_in": ext["a1bvc"],
               "w1_in": ext["w1T"], "w2_in": ext["w2T"],
               "h1_in": ext["h1T"], "h2_in": ext["h2T"], "w3_in": ext["w3T"],
               "s1_in": ext["s1"], "s2_in": ext["s2"], "sh1_in": ext["sh1"],
               "sh2_in": ext["sh2"], "s3_in": ext["s3"],
               "ident_in": ext["ident"], "ones_row_in": ext["ones_row"],
               "ones_col_in": ext["ones_col"],
               "x4_o": x4_o, "x3_s": x3_s, "y_s": y_s, "pad_s": pad_s,
               "_sb": Carver(sbc.t, sbc.nbytes), "_ps": PsumPool(psc.t),
               "_sems": semsD}
        emit_D(nc, block, stk, ioD, fused=True)
    return nc


def kernel_fused(**inputs):
    sh = _prep_shared(inputs)
    x = np.asarray(inputs["L0_attn_x"])
    rel_ext = sh["rel_ext"]
    rel16p = np.ascontiguousarray(
        np.concatenate([rel_ext, rel_ext[:, -1:]], axis=1))
    rs_by_g = [_prep_rs(rel_ext, 0), _prep_rs(rel_ext, 1)]
    nc = build_fused()
    maps = []
    for c in range(8):
        b, g = c // 2, c % 2
        m = {"x_in": np.ascontiguousarray(x[b, g * 2048:(g + 1) * 2048, :]),
             "rs": rs_by_g[g], "rel16": rel16p}
        for nm in ["wq0", "wk0", "wv0", "bq0", "bk0", "bv0", "ident",
                   "ones_row", "ones_col", "gw", "gb", "ew", "be",
                   "a1Wq", "a1Wqc", "a1Wk", "a1Wkc", "a1Wv", "a1Wvc",
                   "a1bq", "a1bqc", "a1bk", "a1bkc", "a1bv", "a1bvc",
                   "w1T", "w2T", "h1T", "h2T", "w3T",
                   "s1", "s2", "sh1", "sh2", "s3"]:
            m[nm] = sh[nm]
        maps.append(m)
    res = run_bass_kernel_spmd(nc, maps, core_ids=list(range(8))).results
    out = np.empty((B, S, D), dtype=np.float32)
    for c in range(8):
        b, g = c // 2, c % 2
        out[b, g * 2048:(g + 1) * 2048, :] = np.asarray(res[c]["x4_o"])
    return out
